# version 02
# baseline (speedup 1.0000x reference)
import numpy as np
import concourse.bass as bass
import concourse.mybir as mybir
from concourse.mybir import AxisListType
from concourse.alu_op_type import AluOpType as OP
from concourse.bass_utils import run_bass_kernel_spmd

F32 = mybir.dt.float32
BF16 = mybir.dt.bfloat16
I32 = mybir.dt.int32
AF = mybir.ActivationFunctionType
AX = AxisListType.X

NCORES = 8
B, S, D = 2, 16384, 1024
TPC = S * B // NCORES
NT = TPC // 128
ND = D // 128
MAGIC = 12582912.0
LN7_3 = float(np.log(7.0) - np.log(3.0))
LN09 = float(np.log(0.9))
FQ = 3072
NF = FQ // 128
MQ = 4096
H = 16
MPH = 128

PHASES = ("attn_a", "ffn1", "attn_b", "ffn2")
DEBUG_TAP = {}


class _Sem:
    __slots__ = ("h", "n")

    def __init__(self, h):
        self.h = h
        self.n = 0


class Eng:

    def __init__(self, cx, name, raw, dma_step=1):
        self._cx = cx
        self.name = name
        self.raw = raw
        self.sem = cx.mksem("S" + name)
        self._pend = None
        self._step = dma_step

    def op(self, meth, *a, **kw):
        ins = getattr(self.raw, meth)(*a, **kw)
        self._pend = ins
        return ins

    def dma(self, out, in_):
        ins = self.raw.dma_start(out=out, in_=in_)
        ins.then_inc(self.sem.h, 16)
        self.sem.n += 16
        self._pend = None
        return ins

    def seal(self):
        if self._pend is not None:
            self._pend.then_inc(self.sem.h, 1)
            self.sem.n += 1
            self._pend = None
        return (self.sem, self.sem.n)

    def wait(self, *others):
        for o in others:
            s, n = o.seal()
            if n > 0:
                self.raw.wait_ge(s.h, n)

    def wait_tok(self, tok):
        s, n = tok
        self.raw.wait_ge(s.h, n)

    def coll(self, *a, **kw):
        s = self._cx.mksem(f"Scc{self._cx.ncc}")
        self._cx.ncc += 1
        ins = self.raw.collective_compute(*a, **kw)
        ins.then_inc(s.h, 1)
        return (s, 1)


class Ctx:
    def __init__(self):
        self.nc = bass.Bass("TRN2", target_bir_lowering=False, debug=False)
        self._mgrs = []
        self.ncc = 0

    def mksem(self, name):
        m = self.nc.semaphore(name)
        h = m.__enter__()
        self._mgrs.append(m)
        return _Sem(h)

    def sb(self, name, shape, dt, pool=None):
        m = self.nc.sbuf_tensor(name, shape, dt)
        t = m.__enter__()
        if pool is not None:
            pool.append(m)
        else:
            self._mgrs.append(m)
        return t

    def free_pool(self, pool):
        for m in reversed(pool):
            m.__exit__(None, None, None)
        pool.clear()

    def ps(self, name, shape, dt):
        m = self.nc.psum_tensor(name, shape, dt)
        t = m.__enter__()
        self._mgrs.append(m)
        return t

    def close(self):
        for m in reversed(self._mgrs):
            m.__exit__(None, None, None)


def build():
    cx = Ctx()
    nc = cx.nc

    def din(name, shape):
        return nc.dram_tensor(name, shape, F32, kind="ExternalInput").ap()

    xs = din("xs", [TPC, D])
    aWk = din("aWk", [64, D]); aWv = din("aWv", [64, D]); aWq = din("aWq", [64, D])
    fa_in = din("fa", [1, 1]); sa_in = din("sa", [1, 1])
    nm_w1 = din("nm_w1", [128, 64]); nm_b1 = din("nm_b1", [1, 128])
    nm_w2 = din("nm_w2", [64, 128]); nm_b2 = din("nm_b2", [1, 64])
    ir_w1 = din("ir_w1", [64, D]); ir_b1 = din("ir_b1", [1, 64])
    ir_w2 = din("ir_w2", [1, 64]); ir_b2 = din("ir_b2", [1, 1])
    pw = din("pw", [3, D]); aWo = din("aWo", [D, 64]); norm_w = din("norm_w", [1, 64])
    rankm = din("rankm", [1, 8])
    if "attn_b" in PHASES:
        bWq = din("bWq", [D, D]); bWk = din("bWk", [D, D]); bWv = din("bWv", [D, D])
        bWo = din("bWo", [D, D]); omega = din("omega", [MPH, 64])
        bW1 = din("bW1", [D, MQ]); bb1 = din("bb1", [1, MQ])
        bW2 = din("bW2", [MQ, D]); bb2 = din("bb2", [1, D])
        gam = din("gamma", [1, D]); bet = din("beta", [1, D])
    if "ffn1" in PHASES or "ffn2" in PHASES:
        w_gate = din("w_gate", [FQ, D]); w_up = din("w_up", [FQ, D]); w_down = din("w_down", [D, FQ])

    out = nc.dram_tensor("out", [TPC, D], F32, kind="ExternalOutput").ap()
    outt = out.rearrange("(n p) d -> n p d", p=128)

    taps = {nm: nc.dram_tensor("tp_" + nm, shape, F32 if dt == "f32" else BF16, kind="ExternalOutput").ap()
            for nm, (shape, dt) in DEBUG_TAP.items()}

    def tap(GPeng, name, src):
        if name in taps:
            GPeng.dma(taps[name], src)

    spillA = nc.dram_tensor("spillA", [NT * 2, 128, 192], BF16).ap()
    spillQ = nc.dram_tensor("spillQ", [NT, 128, 128], BF16).ap()
    ccA_in = nc.dram_tensor("ccA_in", [64, 256], F32).ap()
    ccA_out = nc.dram_tensor("ccA_out", [256, 256], F32).ap()
    ccN_in = nc.dram_tensor("ccN_in", [1, 66], F32).ap()
    ccN_out = nc.dram_tensor("ccN_out", [4, 66], F32).ap()
    if "attn_b" in PHASES:
        qphi_sc = nc.dram_tensor("qphi_sc", [H, TPC, MPH], BF16).ap()
        kphi_sc = nc.dram_tensor("kphi_sc", [H, TPC, MPH], BF16).ap()
        v_sc = nc.dram_tensor("v_sc", [TPC, D], BF16).ap()
        bWq_sc = nc.dram_tensor("bWq_sc", [D, D], BF16).ap()
        bWk_sc = nc.dram_tensor("bWk_sc", [D, D], BF16).ap()
        bWv_sc = nc.dram_tensor("bWv_sc", [D, D], BF16).ap()
        bWo_sc = nc.dram_tensor("bWo_sc", [D, D], BF16).ap()
        bW1_sc = nc.dram_tensor("bW1_sc", [D, MQ], BF16).ap()
        bW2_sc = nc.dram_tensor("bW2_sc", [MQ, D], BF16).ap()
        ccG_in = nc.dram_tensor("ccG_in", [128, H * 66], F32).ap()
        ccG_out = nc.dram_tensor("ccG_out", [512, H * 66], F32).ap()
    if "ffn1" in PHASES or "ffn2" in PHASES:
        upT_sc = nc.dram_tensor("upT_sc", [D, FQ], BF16).ap()
        gateT_sc = nc.dram_tensor("gateT_sc", [D, FQ], BF16).ap()
        wdT_sc = nc.dram_tensor("wdT_sc", [FQ, D], BF16).ap()
        up_sc = nc.dram_tensor("up_sc", [FQ, TPC], F32).ap()
        gate_sc = nc.dram_tensor("gate_sc", [FQ, TPC], F32).ap()
        cc_mx_in = nc.dram_tensor("cc_mx_in", [1, 2], F32).ap()
        cc_mx_out = [nc.dram_tensor(f"cc_mx_out{i}", [1, 2], F32).ap() for i in range(8)]

    TE = Eng(cx, "te", nc.tensor)
    VE = Eng(cx, "ve", nc.vector)
    SC = Eng(cx, "sc", nc.scalar)
    GP = Eng(cx, "gp", nc.gpsimd)
    SY = Eng(cx, "sy", nc.sync)
    souts = cx.mksem("souts")

    def full_barrier():
        toks = [(E, E.seal()) for E in (TE, VE, SC, GP, SY)]
        for E, _ in toks:
            for E2, (sh, n) in toks:
                if E2 is not E and n > 0:
                    E.raw.wait_ge(sh.h, n)

    resid = cx.sb("resid", [128, NT * 1024], F32)
    stats1 = cx.sb("stats1", [128, NT * 2], F32)
    idf = cx.sb("idf", [128, 128], F32)
    idb = cx.sb("idb", [128, 128], BF16)
    onesf = cx.sb("onesf", [128, 128], F32)
    onesb = cx.sb("onesb", [128, 128], BF16)
    mg0i = cx.sb("mg0i", [128, 128], F32)
    inclb = cx.sb("inclb", [128, 128], BF16)
    iotac = cx.sb("iotac", [128, 1], F32)
    i127c = cx.sb("i127c", [128, 1], F32)

    t0 = cx.sb("t0", [128, 512], F32)
    t1 = cx.sb("t1", [128, 512], F32)
    t2 = cx.sb("t2", [128, 512], F32)
    b0 = cx.sb("b0", [128, 512], BF16)
    b1 = cx.sb("b1", [128, 512], BF16)
    b2 = cx.sb("b2", [128, 512], BF16)
    b3 = cx.sb("b3", [128, 512], BF16)
    cc = [cx.sb(f"cc{i}", [128, 1], F32) for i in range(8)]
    wtmp = cx.sb("wtmp", [128, 1024], F32)
    wtmpb = cx.sb("wtmpb", [128, 1024], BF16)
    w64 = wtmp[0:64, :]
    w64b = wtmpb[0:64, :]
    brow = wtmp[0:1, :]
    browb = wtmpb[0:1, :]
    rankc = cx.sb("rankc", [128, 8], F32)
    u0 = cx.sb("u0", [128, 1024], F32)
    u1 = cx.sb("u1", [128, 1024], F32)
    mgc = cx.sb("mgc", [128, 1], F32)
    nmgc = cx.sb("nmgc", [128, 1], F32)

    aa = []
    xfmb = cx.sb("xfmb", [128, ND * 128], BF16, pool=aa)
    maskX = {m: cx.sb(f"maskX{m}", [128, 128], F32, pool=aa) for m in "fs"}
    maskS = {m: cx.sb(f"maskS{m}", [128, 128], BF16, pool=aa) for m in "fs"}
    maskI = {m: cx.sb(f"maskI{m}", [128, 128], BF16, pool=aa) for m in "fs"}
    d1col = {m: cx.sb(f"d1col{m}", [128, 1], F32, pool=aa) for m in "fs"}
    d2bcol = {m: cx.sb(f"d2bcol{m}", [128, 1], F32, pool=aa) for m in "fs"}
    lamC = {m: cx.sb(f"lamC{m}", [128, 1], F32, pool=aa) for m in "fs"}
    betac = {m: cx.sb(f"betac{m}", [128, 1], F32, pool=aa) for m in "fs"}
    lamc = {m: cx.sb(f"lamc{m}", [128, 1], F32, pool=aa) for m in "fs"}

    wall = cx.sb("wall", [128, ND * 392], BF16, pool=aa)
    wbarr = cx.sb("wbarr", [128, 392], F32, pool=aa)
    nmb1r = cx.sb("nmb1r", [128, 128], F32, pool=aa)
    nmb2r = cx.sb("nmb2r", [128, 64], F32, pool=aa)
    irb1r = cx.sb("irb1r", [128, 64], F32, pool=aa)
    ir2r = cx.sb("ir2r", [128, 64], F32, pool=aa)
    irb2c = cx.sb("irb2c", [128, 1], F32, pool=aa)
    nwr = cx.sb("nwr", [128, 64], F32, pool=aa)
    nm2T = cx.sb("nm2T", [128, 64], BF16, pool=aa)
    woT = cx.sb("woT", [64, 1024], BF16, pool=aa)

    kbf = cx.sb("kbf", [128, NT * 64], BF16, pool=aa)
    qbf = cx.sb("qbf", [128, NT * 64], BF16, pool=aa)
    vsb = cx.sb("vsb", [128, NT * 64], F32, pool=aa)
    nusb = cx.sb("nusb", [128, NT * 64], BF16, pool=aa)
    pwtsb = cx.sb("pwtsb", [128, NT * 4], F32, pool=aa)
    ucol = cx.sb("ucol", [128, NT], F32, pool=aa)
    oprt = cx.sb("oprt", [128, NT * 64], BF16, pool=aa)
    nsloc = cx.sb("nsloc", [128, NT * 64], BF16, pool=aa)
    pcol = cx.sb("pcol", [128, NT], F32, pool=aa)
    pcr = cx.sb("pcr", [1, NT], F32, pool=aa)
    rcr = cx.sb("rcr", [1, NT * 64], BF16, pool=aa)
    mn = {m: cx.sb(f"mn{m}", [64, 128], F32, pool=aa) for m in "fs"}
    mnb = {m: cx.sb(f"mnb{m}", [64, 128], BF16, pool=aa) for m in "fs"}
    s0f = {m: cx.sb(f"s0f{m}", [64, 64], F32, pool=aa) for m in "fs"}
    s0b = {m: cx.sb(f"s0b{m}", [64, 64], BF16, pool=aa) for m in "fs"}

    proj = u0[:, 0:392]
    solR = u1[:, 0:128]
    gsf = u1[:, 128:256]
    solRb = cx.sb("solRb", [128, 128], BF16, pool=aa)
    xrb = [wtmpb[:, j * 128:(j + 1) * 128] for j in range(2)]
    _wtb = wtmp[:].bitcast(BF16)
    xcp = [wtmpb[:, 256 + j * 128:384 + j * 128] for j in range(6)] + [_wtb[:, 0:128]]

    pb = [cx.ps(f"pb{i}", [128, 512], F32) for i in range(6)]
    pf = [cx.ps(f"pf{i}", [128, 1024], BF16) for i in range(2)]
    with nc.Block() as block:
        xst = xs.rearrange("(n p) d -> n p d", p=128)
        for i in range(NT):
            SY.dma(resid[:, i * 1024:(i + 1) * 1024], xst[i])
        ld_resid = (SY.sem, SY.sem.n)

        e_str = t0[:, 0:128]
        mg0s = t0[:, 128:256]
        GP.op("iota", t1[:, 0:128].bitcast(I32), pattern=[[1, 128]], base=-1, channel_multiplier=-1)
        GP.op("iota", t1[:, 128:129].bitcast(I32), pattern=[[0, 1]], base=0, channel_multiplier=1)
        GP.op("iota", t1[:, 129:130].bitcast(I32), pattern=[[0, 1]], base=127, channel_multiplier=-1)
        VE.wait(GP)
        VE.op("tensor_copy", e_str, t1[:, 0:128].bitcast(I32))
        VE.op("tensor_copy", iotac[:], t1[:, 128:129].bitcast(I32))
        VE.op("tensor_copy", i127c[:], t1[:, 129:130].bitcast(I32))
        VE.wait(VE)
        VE.op("tensor_scalar", mg0s, e_str, 0.0, None, OP.is_ge)
        VE.op("tensor_scalar", mg0i[:], e_str, -1.0, None, OP.is_ge)
        VE.wait(VE)
        VE.op("tensor_copy", inclb[:], mg0i[:])
        VE.op("tensor_scalar", idf[:], e_str, -1.0, None, OP.is_equal)
        VE.wait(VE)
        VE.op("tensor_copy", idb[:], idf[:])
        VE.op("memset", onesf[:], 1.0)
        VE.wait(VE)
        VE.op("tensor_copy", onesb[:], onesf[:])

        alin = {"f": cc[0], "s": cc[1]}
        alln = {"f": cc[2], "s": cc[3]}
        alb = {"f": cc[4], "s": cc[5]}
        GP.dma(cc[0][0:1, 0:1], fa_in)
        GP.dma(cc[1][0:1, 0:1], sa_in)
        GP.dma(rankc[0:1, :], rankm)
        SC.wait(GP)
        for m in "fs":
            SC.op("activation", alln[m][0:1, 0:1], alin[m][0:1, 0:1], AF.Ln, bias=0.0, scale=1.0)
            SC.op("activation", alb[m][0:1, 0:1], alin[m][0:1, 0:1], AF.Identity, bias=1.0, scale=-1.0)
        TE.wait(SC, VE)
        for i, m in enumerate("fs"):
            TE.op("matmul", pb[0][:, i:i + 1], onesf[0:1, :], alln[m][0:1, 0:1], start=True, stop=True)
            TE.op("matmul", pb[0][:, 2 + i:3 + i], onesf[0:1, :], alb[m][0:1, 0:1], start=True, stop=True)
        TE.op("matmul", pb[0][:, 4:12], onesf[0:1, :], rankc[0:1, :], start=True, stop=True)
        VE.wait(TE)
        lnrep = {"f": cc[6], "s": cc[7]}
        for i, m in enumerate("fs"):
            VE.op("tensor_copy", lnrep[m][:], pb[0][:, i:i + 1])
            VE.op("tensor_copy", betac[m][:], pb[0][:, 2 + i:3 + i])
        VE.op("tensor_copy", rankc[:], pb[0][:, 4:12])
        for m in "fs":
            VE.op("tensor_scalar", lamc[m][:], betac[m][:], -1.0, 1.0, OP.mult, OP.add)
            VE.op("tensor_tensor", t2[:, 0:1], iotac[:], lnrep[m][:], OP.mult)
            VE.op("tensor_tensor", t2[:, 1:2], i127c[:], lnrep[m][:], OP.mult)
            VE.op("tensor_scalar", t2[:, 2:3], lnrep[m][:], 128.0, None, OP.mult)
            VE.op("tensor_scalar", t2[:, 128:256], e_str, lnrep[m][:], None, OP.mult)
            SC.wait(VE)
            SC.op("activation", d1col[m][:], t2[:, 0:1], AF.Exp, bias=0.0, scale=1.0)
            SC.op("activation", cc[3][:], t2[:, 1:2], AF.Exp, bias=0.0, scale=1.0)
            SC.op("activation", lamC[m][:], t2[:, 2:3], AF.Exp, bias=0.0, scale=1.0)
            SC.op("activation", t1[:, 0:128], t2[:, 128:256], AF.Exp, bias=0.0, scale=1.0)
            VE.wait(SC)
            VE.op("tensor_tensor", d2bcol[m][:], cc[3][:], betac[m][:], OP.mult)
            VE.op("tensor_tensor", t1[:, 128:256], t1[:, 0:128], mg0s, OP.mult)
            VE.wait(VE)
            VE.op("tensor_copy", maskS[m][:], t1[:, 128:256])
            VE.op("tensor_scalar", maskX[m][:], t1[:, 128:256], betac[m][:], None, OP.mult)
            VE.op("tensor_tensor", t1[:, 256:384], t1[:, 0:128], mg0i[:], OP.mult)
            VE.wait(VE)
            VE.op("tensor_scalar", t1[:, 384:512], t1[:, 256:384], lamc[m][:], None, OP.mult)
            VE.wait(VE)
            VE.op("tensor_copy", maskI[m][:], t1[:, 384:512])

        def load_t_cols(wap, col, rows=64):
            SY.wait(TE, VE)
            SY.dma(w64[0:rows, :], wap)
            VE.wait(SY)
            VE.op("tensor_copy", w64b[0:rows, :], w64[0:rows, :])
            TE.wait(VE)
            for d in range(ND):
                TE.op("transpose", pf[d % 2][:, 0:rows], w64b[0:rows, d * 128:(d + 1) * 128], idb[0:rows, 0:rows])
                VE.wait(TE)
                VE.op("tensor_copy", wall[:, d * 392 + col: d * 392 + col + rows], pf[d % 2][:, 0:rows])
                TE.wait(VE)

        load_t_cols(aWk, 0)
        load_t_cols(aWq, 64)
        load_t_cols(aWv, 128)
        load_t_cols(ir_w1, 192)
        load_t_cols(pw, 256, rows=3)
        SY.wait(TE, VE)
        SY.dma(t2[:, 0:64], nm_w1)
        SY.dma(w64, aWv)
        VE.wait(SY)
        VE.op("tensor_copy", w64b, w64)
        VE.op("tensor_copy", b0[:, 0:64], t2[:, 0:64])
        TE.wait(VE)
        TE.op("transpose", pf[0][0:64, 0:128], b0[:, 0:64], idb[:])
        VE.wait(TE)
        VE.op("tensor_copy", b1[0:64, 0:128], pf[0][0:64, 0:128])
        TE.wait(VE)
        TE.op("matmul", pb[0][:], b1[0:64, 0:128], w64b[:, 0:512], start=True, stop=True)
        TE.op("matmul", pb[1][:], b1[0:64, 0:128], w64b[:, 512:1024], start=True, stop=True)
        VE.wait(TE)
        VE.op("tensor_copy", wtmpb[:, 0:512], pb[0][:])
        VE.op("tensor_copy", wtmpb[:, 512:1024], pb[1][:])
        TE.wait(VE)
        for d in range(ND):
            TE.op("transpose", pf[d % 2][:, 0:128], wtmpb[:, d * 128:(d + 1) * 128], idb[:])
            VE.wait(TE)
            VE.op("tensor_copy", wall[:, d * 392 + 259: d * 392 + 387], pf[d % 2][:, 0:128])
            TE.wait(VE)
        for d in range(ND):
            VE.op("memset", wall[:, d * 392 + 387:(d + 1) * 392], 0.0)
        TE.wait(VE)
        for d in range(ND):
            TE.op("matmul", pb[2][0:1, 0:392], onesb[:, 0:1], wall[:, d * 392: d * 392 + 392],
                  start=(d == 0), stop=(d == ND - 1))
        VE.wait(TE)
        VE.op("tensor_copy", b0[0:1, 0:392], pb[2][0:1, 0:392])
        TE.wait(VE)
        TE.op("matmul", pb[2][:, 0:392], onesb[0:1, :], b0[0:1, 0:392], start=True, stop=True)
        VE.wait(TE)
        VE.op("tensor_copy", wbarr[:], pb[2][:, 0:392])

        def replicate(dst, src_ap, width):
            SY.wait(TE, VE)
            SY.dma(brow[0:1, 0:width], src_ap)
            VE.wait(SY)
            VE.op("tensor_copy", browb[0:1, 0:width], brow[0:1, 0:width])
            TE.wait(VE)
            TE.op("matmul", pb[3][:, 0:width], onesb[0:1, :], browb[0:1, 0:width], start=True, stop=True)
            VE.wait(TE)
            VE.op("tensor_copy", dst, pb[3][:, 0:width])
            TE.wait(VE)

        replicate(nmb1r[:], nm_b1, 128)
        replicate(nmb2r[:], nm_b2, 64)
        replicate(irb1r[:], ir_b1, 64)
        replicate(ir2r[:], ir_w2, 64)
        replicate(irb2c[:], ir_b2, 1)
        replicate(nwr[:], norm_w, 64)
        SY.wait(TE, VE)
        SY.dma(w64[:, 0:128], nm_w2)
        VE.wait(SY)
        VE.op("tensor_copy", w64b[:, 0:128], w64[:, 0:128])
        TE.wait(VE)
        TE.op("transpose", pf[0][:, 0:64], w64b[0:64, 0:128], idb[0:64, 0:64])
        VE.wait(TE)
        VE.op("tensor_copy", nm2T[:], pf[0][:, 0:64])
        for d in range(ND):
            SY.wait(TE, VE)
            SY.dma(wtmp[:, 0:64], aWo[d * 128:(d + 1) * 128, :])
            VE.wait(SY)
            VE.op("tensor_copy", b0[:, 0:64], wtmp[:, 0:64])
            TE.wait(VE)
            TE.op("transpose", pf[d % 2][0:64, 0:128], b0[:, 0:64], idb[:])
            VE.wait(TE)
            VE.op("tensor_copy", woT[:, d * 128:(d + 1) * 128], pf[d % 2][0:64, 0:128])
            TE.wait(VE)

        SC.wait_tok(ld_resid)
        VE.wait_tok(ld_resid)
        for i in range(NT):
            xt = resid[:, i * 1024:(i + 1) * 1024]
            SC.op("activation", t0[:, 0:512], xt[:, 0:512], AF.Square, bias=0.0, scale=1.0, accum_out=cc[0][:])
            SC.op("activation", t1[:, 0:512], xt[:, 512:1024], AF.Square, bias=0.0, scale=1.0, accum_out=cc[1][:])
            VE.op("tensor_reduce", cc[2][:], xt, axis=AX, op=OP.add)
            VE.wait(SC)
            VE.wait(VE)
            VE.op("tensor_tensor", cc[3][:], cc[0][:], cc[1][:], OP.add)
            VE.op("tensor_scalar", cc[2][:], cc[2][:], 1.0 / 1024.0, None, OP.mult)
            SC.wait(VE)
            SC.op("activation", cc[4][:], cc[2][:], AF.Square, bias=0.0, scale=1.0)
            VE.wait(SC)
            VE.op("tensor_scalar", cc[3][:], cc[3][:], 1.0 / 1024.0, None, OP.mult)
            VE.wait(VE)
            VE.op("tensor_tensor", cc[3][:], cc[3][:], cc[4][:], OP.subtract)
            VE.wait(VE)
            VE.op("tensor_scalar", cc[5][:], cc[3][:], 1e-5, None, OP.add)
            SC.wait(VE)
            SC.op("activation", cc[4][:], cc[5][:], AF.Sqrt, bias=0.0, scale=1.0)
            VE.wait(SC)
            VE.op("reciprocal", stats1[:, 2 * i:2 * i + 1], cc[4][:])
            VE.wait(VE)
            VE.op("tensor_tensor", cc[4][:], cc[2][:], stats1[:, 2 * i:2 * i + 1], OP.mult)
            VE.wait(VE)
            VE.op("tensor_scalar", stats1[:, 2 * i + 1:2 * i + 2], cc[4][:], -1.0, None, OP.mult)
        if "attn_a" in PHASES:
            for i in range(NT):
                xt = resid[:, i * 1024:(i + 1) * 1024]
                rstd = stats1[:, 2 * i:2 * i + 1]
                nmrs = stats1[:, 2 * i + 1:2 * i + 2]
                TE.wait(VE)
                for d in range(ND):
                    TE.op("transpose", pb[1][:, 0:128], xt[:, d * 128:(d + 1) * 128], idf[:])
                    VE.wait(TE)
                    VE.op("tensor_copy", xfmb[:, d * 128:(d + 1) * 128], pb[1][:, 0:128])
                    TE.wait(VE)
                for d in range(ND):
                    TE.op("matmul", pb[0][:, 0:392], xfmb[:, d * 128:(d + 1) * 128],
                          wall[:, d * 392:(d + 1) * 392], start=(d == 0), stop=(d == ND - 1))
                VE.wait(TE)
                VE.op("tensor_scalar", t0[:, 0:392], pb[0][:, 0:392], rstd, None, OP.mult)
                VE.op("scalar_tensor_tensor", proj, wbarr[:], nmrs, t0[:, 0:392], OP.mult, OP.add)
                VE.wait(VE)

                SC.wait(VE)
                SC.op("activation", t0[:, 0:64], proj[:, 0:64], AF.Square, bias=0.0, scale=1.0, accum_out=cc[0][:])
                SC.op("activation", t0[:, 64:128], proj[:, 64:128], AF.Square, bias=0.0, scale=1.0, accum_out=cc[1][:])
                SC.wait(SC)
                SC.op("activation", cc[2][:], cc[0][:], AF.Sqrt, bias=0.0, scale=1.0)
                SC.op("activation", cc[3][:], cc[1][:], AF.Sqrt, bias=0.0, scale=1.0)
                VE.wait(SC)
                VE.op("tensor_scalar", cc[2][:], cc[2][:], 1e-12, None, OP.max)
                VE.op("tensor_scalar", cc[3][:], cc[3][:], 1e-12, None, OP.max)
                VE.wait(VE)
                VE.op("reciprocal", cc[0][:], cc[2][:])
                VE.op("reciprocal", cc[1][:], cc[3][:])
                VE.wait(VE)
                VE.op("tensor_scalar", kbf[:, i * 64:(i + 1) * 64], proj[:, 0:64], cc[0][:], None, OP.mult)
                VE.op("tensor_scalar", qbf[:, i * 64:(i + 1) * 64], proj[:, 64:128], cc[1][:], None, OP.mult)

                VE.op("tensor_copy", vsb[:, i * 64:(i + 1) * 64], proj[:, 128:192])
                VE.op("tensor_tensor", t0[:, 0:128], proj[:, 259:387], nmb1r[:], OP.add)
                SC.wait(VE)
                SC.op("activation", t1[:, 0:128], t0[:, 0:128], AF.Gelu, bias=0.0, scale=1.0)
                VE.wait(SC)
                VE.op("tensor_copy", b0[:, 0:128], t1[:, 0:128])
                TE.wait(VE)
                TE.op("transpose", pf[0][:, 0:128], b0[:, 0:128], idb[:])
                VE.wait(TE)
                VE.op("tensor_copy", b1[:, 0:128], pf[0][:, 0:128])
                TE.wait(VE)
                TE.op("matmul", pb[1][:, 0:64], b1[:, 0:128], nm2T[:], start=True, stop=True)
                VE.wait(TE)
                VE.op("tensor_tensor", nusb[:, i * 64:(i + 1) * 64], pb[1][:, 0:64], nmb2r[:], OP.add)

                VE.op("tensor_tensor", t0[:, 192:256], proj[:, 192:256], irb1r[:], OP.add)
                SC.wait(VE)
                SC.op("activation", t0[:, 128:192], t0[:, 192:256], AF.Gelu, bias=0.0, scale=1.0)
                VE.wait(SC)
                VE.op("tensor_tensor", t1[:, 0:64], t0[:, 128:192], ir2r[:], OP.mult)
                VE.wait(VE)
                VE.op("tensor_reduce", cc[0][:], t1[:, 0:64], axis=AX, op=OP.add)
                VE.wait(VE)
                VE.op("tensor_tensor", cc[1][:], cc[0][:], irb2c[:], OP.add)
                VE.wait(VE)
                VE.op("tensor_scalar", ucol[:, i:i + 1], cc[1][:], 0.0, None, OP.is_gt)

                VE.op("tensor_reduce", cc[0][:], proj[:, 256:259], axis=AX, op=OP.max)
                VE.wait(VE)
                VE.op("tensor_scalar", cc[1][:], cc[0][:], -1.0, None, OP.mult)
                SC.wait(VE)
                SC.op("activation", t0[:, 0:3], proj[:, 256:259], AF.Exp, bias=cc[1][:], scale=1.0)
                VE.wait(SC)
                VE.op("tensor_reduce", cc[2][:], t0[:, 0:3], axis=AX, op=OP.add)
                VE.wait(VE)
                VE.op("reciprocal", cc[2][:], cc[2][:])
                VE.wait(VE)
                VE.op("tensor_scalar", pwtsb[:, i * 4:i * 4 + 3], t0[:, 0:3], cc[2][:], None, OP.mult)

                TE.wait(VE)
                TE.op("transpose", pf[0][0:64, 0:128], kbf[:, i * 64:(i + 1) * 64], idb[:])
                TE.op("transpose", pf[1][0:64, 0:128], qbf[:, i * 64:(i + 1) * 64], idb[:])
                VE.wait(TE)
                VE.op("tensor_copy", b0[0:64, 0:128], pf[0][0:64, 0:128])
                VE.op("tensor_copy", b1[0:64, 0:128], pf[1][0:64, 0:128])
                TE.wait(VE)
                TE.op("matmul", pb[2][:, 0:128], b0[0:64, 0:128], b0[0:64, 0:128], start=True, stop=True)
                TE.op("matmul", pb[3][:, 0:128], b0[0:64, 0:128], b1[0:64, 0:128], start=True, stop=True)
                VE.wait(TE)
                VE.wait(GP)
                VE.op("tensor_copy", gsf, pb[2][:, 0:128])
                VE.op("tensor_copy", b3[:, 0:128], pb[3][:, 0:128])
                GP.wait(VE)
                GP.dma(spillQ[i], b3[:, 0:128])

                for mi, m in enumerate("fs"):
                    VE.wait(GP)
                    if m == "f":
                        VE.op("tensor_tensor", xcp[0], gsf, maskX[m][:], OP.mult)
                    else:
                        VE.op("scalar_tensor_tensor", xcp[0], gsf, ucol[:, i:i + 1], maskX[m][:], OP.mult, OP.mult)
                    TE.wait(VE)
                    TE.op("transpose", pf[0][:, 0:128], xcp[0], idb[:])
                    VE.wait(TE)
                    VE.op("tensor_copy", xrb[0], pf[0][:, 0:128])
                    VE.op("tensor_copy", solR[:, 0:64], vsb[:, i * 64:(i + 1) * 64])
                    VE.op("tensor_scalar", solR[:, 64:128], kbf[:, i * 64:(i + 1) * 64], d1col[m][:], None, OP.mult)
                    VE.wait(VE)
                    VE.op("tensor_copy", solRb[:], solR)
                    TE.wait(VE)
                    for lev in range(6):
                        TE.op("matmul", pb[1][:, 0:128], xcp[lev], xrb[lev % 2], start=True, stop=True)
                        VE.wait(TE)
                        VE.op("tensor_copy", xrb[(lev + 1) % 2], pb[1][:, 0:128])
                        TE.wait(VE)
                        TE.op("transpose", pf[0][:, 0:128], xrb[(lev + 1) % 2], idb[:])
                        VE.wait(TE)
                        VE.op("tensor_copy", xcp[lev + 1], pf[0][:, 0:128])
                        TE.wait(VE)
                    for j in range(6, -1, -1):
                        TE.op("matmul", pb[3][:, 0:128], xcp[j], solRb[:], start=True, stop=True)
                        VE.wait(TE)
                        VE.op("tensor_tensor", solR, solR, pb[3][:, 0:128],
                              OP.add if j > 0 else OP.subtract)
                        VE.wait(VE)
                        VE.op("tensor_copy", solRb[:], solR)
                        TE.wait(VE)
                    if m == "f":
                        VE.op("tensor_scalar", b2[:, 0:64], kbf[:, i * 64:(i + 1) * 64], d2bcol[m][:], None, OP.mult)
                    else:
                        VE.op("tensor_scalar", b2[:, 0:64], kbf[:, i * 64:(i + 1) * 64], d2bcol[m][:], ucol[:, i:i + 1], OP.mult, OP.mult)
                    GP.wait(VE)
                    GP.dma(spillA[2 * i + mi][:, 0:128], solRb[:])
                    GP.dma(spillA[2 * i + mi][:, 128:192], b2[:, 0:64])
                    TE.wait(VE)
                    TE.op("matmul", pb[4][0:64, 0:128], b2[:, 0:64], solRb[:], start=True, stop=True)
                    VE.wait(TE)
                    VE.op("scalar_tensor_tensor", t0[0:64, 0:64], idf[0:64, 0:64], lamC[m][0:64, :], pb[4][0:64, 64:128], OP.mult, OP.subtract)
                    VE.op("tensor_copy", t1[0:64, 64:128], pb[4][0:64, 0:64])
                    VE.wait(VE)
                    VE.op("tensor_copy", b2[0:64, 64:128], t0[0:64, 0:64])
                    if i == 0:
                        VE.op("tensor_copy", mn[m][0:64, 64:128], t0[0:64, 0:64])
                        VE.op("tensor_copy", mn[m][0:64, 0:64], pb[4][0:64, 0:64])
                        VE.wait(VE)
                        VE.op("tensor_copy", mnb[m][:], mn[m][:])
                    else:
                        TE.wait(VE)
                        TE.op("transpose", pf[0][0:64, 0:64], b2[0:64, 64:128], idb[0:64, 0:64])
                        VE.wait(TE)
                        VE.op("tensor_copy", b3[0:64, 128:192], pf[0][0:64, 0:64])
                        TE.wait(VE)
                        TE.op("matmul", pb[5][0:64, 0:128], b3[0:64, 128:192], mnb[m][:], start=True, stop=True)
                        VE.wait(TE)
                        VE.op("tensor_tensor", mn[m][0:64, 0:64], pb[5][0:64, 0:64], t1[0:64, 64:128], OP.add)
                        VE.op("tensor_copy", mn[m][0:64, 64:128], pb[5][0:64, 64:128])
                        VE.wait(VE)
                        VE.op("tensor_copy", mnb[m][:], mn[m][:])
                    TE.wait(VE)

            GP.wait(VE)
            tap(GP, "stats1", stats1[:])
            tap(GP, "wall", wall[:])
            tap(GP, "xfmbL", xfmb[:])
            tap(GP, "wbarr", wbarr[:])
            tap(GP, "kbf", kbf[:])
            tap(GP, "qbf", qbf[:])
            tap(GP, "vsb", vsb[:])
            tap(GP, "nusb", nusb[:])
            tap(GP, "ucol", ucol[:])
            tap(GP, "pwtsb", pwtsb[:])
            tap(GP, "mnf", mn["f"][0:64, :])
            tap(GP, "mns", mn["s"][0:64, :])
            tap(GP, "solA_d", spillA)
            GP.wait(VE)
            GP.dma(ccA_in[:, 0:64], mn["f"][0:64, 0:64])
            GP.dma(ccA_in[:, 64:128], mn["f"][0:64, 64:128])
            GP.dma(ccA_in[:, 128:192], mn["s"][0:64, 0:64])
            GP.dma(ccA_in[:, 192:256], mn["s"][0:64, 64:128])
            GP.wait(GP)
            tokA = GP.coll("AllGather", OP.bypass,
                           replica_groups=[[0, 1, 2, 3], [4, 5, 6, 7]],
                           ins=[ccA_in.opt()], outs=[ccA_out.opt()])
            gat = wtmp[0:64, :]
            GP.wait_tok(tokA)
            GP.wait(TE, VE)
            for r in range(4):
                GP.dma(gat[:, r * 256:(r + 1) * 256], ccA_out[r * 64:(r + 1) * 64, :])
            VE.wait(GP)
            VE.op("tensor_scalar", t2[:, 200:208], rankc[:], -1.0, 1.0, OP.mult, OP.add)
            for mi, m in enumerate("fs"):
                VE.op("memset", s0f[m][:], 0.0)
                VE.op("memset", s0b[m][:], 0.0)
            VE.wait(VE)
            for r in range(4):
                for mi, m in enumerate("fs"):
                    Nj = gat[0:64, r * 256 + mi * 128: r * 256 + mi * 128 + 64]
                    Mj = gat[0:64, r * 256 + mi * 128 + 64: r * 256 + mi * 128 + 128]
                    rj = rankc[0:64, r:r + 1]
                    rnj = t2[0:64, 200 + r:201 + r]
                    VE.op("tensor_scalar", t0[0:64, 0:64], Mj, rj, None, OP.mult)
                    VE.wait(VE)
                    VE.op("scalar_tensor_tensor", t0[0:64, 0:64], idf[0:64, 0:64], rnj, t0[0:64, 0:64], OP.mult, OP.add)
                    VE.wait(VE)
                    VE.op("tensor_copy", b2[0:64, 0:64], t0[0:64, 0:64])
                    VE.op("tensor_scalar", t1[0:64, 0:64], Nj, rj, None, OP.mult)
                    TE.wait(VE)
                    TE.op("transpose", pf[0][0:64, 0:64], b2[0:64, 0:64], idb[0:64, 0:64])
                    VE.wait(TE)
                    VE.op("tensor_copy", b3[0:64, 0:64], pf[0][0:64, 0:64])
                    TE.wait(VE)
                    TE.op("matmul", pb[0][0:64, 0:64], b3[0:64, 0:64], s0b[m][:], start=True, stop=True)
                    VE.wait(TE)
                    VE.op("tensor_tensor", s0f[m][:], pb[0][0:64, 0:64], t1[0:64, 0:64], OP.add)
                    VE.wait(VE)
                    VE.op("tensor_copy", s0b[m][:], s0f[m][:])

            GP.wait(VE)
            tap(GP, "gatA2", gat[0:64, :])
            tap(GP, "rankcT", rankc[:])
            tap(GP, "s0fA2", s0f["f"][:])
            tap(GP, "s0sA2", s0f["s"][:])
            solA = cx.sb("solA", [128, 192 * 2], BF16, pool=aa)
            qgr = cx.sb("qgr", [128, 128], BF16, pool=aa)
            pre = {"f": u1[:, 128:192], "s": u1[:, 192:256]}
            post = {"f": u1[:, 256:320], "s": u1[:, 320:384]}
            utl = u1[:, 64:128]
            t0sb = u1[:, 0:64]
            utlb = cx.sb("utlb", [128, 64], BF16, pool=aa)
            ub = cx.sb("ub", [128, 64], BF16, pool=aa)
            aqm = cx.sb("aqm", [128, 128], BF16, pool=aa)
            qcs = b3[0:64, 256:384]

            SY.wait(GP)
            for i in range(NT):
                SY.wait(TE, VE)
                SY.dma(solA[:, 0:192], spillA[2 * i])
                SY.dma(solA[:, 192:384], spillA[2 * i + 1])
                SY.dma(qgr[:], spillQ[i])
                TE.wait(SY)
                VE.wait(SY)
                for mi, m in enumerate("fs"):
                    sol = solA[:, mi * 192: mi * 192 + 128]
                    k2b = solA[:, mi * 192 + 128: mi * 192 + 192]
                    TE.op("transpose", pf[0][0:64, 0:128], sol[:, 64:128], idb[:])
                    VE.op("tensor_scalar", b0[:, 0:64], qbf[:, i * 64:(i + 1) * 64], d1col[m][:], None, OP.mult)
                    TE.wait(VE)
                    TE.op("transpose", pf[1][0:64, 0:128], b0[:, 0:64], idb[:])
                    VE.wait(TE)
                    VE.op("tensor_copy", b1[0:64, 0:128], pf[0][0:64, 0:128])
                    VE.op("tensor_copy", qcs, pf[1][0:64, 0:128])
                    TE.wait(VE)
                    TE.op("matmul", pb[0][:, 0:64], b1[0:64, 0:128], s0b[m][:], start=True, stop=True)
                    TE.op("matmul", pb[1][:, 0:64], qcs, s0b[m][:], start=True, stop=True)
                    VE.wait(TE)
                    VE.op("tensor_tensor", utl, sol[:, 0:64], pb[0][:, 0:64], OP.subtract)
                    VE.op("tensor_copy", t0sb, pb[1][:, 0:64])
                    VE.wait(VE)
                    VE.op("tensor_copy", utlb[:], utl)
                    if m == "f":
                        VE.op("tensor_scalar", t0[:, 0:64], utl, betac[m][:], None, OP.mult)
                    else:
                        VE.op("tensor_scalar", t0[:, 0:64], utl, betac[m][:], ucol[:, i:i + 1], OP.mult, OP.mult)
                    VE.wait(VE)
                    VE.op("tensor_copy", ub[:], t0[:, 0:64])
                    VE.op("tensor_tensor", aqm[:], qgr[:], maskS[m][:], OP.mult)
                    TE.wait(VE)
                    TE.op("matmul", pb[2][:, 0:64], aqm[:], ub[:], start=True, stop=True)
                    VE.wait(TE)
                    VE.op("tensor_tensor", pre[m], pb[2][:, 0:64], t0sb, OP.add)
                    VE.op("tensor_tensor", aqm[:], qgr[:], maskI[m][:], OP.mult)
                    TE.wait(VE)
                    TE.op("matmul", pb[2][:, 0:64], aqm[:], ub[:], start=True, stop=True)
                    TE.op("matmul", pb[3][0:64, 0:64], k2b[:], utlb[:], start=True, stop=True)
                    VE.wait(TE)
                    VE.op("scalar_tensor_tensor", post[m], t0sb, lamc[m][:], pb[2][:, 0:64], OP.mult, OP.add)
                    VE.op("scalar_tensor_tensor", s0f[m][:], s0f[m][:], lamC[m][0:64, :], pb[3][0:64, 0:64], OP.mult, OP.add)
                    VE.wait(VE)
                    VE.op("tensor_copy", s0b[m][:], s0f[m][:])
                    TE.wait(VE)

                VE.op("tensor_tensor", t0[:, 0:64], pre["f"], pre["s"], OP.add)
                VE.wait(VE)
                VE.op("tensor_scalar", t0[:, 64:128], t0[:, 0:64], -0.5, None, OP.mult)
                VE.wait(VE)
                VE.op("tensor_tensor", t1[:, 0:64], vsb[:, i * 64:(i + 1) * 64], t0[:, 64:128], OP.add)
                SC.wait(VE)
                SC.op("activation", t1[:, 64:128], t1[:, 0:64], AF.Square, bias=0.0, scale=1.0, accum_out=cc[0][:])
                VE.wait(SC)
                VE.op("tensor_scalar", cc[1][:], cc[0][:], 1.0 / (1.0 + 1e-6), LN7_3, OP.mult, OP.is_gt)
                VE.wait(VE)
                VE.op("tensor_copy", b0[:, 0:1], cc[1][:])
                TE.wait(VE)
                TE.op("matmul", pb[0][:, 0:1], inclb[:], b0[:, 0:1], start=True, stop=True)
                VE.wait(TE)
                VE.op("tensor_scalar", t0[:, 0:1], pb[0][:, 0:1], LN09, None, OP.mult)
                VE.wait(VE)
                SC.wait(VE)
                SC.op("activation", pcol[:, i:i + 1], t0[:, 0:1], AF.Exp, bias=0.0, scale=1.0)
                VE.op("tensor_scalar", t0[:, 1:2], t0[:, 0:1], -1.0, None, OP.mult)
                TE.wait(VE)
                TE.op("transpose", pb[1][0:1, 0:128], t0[:, 0:1], idf[:])
                VE.wait(TE)
                VE.op("tensor_copy", t2[0:1, 0:128], pb[1][0:1, 0:128])
                VE.wait(VE)
                VE.op("tensor_copy", b0[0:1, 0:128], t2[0:1, 0:128])
                TE.wait(VE)
                TE.op("matmul", pb[2][:, 0:128], onesb[0:1, :], b0[0:1, 0:128], start=True, stop=True)
                SC.wait(TE)
                SC.op("activation", t2[:, 256:384], pb[2][:, 0:128], AF.Exp, bias=t0[:, 1:2], scale=1.0)
                VE.wait(SC)
                VE.op("tensor_tensor", t2[:, 128:256], t2[:, 256:384], mg0i[:], OP.mult)
                VE.op("tensor_scalar", t1[:, 0:64], nusb[:, i * 64:(i + 1) * 64], cc[1][:], 0.1, OP.mult, OP.mult)
                VE.wait(VE)
                VE.op("tensor_copy", b1[:, 0:128], t2[:, 128:256])
                VE.op("tensor_copy", b2[:, 0:64], t1[:, 0:64])
                TE.wait(VE)
                TE.op("matmul", pb[3][:, 0:64], b1[:, 0:128], b2[:, 0:64], start=True, stop=True)
                TE.op("matmul", pb[4][0:1, 0:64], b1[:, 127:128], b2[:, 0:64], start=True, stop=True)
                VE.wait(TE)
                VE.op("tensor_copy", nsloc[:, i * 64:(i + 1) * 64], pb[3][:, 0:64])
                VE.op("tensor_copy", rcr[0:1, i * 64:(i + 1) * 64], pb[4][0:1, 0:64])
                SC.op("activation", pcr[0:1, i:i + 1], t2[0:1, 127:128], AF.Exp, bias=0.0, scale=1.0)
                VE.op("tensor_scalar", t0[:, 0:64], post["f"], pwtsb[:, i * 4:i * 4 + 1], None, OP.mult)
                VE.wait(VE)
                VE.op("scalar_tensor_tensor", oprt[:, i * 64:(i + 1) * 64], post["s"], pwtsb[:, i * 4 + 1:i * 4 + 2], t0[:, 0:64], OP.mult, OP.add)

            GP.wait(VE, SC)
            tap(GP, "prefL", pre["f"])
            tap(GP, "presL", pre["s"])
            tap(GP, "postfL", post["f"])
            tap(GP, "postsL", post["s"])
            tap(GP, "pcr", pcr[0:1, :])
            tap(GP, "rcr", rcr[0:1, :])
            tap(GP, "pcol", pcol[:])
            tap(GP, "nsloc", nsloc[:])
            tap(GP, "oprt", oprt[:])
            tap(GP, "s0fA3", s0f["f"][:])
            tap(GP, "s0sA3", s0f["s"][:])
            VE.wait(SC)
            VE.op("memset", t2[0:1, 0:64], 0.0)
            VE.op("memset", t2[0:1, 64:65], 1.0)
            VE.wait(VE)
            for i in range(NT):
                VE.op("scalar_tensor_tensor", t2[0:1, 0:64], t2[0:1, 0:64], pcr[0:1, i:i + 1], rcr[0:1, i * 64:(i + 1) * 64], OP.mult, OP.add)
                VE.op("tensor_tensor", t2[0:1, 64:65], t2[0:1, 64:65], pcr[0:1, i:i + 1], OP.mult)
                VE.wait(VE)
            GP.wait(VE)
            GP.dma(ccN_in[0:1, 0:1], t2[0:1, 64:65])
            GP.dma(ccN_in[0:1, 1:65], t2[0:1, 0:64])
            GP.wait(GP)
            tokN = GP.coll("AllGather", OP.bypass,
                           replica_groups=[[0, 1, 2, 3], [4, 5, 6, 7]],
                           ins=[ccN_in.opt()], outs=[ccN_out.opt()])
            gat2 = wtmp[0:1, 0:264]
            GP.wait_tok(tokN)
            GP.wait(TE, VE)
            GP.dma(gat2[0:1, :], ccN_out.rearrange("(o a) b -> o (a b)", o=1))
            VE.wait(GP)
            VE.op("memset", t2[0:1, 128:192], 0.0)
            VE.wait(VE)
            for r in range(4):
                Pj = gat2[0:1, r * 66: r * 66 + 1]
                rjv = gat2[0:1, r * 66 + 1: r * 66 + 65]
                rj = rankc[0:1, r:r + 1]
                VE.op("tensor_scalar", t0[0:1, 0:1], Pj, rj, None, OP.mult)
                VE.op("tensor_scalar", t0[0:1, 1:2], rj, -1.0, 1.0, OP.mult, OP.add)
                VE.wait(VE)
                VE.op("tensor_tensor", t0[0:1, 0:1], t0[0:1, 0:1], t0[0:1, 1:2], OP.add)
                VE.op("tensor_scalar", t0[0:1, 2:66], rjv, rj, None, OP.mult)
                VE.wait(VE)
                VE.op("scalar_tensor_tensor", t2[0:1, 128:192], t2[0:1, 128:192], t0[0:1, 0:1], t0[0:1, 2:66], OP.mult, OP.add)
                VE.wait(VE)
            ns0c = cx.sb("ns0c", [1, 64], F32, pool=aa)
            VE.op("tensor_copy", ns0c[0:1, :], t2[0:1, 128:192])
            VE.wait(VE)
            GP.wait(VE)
            tap(GP, "nscs", ns0c[0:1, :])

            for i in range(NT):
                VE.op("tensor_copy", browb[0:1, 0:64], ns0c[0:1, :])
                TE.wait(VE)
                TE.op("matmul", pb[0][:, 0:64], onesb[0:1, :], browb[0:1, 0:64], start=True, stop=True)
                VE.wait(TE)
                VE.op("scalar_tensor_tensor", t0[:, 0:64], pb[0][:, 0:64], pcol[:, i:i + 1], nsloc[:, i * 64:(i + 1) * 64], OP.mult, OP.add)
                VE.wait(VE)
                VE.op("scalar_tensor_tensor", t1[:, 0:64], t0[:, 0:64], pwtsb[:, i * 4 + 2:i * 4 + 3], oprt[:, i * 64:(i + 1) * 64], OP.mult, OP.add)
                SC.wait(VE)
                SC.op("activation", t2[:, 0:64], t1[:, 0:64], AF.Square, bias=0.0, scale=1.0, accum_out=cc[0][:])
                VE.wait(SC)
                VE.op("tensor_scalar", cc[1][:], cc[0][:], 1.0 / 64.0, 1e-6, OP.mult, OP.add)
                SC.wait(VE)
                SC.op("activation", cc[2][:], cc[1][:], AF.Sqrt, bias=0.0, scale=1.0)
                VE.wait(SC)
                VE.op("reciprocal", cc[3][:], cc[2][:])
                VE.wait(VE)
                VE.op("tensor_scalar", t2[:, 0:64], t1[:, 0:64], cc[3][:], None, OP.mult)
                VE.wait(VE)
                VE.op("tensor_tensor", t2[:, 64:128], t2[:, 0:64], nwr[:], OP.mult)
                VE.wait(VE)
                VE.op("tensor_copy", b0[:, 0:64], t2[:, 64:128])
                TE.wait(VE)
                TE.op("transpose", pf[0][0:64, 0:128], b0[:, 0:64], idb[:])
                VE.wait(TE)
                VE.op("tensor_copy", b1[0:64, 0:128], pf[0][0:64, 0:128])
                TE.wait(VE)
                TE.op("matmul", pb[1][:, 0:512], b1[0:64, 0:128], woT[:, 0:512], start=True, stop=True)
                TE.op("matmul", pb[2][:, 0:512], b1[0:64, 0:128], woT[:, 512:1024], start=True, stop=True)
                VE.wait(TE)
                VE.op("tensor_tensor", resid[:, i * 1024:i * 1024 + 512], resid[:, i * 1024:i * 1024 + 512], pb[1][:, 0:512], OP.add)
                VE.op("tensor_tensor", resid[:, i * 1024 + 512:i * 1024 + 1024], resid[:, i * 1024 + 512:i * 1024 + 1024], pb[2][:, 0:512], OP.add)
                VE.op("scalar_tensor_tensor", ns0c[0:1, :], ns0c[0:1, :], pcr[0:1, i:i + 1], rcr[0:1, i * 64:(i + 1) * 64], OP.mult, OP.add)
                VE.wait(VE)
                TE.wait(VE)

        full_barrier()
        cx.free_pool(aa)
        def attn_b_emit():
            SCALE_QK = float(64.0 ** -0.25)
            bbP = []
            Gtot = cx.sb("Gtot", [128, H * 66], F32, pool=bbP)
            Sext = cx.sb("Sext", [128, H * 66], F32, pool=bbP)
            Sexb = cx.sb("Sexb", [128, 66], BF16, pool=bbP)
            gamr = cx.sb("gamr", [128, 1024], BF16, pool=bbP)
            betr = cx.sb("betr", [128, 1024], BF16, pool=bbP)
            b2r = cx.sb("b2r", [128, 1024], F32, pool=bbP)
            b1c = cx.sb("b1c", [128, 32], F32, pool=bbP)
            omgT = cx.sb("omgT", [128, 128], BF16, pool=bbP)

            for src, dst, nr, ncs in ((bWq, bWq_sc, 8, 1), (bWk, bWk_sc, 8, 1), (bWv, bWv_sc, 8, 1),
                                      (bWo, bWo_sc, 8, 1), (bW1, bW1_sc, 8, 4), (bW2, bW2_sc, 32, 1)):
                w = src.rearrange("(a p) (b q) -> a b p q", p=128, q=1024)
                d2 = dst.rearrange("(a p) (b q) -> a b p q", p=128, q=1024)
                for r in range(nr):
                    for c_ in range(ncs):
                        SY.wait(VE, GP)
                        SY.dma(wtmp[:], w[r, c_])
                        VE.wait(SY)
                        VE.op("tensor_copy", wtmpb[:], wtmp[:])
                        GP.wait(VE)
                        GP.dma(d2[r, c_], wtmpb[:])
            SY.wait(VE, GP, TE)
            SY.dma(wtmp[:, 0:64], omega)
            VE.wait(SY)
            VE.wait(GP)
            VE.op("tensor_copy", wtmpb[:, 0:64], wtmp[:, 0:64])
            TE.wait(VE)
            TE.op("transpose", pf[0][0:64, 0:128], wtmpb[:, 0:64], idb[:])
            VE.wait(TE)
            VE.op("memset", omgT[:], 0.0)
            VE.wait(VE)
            VE.op("tensor_copy", omgT[0:64, :], pf[0][0:64, 0:128])
            VE.op("memset", omgT[64:65, :], -0.5)
            replicate(gamr[:, 0:512], gam[0:1, 0:512], 512)
            replicate(gamr[:, 512:1024], gam[0:1, 512:1024], 512)
            replicate(betr[:, 0:512], bet[0:1, 0:512], 512)
            replicate(betr[:, 512:1024], bet[0:1, 512:1024], 512)
            replicate(b2r[:, 0:512], bb2[0:1, 0:512], 512)
            replicate(b2r[:, 512:1024], bb2[0:1, 512:1024], 512)
            for rr in range(4):
                SY.wait(VE, TE)
                SY.dma(wtmp[0:1, :], bb1[0:1, rr * 1024:(rr + 1) * 1024])
                TE.wait(SY)
                for c8 in range(8):
                    ch = rr * 8 + c8
                    TE.op("transpose", pb[5][:, 0:1], wtmp[0:1, c8 * 128:(c8 + 1) * 128], idf[0:1, 0:1])
                    VE.wait(TE)
                    VE.op("tensor_copy", b1c[:, ch:ch + 1], pb[5][:, 0:1])
                    TE.wait(VE)
            full_barrier()
            for i in range(NT):
                xt = resid[:, i * 1024:(i + 1) * 1024]
                SC.wait(VE)
                SC.op("activation", t0[:, 0:512], xt[:, 0:512], AF.Square, bias=0.0, scale=1.0, accum_out=cc[0][:])
                SC.op("activation", t1[:, 0:512], xt[:, 512:1024], AF.Square, bias=0.0, scale=1.0, accum_out=cc[1][:])
                VE.op("tensor_reduce", cc[2][:], xt, axis=AX, op=OP.add)
                VE.wait(SC)
                VE.wait(VE)
                VE.op("tensor_tensor", cc[3][:], cc[0][:], cc[1][:], OP.add)
                VE.op("tensor_scalar", cc[2][:], cc[2][:], 1.0 / 1024.0, None, OP.mult)
                SC.wait(VE)
                SC.op("activation", cc[4][:], cc[2][:], AF.Square, bias=0.0, scale=1.0)
                VE.wait(SC)
                VE.op("tensor_scalar", cc[3][:], cc[3][:], 1.0 / 1024.0, None, OP.mult)
                VE.wait(VE)
                VE.op("tensor_tensor", cc[3][:], cc[3][:], cc[4][:], OP.subtract)
                VE.wait(VE)
                VE.op("tensor_scalar", cc[5][:], cc[3][:], 1e-5, None, OP.add)
                SC.wait(VE)
                SC.op("activation", cc[4][:], cc[5][:], AF.Sqrt, bias=0.0, scale=1.0)
                VE.wait(SC)
                VE.op("reciprocal", stats1[:, 2 * i:2 * i + 1], cc[4][:])
                VE.wait(VE)
                VE.op("tensor_tensor", cc[4][:], cc[2][:], stats1[:, 2 * i:2 * i + 1], OP.mult)
                VE.wait(VE)
                VE.op("tensor_scalar", stats1[:, 2 * i + 1:2 * i + 2], cc[4][:], -1.0, None, OP.mult)
            VE.op("memset", Gtot[:], 0.0)

            bb1p = []
            l3f = cx.sb("l3f", [128, 8 * 1024], BF16, pool=bb1p)
            qaug = cx.sb("qaug", [128, 1024], BF16, pool=bb1p)
            wst = [cx.sb(f"wst{j}", [128, 128], BF16, pool=bb1p) for j in range(2)]
            vext = cx.sb("vext", [128, 66], BF16, pool=bb1p)
            vtm = cx.sb("vtm", [128, 128], BF16, pool=bb1p)
            VE.op("memset", vext[:], 0.0)
            VE.wait(VE)
            VE.op("memset", vext[:, 64:65], 1.0)
            for g in range(4):
                for ii in range(8):
                    i = g * 8 + ii
                    xt = resid[:, i * 1024:(i + 1) * 1024]
                    VE.wait(TE)
                    VE.op("tensor_scalar", u0[:], xt, stats1[:, 2 * i:2 * i + 1], stats1[:, 2 * i + 1:2 * i + 2], OP.mult, OP.add)
                    VE.wait(VE)
                    VE.op("tensor_copy", wtmpb[:], u0[:])
                    TE.wait(VE)
                    for d in range(ND):
                        TE.op("transpose", pf[d % 2][:, 0:128], wtmpb[:, d * 128:(d + 1) * 128], idb[:])
                        VE.wait(TE)
                        VE.op("tensor_copy", l3f[:, d * 1024 + ii * 128: d * 1024 + ii * 128 + 128], pf[d % 2][:, 0:128])
                        TE.wait(VE)
                for wi, (wsc, kind) in enumerate(((bWv_sc, "v"), (bWk_sc, "k"), (bWq_sc, "q"))):
                    for o8 in range(8):
                        pA = pb[0] if o8 % 2 == 0 else pb[2]
                        pB = pb[1] if o8 % 2 == 0 else pb[3]
                        wpar = o8 % 2
                        for d in range(ND):
                            SY.wait(TE)
                            SY.dma(wst[wpar][:], wsc[d * 128:(d + 1) * 128, o8 * 128:(o8 + 1) * 128])
                            TE.wait(SY)
                            TE.op("matmul", pA[:, 0:512], wst[wpar][:], l3f[:, d * 1024:d * 1024 + 512], start=(d == 0), stop=(d == ND - 1))
                            TE.op("matmul", pB[:, 0:512], wst[wpar][:], l3f[:, d * 1024 + 512:d * 1024 + 1024], start=(d == 0), stop=(d == ND - 1))
                        VE.wait(TE)
                        if kind == "v":
                            VE.op("tensor_copy", b0[:, 0:512], pA[:, 0:512])
                            VE.op("tensor_copy", b1[:, 0:512], pB[:, 0:512])
                            TE.wait(VE)
                            for tt in range(8):
                                src = b0 if tt < 4 else b1
                                TE.op("transpose", pf[tt % 2][:, 0:128], src[:, (tt % 4) * 128:(tt % 4) * 128 + 128], idb[:])
                                VE.wait(TE)
                                VE.op("tensor_copy", vtm[:], pf[tt % 2][:, 0:128])
                                GP.wait(VE)
                                GP.dma(v_sc[g * 1024 + tt * 128: g * 1024 + (tt + 1) * 128, o8 * 128:(o8 + 1) * 128], vtm[:])
                                VE.wait(GP)
                                TE.wait(VE)
                        else:
                            VE.op("tensor_scalar", u0[:, 0:512], pA[:, 0:512], SCALE_QK, None, OP.mult)
                            VE.op("tensor_scalar", u0[:, 512:1024], pB[:, 0:512], SCALE_QK, None, OP.mult)
                            SC.wait(VE)
                            SC.op("activation", u1[:], u0[:], AF.Square, bias=0.0, scale=1.0)
                            VE.wait(SC)
                            VE.op("tensor_copy", b0[:, 0:512], u1[:, 0:512])
                            VE.op("tensor_copy", b1[:, 0:512], u1[:, 512:1024])
                            TE.wait(VE)
                            for hh in range(2):
                                TE.op("matmul", pb[4][0:1, 0:512], onesb[hh * 64:hh * 64 + 64, 0:1], b0[hh * 64:hh * 64 + 64, 0:512], start=True, stop=True)
                                TE.op("matmul", pb[5][0:1, 0:512], onesb[hh * 64:hh * 64 + 64, 0:1], b1[hh * 64:hh * 64 + 64, 0:512], start=True, stop=True)
                                VE.wait(TE)
                                VE.op("tensor_copy", qaug[0:64, :], u0[hh * 64:hh * 64 + 64, :])
                                VE.wait(VE)
                                VE.op("tensor_copy", qaug[64:65, 0:512], pb[4][0:1, 0:512])
                                VE.op("tensor_copy", qaug[64:65, 512:1024], pb[5][0:1, 0:512])
                                VE.wait(VE)
                                TE.wait(VE)
                                h = o8 * 2 + hh
                                for tt in range(8):
                                    TE.op("matmul", pb[4 if tt % 2 == 0 else 5][:, 0:128], qaug[0:65, tt * 128:(tt + 1) * 128], omgT[0:65, :], start=True, stop=True)
                                    VE.wait(TE)
                                    VE.op("tensor_reduce", cc[0][:], pb[4 if tt % 2 == 0 else 5][:, 0:128], axis=AX, op=OP.max)
                                    VE.wait(VE)
                                    VE.op("tensor_scalar", cc[1][:], cc[0][:], -1.0, None, OP.mult)
                                    SC.wait(VE)
                                    SC.op("activation", u1[:, tt * 128:(tt + 1) * 128], pb[4 if tt % 2 == 0 else 5][:, 0:128], AF.Exp, bias=cc[1][:], scale=1.0)
                                    TE.wait(SC)
                                VE.wait(SC)
                                VE.op("tensor_scalar", u1[:], u1[:], float(1.0 / np.sqrt(128.0)), 1e-6, OP.mult, OP.add)
                                VE.wait(VE)
                                VE.wait(GP)
                                VE.op("tensor_copy", wtmpb[:], u1[:])
                                GP.wait(VE)
                                psc = kphi_sc if kind == "k" else qphi_sc
                                for tt in range(8):
                                    GP.dma(psc[h, g * 1024 + tt * 128: g * 1024 + (tt + 1) * 128, :], wtmpb[:, tt * 128:(tt + 1) * 128])
                                VE.wait(GP)
                                if kind == "k":
                                    for tt in range(8):
                                        SY.wait(TE, VE)
                                        SY.dma(vtm[:, 0:64], v_sc[g * 1024 + tt * 128: g * 1024 + (tt + 1) * 128, h * 64:(h + 1) * 64])
                                        VE.wait(SY)
                                        VE.op("tensor_copy", vext[:, 0:64], vtm[:, 0:64])
                                        VE.wait(VE)
                                        TE.wait(VE)
                                        TE.op("matmul", pb[4][:, 0:66], wtmpb[:, tt * 128:(tt + 1) * 128], vext[:], start=True, stop=True)
                                        VE.wait(TE)
                                        VE.op("tensor_tensor", Gtot[:, h * 66:(h + 1) * 66], Gtot[:, h * 66:(h + 1) * 66], pb[4][:, 0:66], OP.add)
                                        VE.wait(VE)
                                        TE.wait(VE)
            full_barrier()
            cx.free_pool(bb1p)
            GP.wait(VE)
            tap(GP, "qph0", qphi_sc[0])
            tap(GP, "kph0", kphi_sc[0])
            tap(GP, "qph7", qphi_sc[7])
            tap(GP, "vsc", v_sc)
            tap(GP, "gtotT", Gtot[:])

            GP.wait(VE)
            GP.dma(ccG_in, Gtot[:])
            GP.wait(GP)
            tokG = GP.coll("AllGather", OP.bypass,
                           replica_groups=[[0, 1, 2, 3], [4, 5, 6, 7]],
                           ins=[ccG_in.opt()], outs=[ccG_out.opt()])
            GP.wait_tok(tokG)
            VE.op("memset", Sext[:], 0.0)
            bb2p = []
            for r in range(4):
                GP.wait(VE, TE)
                GP.dma(wtmp[:], ccG_out[r * 128:(r + 1) * 128, 0:1024])
                GP.dma(t2[:, 0:32], ccG_out[r * 128:(r + 1) * 128, 1024:1056])
                VE.wait(GP)
                VE.op("scalar_tensor_tensor", Sext[:, 0:1024], wtmp[:], rankc[:, r:r + 1], Sext[:, 0:1024], OP.mult, OP.add)
                VE.op("scalar_tensor_tensor", Sext[:, 1024:1056], t2[:, 0:32], rankc[:, r:r + 1], Sext[:, 1024:1056], OP.mult, OP.add)
                VE.wait(VE)

            GP.wait(VE)
            tap(GP, "sextT", Sext[:])
            qpb = cx.sb("qpb", [128, 2 * 128], BF16, pool=bb2p)
            kpb = cx.sb("kpb", [128, 2 * 128], BF16, pool=bb2p)
            vtb = cx.sb("vtb", [128, 1024], BF16, pool=bb2p)
            woB = cx.sb("woB", [128, 8 * 1024], BF16, pool=bb2p)
            hfm = cx.sb("hfm", [128, 8 * 512], BF16, pool=bb2p)
            mid = cx.sb("mid", [128, 4 * 256], BF16, pool=bb2p)
            vexb = b2[:, 384:450]
            qpM = b2[:, 0:128]
            kpM = b2[:, 128:256]
            amb = b2[:, 256:384]
            w2t = [cx.sb(f"w2t{j}", [128, 1024], BF16, pool=bb2p) for j in range(2)]
            VE.op("memset", vexb, 0.0)
            VE.wait(VE)
            VE.op("memset", vexb[:, 64:65], 1.0)
            for r in range(8):
                SY.wait(VE, GP)
                SY.dma(woB[:, r * 1024:(r + 1) * 1024], bWo_sc[r * 128:(r + 1) * 128, :])

            for j in range(NT):
                SY.wait(TE, VE)
                SY.dma(qpb[:, 0:128], qphi_sc[0, j * 128:(j + 1) * 128, :])
                SY.dma(kpb[:, 0:128], kphi_sc[0, j * 128:(j + 1) * 128, :])
                SY.dma(vtb[:], v_sc[j * 128:(j + 1) * 128, :])
                TE.wait(SY)
                VE.wait(SY)
                for h in range(H):
                    hp = h % 2
                    if h + 1 < H:
                        SY.wait(TE)
                        SY.dma(qpb[:, (1 - hp) * 128:(2 - hp) * 128], qphi_sc[h + 1, j * 128:(j + 1) * 128, :])
                        SY.dma(kpb[:, (1 - hp) * 128:(2 - hp) * 128], kphi_sc[h + 1, j * 128:(j + 1) * 128, :])
                    TE.wait(SY)
                    TE.op("transpose", pf[0][:, 0:128], qpb[:, hp * 128:(hp + 1) * 128], idb[:])
                    TE.op("transpose", pf[1][:, 0:128], kpb[:, hp * 128:(hp + 1) * 128], idb[:])
                    VE.wait(TE)
                    VE.op("tensor_copy", qpM, pf[0][:, 0:128])
                    VE.op("tensor_copy", kpM, pf[1][:, 0:128])
                    VE.op("tensor_copy", vexb[:, 0:64], vtb[:, h * 64:(h + 1) * 64])
                    VE.op("tensor_copy", Sexb[:], Sext[:, h * 66:(h + 1) * 66])
                    VE.wait(VE)
                    TE.wait(VE)
                    TE.op("matmul", pb[2][:, 0:128], kpM, qpM, start=True, stop=True)
                    VE.wait(TE)
                    VE.op("tensor_tensor", amb, pb[2][:, 0:128], mg0i[:], OP.mult)
                    VE.wait(VE)
                    TE.wait(VE)
                    TE.op("matmul", pb[3][:, 0:66], amb, vexb, start=True, stop=False)
                    TE.op("matmul", pb[3][:, 0:66], qpM, Sexb[:], start=False, stop=True)
                    TE.op("matmul", pb[4][:, 0:66], kpb[:, hp * 128:(hp + 1) * 128], vexb, start=True, stop=True)
                    VE.wait(TE)
                    VE.op("tensor_scalar", cc[0][:], pb[3][:, 64:65], 1e-6, None, OP.add)
                    VE.op("tensor_tensor", Sext[:, h * 66:(h + 1) * 66], Sext[:, h * 66:(h + 1) * 66], pb[4][:, 0:66], OP.add)
                    VE.wait(VE)
                    VE.op("reciprocal", cc[1][:], cc[0][:])
                    VE.wait(VE)
                    VE.op("tensor_scalar", t0[:, 0:64], pb[3][:, 0:64], cc[1][:], None, OP.mult)
                    VE.wait(VE)
                    VE.op("tensor_copy", b0[:, 0:64], t0[:, 0:64])
                    TE.wait(VE)
                    TE.op("transpose", pf[0][0:64, 0:128], b0[:, 0:64], idb[:])
                    VE.wait(TE)
                    hb = (h % 2) * 64
                    VE.op("tensor_copy", b1[hb:hb + 64, 0:128], pf[0][0:64, 0:128])
                    TE.wait(VE)
                    rhsWo = woB[:, (h // 2) * 1024:(h // 2 + 1) * 1024][hb:hb + 64, :]
                    TE.op("matmul", pb[0][:, 0:512], b1[hb:hb + 64, 0:128], rhsWo[:, 0:512], start=(h == 0), stop=(h == H - 1))
                    TE.op("matmul", pb[1][:, 0:512], b1[hb:hb + 64, 0:128], rhsWo[:, 512:1024], start=(h == 0), stop=(h == H - 1))
                VE.wait(TE)
                xt = resid[:, j * 1024:(j + 1) * 1024]
                VE.op("tensor_scalar", u0[:], xt, stats1[:, 2 * j:2 * j + 1], stats1[:, 2 * j + 1:2 * j + 2], OP.mult, OP.add)
                VE.wait(VE)
                VE.op("tensor_tensor", u0[:, 0:512], u0[:, 0:512], pb[0][:, 0:512], OP.add)
                VE.op("tensor_tensor", u0[:, 512:1024], u0[:, 512:1024], pb[1][:, 0:512], OP.add)
                VE.wait(VE)
                TE.wait(VE)
                SC.wait(VE)
                SC.op("activation", t0[:, 0:512], u0[:, 0:512], AF.Square, bias=0.0, scale=1.0, accum_out=cc[0][:])
                SC.op("activation", t1[:, 0:512], u0[:, 512:1024], AF.Square, bias=0.0, scale=1.0, accum_out=cc[1][:])
                VE.op("tensor_reduce", cc[2][:], u0[:], axis=AX, op=OP.add)
                VE.wait(SC)
                VE.wait(VE)
                VE.op("tensor_tensor", cc[3][:], cc[0][:], cc[1][:], OP.add)
                VE.op("tensor_scalar", cc[2][:], cc[2][:], 1.0 / 1024.0, None, OP.mult)
                SC.wait(VE)
                SC.op("activation", cc[4][:], cc[2][:], AF.Square, bias=0.0, scale=1.0)
                VE.wait(SC)
                VE.op("tensor_scalar", cc[3][:], cc[3][:], 1.0 / 1024.0, None, OP.mult)
                VE.wait(VE)
                VE.op("tensor_tensor", cc[3][:], cc[3][:], cc[4][:], OP.subtract)
                VE.wait(VE)
                VE.op("tensor_scalar", cc[5][:], cc[3][:], 1e6, None, OP.add)
                SC.wait(VE)
                SC.op("activation", cc[4][:], cc[5][:], AF.Sqrt, bias=0.0, scale=1.0)
                VE.wait(SC)
                VE.op("reciprocal", cc[5][:], cc[4][:])
                VE.wait(VE)
                VE.op("tensor_tensor", cc[3][:], cc[2][:], cc[5][:], OP.mult)
                VE.wait(VE)
                VE.op("tensor_scalar", cc[3][:], cc[3][:], -1.0, None, OP.mult)
                VE.wait(VE)
                VE.op("tensor_scalar", u1[:], u0[:], cc[5][:], cc[3][:], OP.mult, OP.add)
                VE.wait(VE)
                VE.op("tensor_tensor", u1[:], u1[:], gamr[:], OP.mult)
                VE.wait(VE)
                VE.op("tensor_tensor", u1[:], u1[:], betr[:], OP.add)
                VE.wait(VE)
                VE.op("tensor_copy", wtmpb[:], u1[:])
                TE.wait(VE)
                for d in range(ND):
                    TE.op("transpose", pf[d % 2][:, 0:128], wtmpb[:, d * 128:(d + 1) * 128], idb[:])
                    VE.wait(TE)
                    VE.op("tensor_copy", hfm[:, d * 512 + (j % 4) * 128: d * 512 + (j % 4) * 128 + 128], pf[d % 2][:, 0:128])
                    TE.wait(VE)
                if j % 4 == 3:
                    gp512 = j // 4
                    for th in range(2):
                        tb = th * 256
                        for q4 in range(8):
                            for fc8 in range(4):
                                fch = q4 * 4 + fc8
                                wpar = fch % 2
                                SY.wait(TE)
                                for d in range(ND):
                                    SY.dma(w2t[wpar][:, d * 128:(d + 1) * 128], bW1_sc[d * 128:(d + 1) * 128, fch * 128:(fch + 1) * 128])
                                TE.wait(SY)
                                for d in range(ND):
                                    TE.op("matmul", pb[2][:, 0:256], w2t[wpar][:, d * 128:(d + 1) * 128], hfm[:, d * 512 + tb: d * 512 + tb + 256], start=(d == 0), stop=(d == ND - 1))
                                SC.wait(TE)
                                SC.op("activation", t0[:, 0:256], pb[2][:, 0:256], AF.Relu, bias=b1c[:, fch:fch + 1], scale=1.0)
                                VE.wait(SC)
                                VE.op("tensor_copy", mid[:, fc8 * 256:(fc8 + 1) * 256], t0[:, 0:256])
                                TE.wait(VE)
                            for fc8 in range(4):
                                fch = q4 * 4 + fc8
                                wpar = fch % 2
                                SY.wait(TE)
                                SY.dma(w2t[wpar][:], bW2_sc[fch * 128:(fch + 1) * 128, :])
                                TE.wait(SY)
                                for ts in range(2):
                                    pA = pb[3] if ts == 0 else pb[4]
                                    pB = pb[5] if ts == 0 else pf[0][:].bitcast(F32)
                                    TE.op("matmul", pA[:, 0:512],
                                          mid[:, fc8 * 256 + ts * 128: fc8 * 256 + (ts + 1) * 128], w2t[wpar][:, 0:512],
                                          start=(fch == 0), stop=(fch == 31))
                                    TE.op("matmul", pB[:, 0:512],
                                          mid[:, fc8 * 256 + ts * 128: fc8 * 256 + (ts + 1) * 128], w2t[wpar][:, 512:1024],
                                          start=(fch == 0), stop=(fch == 31))
                        VE.wait(TE)
                        pmap = [(pb[3], pb[5]), (pb[4], pf[0][:].bitcast(F32))]
                        for ts in range(2):
                            ti_ = gp512 * 4 + th * 2 + ts
                            r0 = resid[:, ti_ * 1024: ti_ * 1024 + 512]
                            r1 = resid[:, ti_ * 1024 + 512: (ti_ + 1) * 1024]
                            pA, pB = pmap[ts]
                            VE.op("tensor_tensor", t0[:, 0:512], pA[:, 0:512], b2r[:, 0:512], OP.add)
                            VE.op("tensor_tensor", t1[:, 0:512], pB[:, 0:512], b2r[:, 512:1024], OP.add)
                            VE.wait(VE)
                            VE.op("tensor_tensor", r0, r0, t0[:, 0:512], OP.add)
                            VE.op("tensor_tensor", r1, r1, t1[:, 0:512], OP.add)
                            VE.wait(VE)
                        TE.wait(VE)
            full_barrier()
            cx.free_pool(bb2p)
            cx.free_pool(bbP)
        if "ffn1" in PHASES or "ffn2" in PHASES:
            VE.op("memset", mgc[:], MAGIC)
            VE.op("memset", nmgc[:], -MAGIC)

            def dequant(src, nrows, ncols, trans_dst):
                nb = 32
                for r in range(nrows // 128):
                    for cs in range(ncols // 1024):
                        SY.wait(TE, VE, SC)
                        SY.dma(wtmp[:], src[r * 128:(r + 1) * 128, cs * 1024:(cs + 1) * 1024])
                        VE.wait(SY)
                        w3 = wtmp[:].rearrange("p (g k) -> p g k", g=nb)
                        VE.op("tensor_reduce", t0[:, 0:nb], w3, axis=AX, op=OP.max)
                        VE.op("tensor_reduce", t0[:, nb:2 * nb], w3, axis=AX, op=OP.min)
                        VE.wait(VE)
                        VE.op("tensor_scalar", t0[:, nb:2 * nb], t0[:, nb:2 * nb], -1.0, None, OP.mult)
                        VE.wait(VE)
                        VE.op("tensor_tensor", t0[:, 0:nb], t0[:, 0:nb], t0[:, nb:2 * nb], OP.max)
                        VE.wait(VE)
                        VE.op("tensor_scalar", t0[:, 0:nb], t0[:, 0:nb], 1.0 / 7.0, 1e-12, OP.mult, OP.max)
                        VE.wait(VE)
                        VE.op("reciprocal", t0[:, nb:2 * nb], t0[:, 0:nb])
                        VE.wait(VE)
                        s3d = t0[:, nb:2 * nb].unsqueeze(-1).broadcast_to([128, nb, 32])
                        VE.op("tensor_tensor", u0[:].rearrange("p (g k) -> p g k", g=nb), w3, s3d, OP.mult)
                        SC.wait(VE)
                        SC.op("activation", u0[:], u0[:], AF.Identity, bias=mgc[:], scale=1.0)
                        SC.wait(SC)
                        SC.op("activation", u0[:], u0[:], AF.Identity, bias=nmgc[:], scale=1.0)
                        VE.wait(SC)
                        s3e = t0[:, 0:nb].unsqueeze(-1).broadcast_to([128, nb, 32])
                        VE.op("tensor_tensor", u1[:].rearrange("p (g k) -> p g k", g=nb), u0[:].rearrange("p (g k) -> p g k", g=nb), s3e, OP.mult)
                        VE.wait(VE)
                        VE.op("tensor_copy", wtmpb[:], u1[:])
                        TE.wait(VE)
                        for dch in range(8):
                            TE.op("transpose", pf[dch % 2][:, 0:128], wtmpb[:, dch * 128:(dch + 1) * 128], idb[:])
                            VE.wait(TE)
                            VE.op("tensor_copy", b0[:, 0:128], pf[dch % 2][:, 0:128])
                            GP.wait(VE)
                            GP.dma(trans_dst[cs * 1024 + dch * 128: cs * 1024 + (dch + 1) * 128, r * 128:(r + 1) * 128], b0[:, 0:128])
                            VE.wait(GP)
                            TE.wait(VE)

            dequant(w_up, FQ, D, upT_sc)
            dequant(w_gate, FQ, D, gateT_sc)
            dequant(w_down, D, FQ, wdT_sc)
            full_barrier()

            def k1rep(dst, src11):
                VE.op("tensor_copy", b0[0:1, 0:1], src11)
                TE.wait(VE)
                TE.op("matmul", pb[5][:, 0:1], onesb[0:1, :], b0[0:1, 0:1], start=True, stop=True)
                VE.wait(TE)
                VE.op("tensor_copy", dst, pb[5][:, 0:1])
                TE.wait(VE)

            def cross_part_max(col, dst11):
                TE.wait(VE)
                TE.op("transpose", pb[5][0:1, 0:128], col, idf[:])
                VE.wait(TE)
                VE.op("tensor_reduce", dst11, pb[5][0:1, 0:128], axis=AX, op=OP.max)
                VE.wait(VE)
                TE.wait(VE)

            def ffn_emit(tag):
                ff = []
                xqf = cx.sb("xqf" + tag, [128, 8 * 1024], BF16, pool=ff)
                ofm = cx.sb("ofm" + tag, [128, 24 * 512], BF16, pool=ff)
                wt = [cx.sb(f"wt{tag}{j}", [128, 1024], BF16, pool=ff) for j in range(4)]
                mxc = cx.sb("mxc" + tag, [128, 4], F32, pool=ff)
                scl = cx.sb("scl" + tag, [1, 16], F32, pool=ff)
                sclr = [cx.sb(f"sclr{tag}{j}", [128, 1], F32, pool=ff) for j in range(6)]
                co = 0 if tag == "f1" else 4
                VE.op("memset", mxc[:], 0.0)
                for i in range(NT):
                    xt = resid[:, i * 1024:(i + 1) * 1024]
                    SC.wait(VE)
                    SC.op("activation", t0[:, 0:512], xt[:, 0:512], AF.Square, bias=0.0, scale=1.0, accum_out=cc[0][:])
                    SC.op("activation", t1[:, 0:512], xt[:, 512:1024], AF.Square, bias=0.0, scale=1.0, accum_out=cc[1][:])
                    VE.op("tensor_reduce", cc[2][:], xt, axis=AX, op=OP.add)
                    VE.wait(SC)
                    VE.wait(VE)
                    VE.op("tensor_tensor", cc[3][:], cc[0][:], cc[1][:], OP.add)
                    VE.op("tensor_scalar", cc[2][:], cc[2][:], 1.0 / 1024.0, None, OP.mult)
                    SC.wait(VE)
                    SC.op("activation", cc[4][:], cc[2][:], AF.Square, bias=0.0, scale=1.0)
                    VE.wait(SC)
                    VE.op("tensor_scalar", cc[3][:], cc[3][:], 1.0 / 1024.0, None, OP.mult)
                    VE.wait(VE)
                    VE.op("tensor_tensor", cc[3][:], cc[3][:], cc[4][:], OP.subtract)
                    VE.wait(VE)
                    VE.op("tensor_scalar", cc[5][:], cc[3][:], 1e-5, None, OP.add)
                    SC.wait(VE)
                    SC.op("activation", cc[4][:], cc[5][:], AF.Sqrt, bias=0.0, scale=1.0)
                    VE.wait(SC)
                    VE.op("reciprocal", stats1[:, 2 * i:2 * i + 1], cc[4][:])
                    VE.wait(VE)
                    VE.op("tensor_tensor", cc[4][:], cc[2][:], stats1[:, 2 * i:2 * i + 1], OP.mult)
                    VE.wait(VE)
                    VE.op("tensor_scalar", stats1[:, 2 * i + 1:2 * i + 2], cc[4][:], -1.0, None, OP.mult)
                    VE.op("tensor_scalar", u0[:], xt, stats1[:, 2 * i:2 * i + 1], stats1[:, 2 * i + 1:2 * i + 2], OP.mult, OP.add)
                    VE.wait(VE)
                    VE.op("tensor_reduce", cc[0][:], u0[:], axis=AX, op=OP.max)
                    VE.op("tensor_reduce", cc[1][:], u0[:], axis=AX, op=OP.min)
                    VE.wait(VE)
                    VE.op("tensor_scalar", cc[1][:], cc[1][:], -1.0, None, OP.mult)
                    VE.wait(VE)
                    VE.op("tensor_tensor", cc[0][:], cc[0][:], cc[1][:], OP.max)
                    VE.wait(VE)
                    VE.op("tensor_tensor", mxc[:, 0:1], mxc[:, 0:1], cc[0][:], OP.max)
                    VE.wait(VE)
                cross_part_max(mxc[:, 0:1], scl[0:1, 0:1])
                GP.wait(VE)
                GP.dma(cc_mx_in[0:1, 0:1], scl[0:1, 0:1])
                GP.wait(GP)
                tokM = GP.coll("AllReduce", OP.max, replica_groups=[list(range(8))],
                               ins=[cc_mx_in.opt()], outs=[cc_mx_out[co].opt()])
                GP.wait_tok(tokM)
                GP.dma(scl[0:1, 0:1], cc_mx_out[co][0:1, 0:1])
                VE.wait(GP)
                VE.op("tensor_scalar", scl[0:1, 1:2], scl[0:1, 0:1], 1.0 / 32767.0, 1e-12, OP.mult, OP.max)
                VE.wait(VE)
                VE.op("reciprocal", scl[0:1, 2:3], scl[0:1, 1:2])
                VE.wait(VE)
                k1rep(sclr[0][:], scl[0:1, 2:3])

                VE.op("memset", mxc[:, 1:3], 0.0)
                for g in range(4):
                    for ii in range(8):
                        i = g * 8 + ii
                        xt = resid[:, i * 1024:(i + 1) * 1024]
                        VE.wait(TE)
                        VE.op("tensor_scalar", u0[:], xt, stats1[:, 2 * i:2 * i + 1], stats1[:, 2 * i + 1:2 * i + 2], OP.mult, OP.add)
                        SC.wait(VE)
                        SC.op("activation", u1[:], u0[:], AF.Identity, bias=mgc[:], scale=sclr[0][:])
                        SC.wait(SC)
                        SC.op("activation", u1[:], u1[:], AF.Identity, bias=nmgc[:], scale=1.0)
                        VE.wait(SC)
                        VE.op("tensor_copy", wtmpb[:], u1[:])
                        TE.wait(VE)
                        for d in range(ND):
                            TE.op("transpose", pf[d % 2][:, 0:128], wtmpb[:, d * 128:(d + 1) * 128], idb[:])
                            VE.wait(TE)
                            VE.op("tensor_copy", xqf[:, d * 1024 + ii * 128: d * 1024 + ii * 128 + 128], pf[d % 2][:, 0:128])
                            TE.wait(VE)
                    for ht in range(NF):
                        pu = [pb[0], pb[1]] if ht % 2 == 0 else [pb[4], pb[5]]
                        pg_ = [pb[2], pb[3]] if ht % 2 == 0 else [pf[0][:].bitcast(F32), pf[1][:].bitcast(F32)]
                        wpar = ht % 2
                        SY.wait(TE)
                        for d in range(ND):
                            SY.dma(wt[wpar][:, d * 128:(d + 1) * 128], upT_sc[d * 128:(d + 1) * 128, ht * 128:(ht + 1) * 128])
                        for d in range(ND):
                            SY.dma(wt[2 + wpar][:, d * 128:(d + 1) * 128], gateT_sc[d * 128:(d + 1) * 128, ht * 128:(ht + 1) * 128])
                        TE.wait(SY)
                        for d in range(ND):
                            TE.op("matmul", pu[0][:, 0:512], wt[wpar][:, d * 128:(d + 1) * 128], xqf[:, d * 1024:d * 1024 + 512], start=(d == 0), stop=(d == ND - 1))
                            TE.op("matmul", pu[1][:, 0:512], wt[wpar][:, d * 128:(d + 1) * 128], xqf[:, d * 1024 + 512:d * 1024 + 1024], start=(d == 0), stop=(d == ND - 1))
                        for d in range(ND):
                            TE.op("matmul", pg_[0][:, 0:512], wt[2 + wpar][:, d * 128:(d + 1) * 128], xqf[:, d * 1024:d * 1024 + 512], start=(d == 0), stop=(d == ND - 1))
                            TE.op("matmul", pg_[1][:, 0:512], wt[2 + wpar][:, d * 128:(d + 1) * 128], xqf[:, d * 1024 + 512:d * 1024 + 1024], start=(d == 0), stop=(d == ND - 1))
                        VE.wait(TE)
                        VE.wait(GP)
                        VE.op("tensor_copy", u0[:, 0:512], pu[0][:, 0:512])
                        VE.op("tensor_copy", u0[:, 512:1024], pu[1][:, 0:512])
                        VE.op("tensor_copy", u1[:, 0:512], pg_[0][:, 0:512])
                        VE.op("tensor_copy", u1[:, 512:1024], pg_[1][:, 0:512])
                        VE.wait(VE)
                        GP.wait(VE)
                        GP.dma(up_sc[ht * 128:(ht + 1) * 128, g * 1024:(g + 1) * 1024], u0[:])
                        GP.dma(gate_sc[ht * 128:(ht + 1) * 128, g * 1024:(g + 1) * 1024], u1[:])
                        for (src, cix) in ((u0, 1), (u1, 2)):
                            VE.op("tensor_reduce", cc[0][:], src[:], axis=AX, op=OP.max)
                            VE.op("tensor_reduce", cc[1][:], src[:], axis=AX, op=OP.min)
                            VE.wait(VE)
                            VE.op("tensor_scalar", cc[1][:], cc[1][:], -1.0, None, OP.mult)
                            VE.wait(VE)
                            VE.op("tensor_tensor", cc[0][:], cc[0][:], cc[1][:], OP.max)
                            VE.wait(VE)
                            VE.op("tensor_tensor", mxc[:, cix:cix + 1], mxc[:, cix:cix + 1], cc[0][:], OP.max)
                            VE.wait(VE)
                cross_part_max(mxc[:, 1:2], scl[0:1, 3:4])
                cross_part_max(mxc[:, 2:3], scl[0:1, 4:5])
                GP.wait(VE)
                GP.dma(cc_mx_in[0:1, 0:2], scl[0:1, 3:5])
                GP.wait(GP)
                tokU = GP.coll("AllReduce", OP.max, replica_groups=[list(range(8))],
                               ins=[cc_mx_in.opt()], outs=[cc_mx_out[co + 1].opt()])
                GP.wait_tok(tokU)
                GP.dma(scl[0:1, 3:5], cc_mx_out[co + 1][0:1, 0:2])
                VE.wait(GP)
                VE.op("tensor_scalar", scl[0:1, 5:7], scl[0:1, 3:5], scl[0:1, 1:2], None, OP.mult)
                VE.wait(VE)
                VE.op("tensor_scalar", scl[0:1, 5:7], scl[0:1, 5:7], 1.0 / 32767.0, 1e-12, OP.mult, OP.max)
                VE.wait(VE)
                VE.op("reciprocal", scl[0:1, 7:9], scl[0:1, 5:7])
                VE.wait(VE)
                VE.op("tensor_scalar", scl[0:1, 9:11], scl[0:1, 7:9], scl[0:1, 1:2], None, OP.mult)
                VE.wait(VE)
                k1rep(sclr[1][:], scl[0:1, 9:10])
                k1rep(sclr[2][:], scl[0:1, 10:11])
                k1rep(sclr[3][:], scl[0:1, 6:7])

                VE.op("memset", mxc[:, 3:4], 0.0)
                for ht in range(NF):
                    for g in range(4):
                        SY.wait(VE, SC, GP)
                        SY.dma(u1[:], gate_sc[ht * 128:(ht + 1) * 128, g * 1024:(g + 1) * 1024])
                        SC.wait(SY)
                        SC.op("activation", u0[:], u1[:], AF.Identity, bias=mgc[:], scale=sclr[2][:])
                        SC.wait(SC)
                        SC.op("activation", u0[:], u0[:], AF.Identity, bias=nmgc[:], scale=1.0)
                        SC.wait(SC)
                        SC.op("activation", u1[:], u0[:], AF.Sigmoid, bias=0.0, scale=sclr[3][:])
                        SC.wait(SC)
                        SC.op("activation", u1[:], u1[:], AF.Identity, bias=mgc[:], scale=65536.0)
                        SC.wait(SC)
                        SC.op("activation", u1[:], u1[:], AF.Identity, bias=nmgc[:], scale=1.0)
                        VE.wait(SC)
                        VE.op("tensor_scalar", u1[:], u1[:], 65535.0, None, OP.min)
                        VE.wait(VE)
                        VE.op("tensor_tensor", u0[:], u0[:], u1[:], OP.mult)
                        VE.wait(VE)
                        VE.op("tensor_reduce", cc[0][:], u0[:], axis=AX, op=OP.max)
                        VE.op("tensor_reduce", cc[1][:], u0[:], axis=AX, op=OP.min)
                        VE.wait(VE)
                        VE.op("tensor_scalar", cc[1][:], cc[1][:], -1.0, None, OP.mult)
                        VE.wait(VE)
                        VE.op("tensor_tensor", cc[0][:], cc[0][:], cc[1][:], OP.max)
                        VE.wait(VE)
                        VE.op("tensor_tensor", mxc[:, 3:4], mxc[:, 3:4], cc[0][:], OP.max)
                        VE.wait(VE)
                cross_part_max(mxc[:, 3:4], scl[0:1, 11:12])
                GP.wait(VE)
                GP.dma(cc_mx_in[0:1, 0:1], scl[0:1, 11:12])
                GP.wait(GP)
                tok3 = GP.coll("AllReduce", OP.max, replica_groups=[list(range(8))],
                               ins=[cc_mx_in.opt()], outs=[cc_mx_out[co + 2].opt()])
                GP.wait_tok(tok3)
                GP.dma(scl[0:1, 11:12], cc_mx_out[co + 2][0:1, 0:1])
                VE.wait(GP)
                VE.op("tensor_tensor", scl[0:1, 12:13], scl[0:1, 11:12], scl[0:1, 6:7], OP.mult)
                VE.wait(VE)
                VE.op("tensor_scalar", scl[0:1, 12:13], scl[0:1, 12:13], 1.0 / 65536.0 / 32767.0, 1e-12, OP.mult, OP.max)
                VE.wait(VE)
                VE.op("reciprocal", scl[0:1, 13:14], scl[0:1, 12:13])
                VE.wait(VE)
                VE.op("tensor_tensor", scl[0:1, 13:14], scl[0:1, 13:14], scl[0:1, 6:7], OP.mult)
                VE.wait(VE)
                VE.op("tensor_scalar", scl[0:1, 13:14], scl[0:1, 13:14], 1.0 / 65536.0, None, OP.mult)
                VE.wait(VE)
                k1rep(sclr[4][:], scl[0:1, 13:14])

                VE.op("memset", mxc[:, 0:1], 0.0)
                for ht in range(NF):
                    for g in range(4):
                        SY.wait(VE, SC, GP)
                        SY.dma(u1[:], gate_sc[ht * 128:(ht + 1) * 128, g * 1024:(g + 1) * 1024])
                        SY.dma(t0[:], up_sc[ht * 128:(ht + 1) * 128, g * 1024:g * 1024 + 512])
                        SY.dma(t1[:], up_sc[ht * 128:(ht + 1) * 128, g * 1024 + 512:(g + 1) * 1024])
                        SC.wait(SY)
                        SC.op("activation", u0[:], u1[:], AF.Identity, bias=mgc[:], scale=sclr[2][:])
                        SC.wait(SC)
                        SC.op("activation", u0[:], u0[:], AF.Identity, bias=nmgc[:], scale=1.0)
                        SC.wait(SC)
                        SC.op("activation", u1[:], u0[:], AF.Sigmoid, bias=0.0, scale=sclr[3][:])
                        SC.wait(SC)
                        SC.op("activation", u1[:], u1[:], AF.Identity, bias=mgc[:], scale=65536.0)
                        SC.wait(SC)
                        SC.op("activation", u1[:], u1[:], AF.Identity, bias=nmgc[:], scale=1.0)
                        VE.wait(SC)
                        VE.op("tensor_scalar", u1[:], u1[:], 65535.0, None, OP.min)
                        VE.wait(VE)
                        VE.op("tensor_tensor", u0[:], u0[:], u1[:], OP.mult)
                        SC.wait(VE)
                        SC.op("activation", u0[:], u0[:], AF.Identity, bias=mgc[:], scale=sclr[4][:])
                        SC.wait(SC)
                        SC.op("activation", u0[:], u0[:], AF.Identity, bias=nmgc[:], scale=1.0)
                        SC.op("activation", u1[:, 0:512], t0[:], AF.Identity, bias=mgc[:], scale=sclr[1][:])
                        SC.op("activation", u1[:, 512:1024], t1[:], AF.Identity, bias=mgc[:], scale=sclr[1][:])
                        SC.wait(SC)
                        SC.op("activation", u1[:], u1[:], AF.Identity, bias=nmgc[:], scale=1.0)
                        VE.wait(SC)
                        VE.op("tensor_tensor", u0[:], u0[:], u1[:], OP.mult)
                        VE.wait(VE)
                        VE.op("tensor_reduce", cc[0][:], u0[:], axis=AX, op=OP.max)
                        VE.op("tensor_reduce", cc[1][:], u0[:], axis=AX, op=OP.min)
                        VE.wait(VE)
                        VE.op("tensor_scalar", cc[1][:], cc[1][:], -1.0, None, OP.mult)
                        VE.wait(VE)
                        VE.op("tensor_tensor", cc[0][:], cc[0][:], cc[1][:], OP.max)
                        VE.wait(VE)
                        VE.op("tensor_tensor", mxc[:, 0:1], mxc[:, 0:1], cc[0][:], OP.max)
                        VE.wait(VE)
                cross_part_max(mxc[:, 0:1], scl[0:1, 14:15])
                GP.wait(VE)
                GP.dma(cc_mx_in[0:1, 0:1], scl[0:1, 14:15])
                GP.wait(GP)
                tok4 = GP.coll("AllReduce", OP.max, replica_groups=[list(range(8))],
                               ins=[cc_mx_in.opt()], outs=[cc_mx_out[co + 3].opt()])
                GP.wait_tok(tok4)
                GP.dma(scl[0:1, 14:15], cc_mx_out[co + 3][0:1, 0:1])
                VE.wait(GP)
                VE.op("tensor_tensor", scl[0:1, 15:16], scl[0:1, 12:13], scl[0:1, 5:6], OP.mult)
                VE.wait(VE)
                VE.op("tensor_tensor", scl[0:1, 14:15], scl[0:1, 14:15], scl[0:1, 15:16], OP.mult)
                VE.wait(VE)
                VE.op("tensor_scalar", scl[0:1, 14:15], scl[0:1, 14:15], 1.0 / 32767.0, 1e-12, OP.mult, OP.max)
                VE.wait(VE)
                VE.op("reciprocal", scl[0:1, 13:14], scl[0:1, 14:15])
                VE.wait(VE)
                VE.op("tensor_tensor", scl[0:1, 13:14], scl[0:1, 13:14], scl[0:1, 15:16], OP.mult)
                VE.wait(VE)
                k1rep(sclr[5][:], scl[0:1, 13:14])
                k1rep(sclr[0][:], scl[0:1, 14:15])

                if tag == "f1":
                    GP.wait(VE)
                    tap(GP, "sclT", scl[0:1, :])
                    tap(GP, "upsp", up_sc[0:1024, 0:64])
                    tap(GP, "gtsp", gate_sc[0:1024, 0:64])
                    tap(GP, "uqT", upT_sc[0:256, 0:256])
                    tap(GP, "wdT", wdT_sc[0:256, 0:256])
                    tap(GP, "statsF", stats1[:])
                for gp_ in range(8):
                    for ht in range(NF):
                        SY.wait(VE, SC, GP, TE)
                        SY.dma(u1[:, 0:512], gate_sc[ht * 128:(ht + 1) * 128, gp_ * 512:(gp_ + 1) * 512])
                        SY.dma(t0[:], up_sc[ht * 128:(ht + 1) * 128, gp_ * 512:(gp_ + 1) * 512])
                        SC.wait(SY)
                        SC.op("activation", u0[:, 0:512], u1[:, 0:512], AF.Identity, bias=mgc[:], scale=sclr[2][:])
                        SC.wait(SC)
                        SC.op("activation", u0[:, 0:512], u0[:, 0:512], AF.Identity, bias=nmgc[:], scale=1.0)
                        SC.wait(SC)
                        SC.op("activation", u1[:, 0:512], u0[:, 0:512], AF.Sigmoid, bias=0.0, scale=sclr[3][:])
                        SC.wait(SC)
                        SC.op("activation", u1[:, 0:512], u1[:, 0:512], AF.Identity, bias=mgc[:], scale=65536.0)
                        SC.wait(SC)
                        SC.op("activation", u1[:, 0:512], u1[:, 0:512], AF.Identity, bias=nmgc[:], scale=1.0)
                        VE.wait(SC)
                        VE.op("tensor_scalar", u1[:, 0:512], u1[:, 0:512], 65535.0, None, OP.min)
                        VE.wait(VE)
                        VE.op("tensor_tensor", u0[:, 0:512], u0[:, 0:512], u1[:, 0:512], OP.mult)
                        SC.wait(VE)
                        SC.op("activation", u0[:, 0:512], u0[:, 0:512], AF.Identity, bias=mgc[:], scale=sclr[4][:])
                        SC.wait(SC)
                        SC.op("activation", u0[:, 0:512], u0[:, 0:512], AF.Identity, bias=nmgc[:], scale=1.0)
                        SC.op("activation", u1[:, 0:512], t0[:], AF.Identity, bias=mgc[:], scale=sclr[1][:])
                        SC.wait(SC)
                        SC.op("activation", u1[:, 0:512], u1[:, 0:512], AF.Identity, bias=nmgc[:], scale=1.0)
                        VE.wait(SC)
                        VE.op("tensor_tensor", u0[:, 0:512], u0[:, 0:512], u1[:, 0:512], OP.mult)
                        SC.wait(VE)
                        SC.op("activation", u0[:, 0:512], u0[:, 0:512], AF.Identity, bias=mgc[:], scale=sclr[5][:])
                        SC.wait(SC)
                        SC.op("activation", u0[:, 0:512], u0[:, 0:512], AF.Identity, bias=nmgc[:], scale=1.0)
                        VE.wait(SC)
                        VE.op("tensor_copy", ofm[:, ht * 512:(ht + 1) * 512], u0[:, 0:512])
                        VE.wait(VE)
                    TE.wait(VE)
                    for h in range(NF):
                        wpar = h % 2
                        SY.wait(TE)
                        SY.dma(wt[wpar][:], wdT_sc[h * 128:(h + 1) * 128, :])
                        TE.wait(SY)
                        for ts in range(4):
                            pA = pb[ts]
                            pB = pb[4 + ts] if ts < 2 else pf[ts - 2][:].bitcast(F32)
                            TE.op("matmul", pA[:, 0:512],
                                  ofm[:, h * 512 + ts * 128: h * 512 + (ts + 1) * 128], wt[wpar][:, 0:512],
                                  start=(h == 0), stop=(h == NF - 1))
                            TE.op("matmul", pB[:, 0:512],
                                  ofm[:, h * 512 + ts * 128: h * 512 + (ts + 1) * 128], wt[wpar][:, 512:1024],
                                  start=(h == 0), stop=(h == NF - 1))
                    VE.wait(TE)
                    base = gp_ * 4
                    for ts in range(4):
                        i = base + ts
                        r0 = resid[:, i * 1024: i * 1024 + 512]
                        r1 = resid[:, i * 1024 + 512: (i + 1) * 1024]
                        pA = pb[ts]
                        pB = pb[4 + ts] if ts < 2 else pf[ts - 2][:].bitcast(F32)
                        VE.op("scalar_tensor_tensor", r0, pA[:, 0:512], sclr[0][:], r0, OP.mult, OP.add)
                        VE.op("scalar_tensor_tensor", r1, pB[:, 0:512], sclr[0][:], r1, OP.mult, OP.add)
                        VE.wait(VE)
                    TE.wait(VE)
                full_barrier()
                cx.free_pool(ff)

            if "ffn1" in PHASES:
                ffn_emit("f1")
            if "attn_b" in PHASES:
                attn_b_emit()
            if "ffn2" in PHASES:
                ffn_emit("f2")
        elif "attn_b" in PHASES:
            attn_b_emit()

        SY.wait(VE, SC, TE, GP)
        for i in range(NT):
            ins = nc.sync.dma_start(out=outt[i], in_=resid[:, i * 1024:(i + 1) * 1024])
            ins.then_inc(souts.h, 16)
            souts.n += 16
        nc.sync.wait_ge(souts.h, souts.n)

    cx.close()
    return nc


_NC_CACHE = {}


def kernel(L0_attn_a_x, L0_attn_a_Wk, L0_attn_a_Wv, L0_attn_a_Wq, L0_attn_a_fast_alpha,
           L0_attn_a_slow_alpha, L0_attn_a_nm_w1, L0_attn_a_nm_b1, L0_attn_a_nm_w2,
           L0_attn_a_nm_b2, L0_attn_a_ir_w1, L0_attn_a_ir_b1, L0_attn_a_ir_w2,
           L0_attn_a_ir_b2, L0_attn_a_pw, L0_attn_a_Wo, L0_attn_a_norm_w,
           L0_attn_b_Wq, L0_attn_b_Wk, L0_attn_b_Wv, L0_attn_b_Wo, L0_attn_b_omega,
           L0_attn_b_W1, L0_attn_b_b1, L0_attn_b_W2, L0_attn_b_b2, L0_attn_b_gamma,
           L0_attn_b_beta, L0_ffn_w_gate, L0_ffn_w_up, L0_ffn_w_down):
    key = (PHASES, tuple(sorted(DEBUG_TAP.keys())))
    if key not in _NC_CACHE:
        _NC_CACHE[key] = build()
    nc = _NC_CACHE[key]

    x = np.ascontiguousarray(np.asarray(L0_attn_a_x, np.float32).reshape(B * S, D))
    r2 = lambda a, shp: np.ascontiguousarray(np.asarray(a, np.float32).reshape(shp))
    in_maps = []
    for c in range(NCORES):
        im = {
            "xs": x[c * TPC:(c + 1) * TPC],
            "aWk": r2(L0_attn_a_Wk, (64, D)), "aWv": r2(L0_attn_a_Wv, (64, D)),
            "aWq": r2(L0_attn_a_Wq, (64, D)),
            "fa": r2(L0_attn_a_fast_alpha, (1, 1)), "sa": r2(L0_attn_a_slow_alpha, (1, 1)),
            "nm_w1": r2(L0_attn_a_nm_w1, (128, 64)), "nm_b1": r2(L0_attn_a_nm_b1, (1, 128)),
            "nm_w2": r2(L0_attn_a_nm_w2, (64, 128)), "nm_b2": r2(L0_attn_a_nm_b2, (1, 64)),
            "ir_w1": r2(L0_attn_a_ir_w1, (64, D)), "ir_b1": r2(L0_attn_a_ir_b1, (1, 64)),
            "ir_w2": r2(L0_attn_a_ir_w2, (1, 64)), "ir_b2": r2(L0_attn_a_ir_b2, (1, 1)),
            "pw": r2(L0_attn_a_pw, (3, D)), "aWo": r2(L0_attn_a_Wo, (D, 64)),
            "norm_w": r2(L0_attn_a_norm_w, (1, 64)),
            "rankm": np.asarray([[1.0 if jj < (c % 4) else 0.0 for jj in range(8)]], np.float32),
        }
        if "attn_b" in PHASES:
            im.update({
                "bWq": r2(L0_attn_b_Wq, (D, D)), "bWk": r2(L0_attn_b_Wk, (D, D)),
                "bWv": r2(L0_attn_b_Wv, (D, D)), "bWo": r2(L0_attn_b_Wo, (D, D)),
                "omega": r2(L0_attn_b_omega, (MPH, 64)),
                "bW1": r2(L0_attn_b_W1, (D, MQ)), "bb1": r2(L0_attn_b_b1, (1, MQ)),
                "bW2": r2(L0_attn_b_W2, (MQ, D)), "bb2": r2(L0_attn_b_b2, (1, D)),
                "gamma": r2(L0_attn_b_gamma, (1, D)), "beta": r2(L0_attn_b_beta, (1, D)),
            })
        if "ffn1" in PHASES or "ffn2" in PHASES:
            im.update({
                "w_gate": r2(L0_ffn_w_gate, (FQ, D)), "w_up": r2(L0_ffn_w_up, (FQ, D)),
                "w_down": r2(L0_ffn_w_down, (D, FQ)),
            })
        in_maps.append(im)

    res = run_bass_kernel_spmd(nc, in_maps, core_ids=list(range(NCORES)))
    outp = np.concatenate([np.asarray(res.results[c]["out"]) for c in range(NCORES)], axis=0)
    if DEBUG_TAP:
        return outp.reshape(B, S, D), [
            {nm: np.asarray(res.results[c]["tp_" + nm]) for nm in DEBUG_TAP} for c in range(NCORES)]
    return outp.reshape(B, S, D)


# version 03
# speedup vs baseline: 1.1398x; 1.1398x over previous
import numpy as np
import concourse.bass as bass
import concourse.mybir as mybir
from concourse.mybir import AxisListType
from concourse.alu_op_type import AluOpType as OP
from concourse.bass_utils import run_bass_kernel_spmd

F32 = mybir.dt.float32
BF16 = mybir.dt.bfloat16
I32 = mybir.dt.int32
AF = mybir.ActivationFunctionType
AX = AxisListType.X

NCORES = 8
B, S, D = 2, 16384, 1024
TPC = S * B // NCORES
NT = TPC // 128
ND = D // 128
MAGIC = 12582912.0
LN7_3 = float(np.log(7.0) - np.log(3.0))
LN09 = float(np.log(0.9))
FQ = 3072
NF = FQ // 128
MQ = 4096
H = 16
MPH = 128

PHASES = ("attn_a", "ffn1", "attn_b", "ffn2")
DEBUG_TAP = {}


class _Sem:
    __slots__ = ("h", "n")

    def __init__(self, h):
        self.h = h
        self.n = 0


class Eng:

    def __init__(self, cx, name, raw, dma_step=1):
        self._cx = cx
        self.name = name
        self.raw = raw
        self.sem = cx.mksem("S" + name)
        self._pend = None
        self._step = dma_step
        self._seen = {}

    def op(self, meth, *a, **kw):
        ins = getattr(self.raw, meth)(*a, **kw)
        self._pend = ins
        return ins

    def dma(self, out, in_):
        ins = self.raw.dma_start(out=out, in_=in_)
        ins.then_inc(self.sem.h, 16)
        self.sem.n += 16
        self._pend = None
        return ins

    def seal(self):
        if self._pend is not None:
            self._pend.then_inc(self.sem.h, 1)
            self.sem.n += 1
            self._pend = None
        return (self.sem, self.sem.n)

    def wait(self, *others):
        for o in others:
            s, n = o.seal()
            if n > 0 and self._seen.get(id(s), -1) < n:
                self.raw.wait_ge(s.h, n)
                self._seen[id(s)] = n

    def wait_tok(self, tok):
        s, n = tok
        self.raw.wait_ge(s.h, n)

    def coll(self, *a, **kw):
        s = self._cx.mksem(f"Scc{self._cx.ncc}")
        self._cx.ncc += 1
        ins = self.raw.collective_compute(*a, **kw)
        ins.then_inc(s.h, 1)
        return (s, 1)


class Ctx:
    def __init__(self):
        self.nc = bass.Bass("TRN2", target_bir_lowering=False, debug=False)
        self._mgrs = []
        self.ncc = 0

    def mksem(self, name):
        m = self.nc.semaphore(name)
        h = m.__enter__()
        self._mgrs.append(m)
        return _Sem(h)

    def sb(self, name, shape, dt, pool=None):
        m = self.nc.sbuf_tensor(name, shape, dt)
        t = m.__enter__()
        if pool is not None:
            pool.append(m)
        else:
            self._mgrs.append(m)
        return t

    def free_pool(self, pool):
        for m in reversed(pool):
            m.__exit__(None, None, None)
        pool.clear()

    def ps(self, name, shape, dt):
        m = self.nc.psum_tensor(name, shape, dt)
        t = m.__enter__()
        self._mgrs.append(m)
        return t

    def close(self):
        for m in reversed(self._mgrs):
            m.__exit__(None, None, None)


def build():
    cx = Ctx()
    nc = cx.nc

    def din(name, shape):
        return nc.dram_tensor(name, shape, F32, kind="ExternalInput").ap()

    xs = din("xs", [TPC, D])
    aWk = din("aWk", [64, D]); aWv = din("aWv", [64, D]); aWq = din("aWq", [64, D])
    fa_in = din("fa", [1, 1]); sa_in = din("sa", [1, 1])
    nm_w1 = din("nm_w1", [128, 64]); nm_b1 = din("nm_b1", [1, 128])
    nm_w2 = din("nm_w2", [64, 128]); nm_b2 = din("nm_b2", [1, 64])
    ir_w1 = din("ir_w1", [64, D]); ir_b1 = din("ir_b1", [1, 64])
    ir_w2 = din("ir_w2", [1, 64]); ir_b2 = din("ir_b2", [1, 1])
    pw = din("pw", [3, D]); aWo = din("aWo", [D, 64]); norm_w = din("norm_w", [1, 64])
    rankm = din("rankm", [1, 8])
    if "attn_b" in PHASES:
        bWq = din("bWq", [D, D]); bWk = din("bWk", [D, D]); bWv = din("bWv", [D, D])
        bWo = din("bWo", [D, D]); omega = din("omega", [MPH, 64])
        bW1 = din("bW1", [D, MQ]); bb1 = din("bb1", [1, MQ])
        bW2 = din("bW2", [MQ, D]); bb2 = din("bb2", [1, D])
        gam = din("gamma", [1, D]); bet = din("beta", [1, D])
    if "ffn1" in PHASES or "ffn2" in PHASES:
        w_gate = din("w_gate", [FQ, D]); w_up = din("w_up", [FQ, D]); w_down = din("w_down", [D, FQ])

    out = nc.dram_tensor("out", [TPC, D], F32, kind="ExternalOutput").ap()
    outt = out.rearrange("(n p) d -> n p d", p=128)

    taps = {nm: nc.dram_tensor("tp_" + nm, shape, F32 if dt == "f32" else BF16, kind="ExternalOutput").ap()
            for nm, (shape, dt) in DEBUG_TAP.items()}

    def tap(GPeng, name, src):
        if name in taps:
            GPeng.dma(taps[name], src)

    spillA = nc.dram_tensor("spillA", [NT * 2, 128, 192], BF16).ap()
    spillQ = nc.dram_tensor("spillQ", [NT, 128, 128], BF16).ap()
    ccA_in = nc.dram_tensor("ccA_in", [64, 256], F32).ap()
    ccA_out = nc.dram_tensor("ccA_out", [256, 256], F32).ap()
    ccN_in = nc.dram_tensor("ccN_in", [1, 66], F32).ap()
    ccN_out = nc.dram_tensor("ccN_out", [4, 66], F32).ap()
    if "attn_b" in PHASES:
        qphi_sc = nc.dram_tensor("qphi_sc", [H, TPC, MPH], BF16).ap()
        kphi_sc = nc.dram_tensor("kphi_sc", [H, TPC, MPH], BF16).ap()
        v_sc = nc.dram_tensor("v_sc", [TPC, D], BF16).ap()
        bWq_sc = nc.dram_tensor("bWq_sc", [D, D], BF16).ap()
        bWk_sc = nc.dram_tensor("bWk_sc", [D, D], BF16).ap()
        bWv_sc = nc.dram_tensor("bWv_sc", [D, D], BF16).ap()
        bWo_sc = nc.dram_tensor("bWo_sc", [D, D], BF16).ap()
        bW1_sc = nc.dram_tensor("bW1_sc", [D, MQ], BF16).ap()
        bW2_sc = nc.dram_tensor("bW2_sc", [MQ, D], BF16).ap()
        ccG_in = nc.dram_tensor("ccG_in", [128, H * 66], F32).ap()
        ccG_out = nc.dram_tensor("ccG_out", [512, H * 66], F32).ap()
    if "ffn1" in PHASES or "ffn2" in PHASES:
        upT_sc = nc.dram_tensor("upT_sc", [D, FQ], BF16).ap()
        gateT_sc = nc.dram_tensor("gateT_sc", [D, FQ], BF16).ap()
        wdT_sc = nc.dram_tensor("wdT_sc", [FQ, D], BF16).ap()
        up_sc = nc.dram_tensor("up_sc", [FQ, TPC], F32).ap()
        gate_sc = nc.dram_tensor("gate_sc", [FQ, TPC], F32).ap()
        cc_mx_in = nc.dram_tensor("cc_mx_in", [1, 2], F32).ap()
        cc_mx_out = [nc.dram_tensor(f"cc_mx_out{i}", [1, 2], F32).ap() for i in range(8)]

    TE = Eng(cx, "te", nc.tensor)
    VE = Eng(cx, "ve", nc.vector)
    SC = Eng(cx, "sc", nc.scalar)
    GP = Eng(cx, "gp", nc.gpsimd)
    SY = Eng(cx, "sy", nc.sync)
    souts = cx.mksem("souts")

    def full_barrier():
        toks = [(E, E.seal()) for E in (TE, VE, SC, GP, SY)]
        for E, _ in toks:
            for E2, (sh, n) in toks:
                if E2 is not E and n > 0:
                    E.raw.wait_ge(sh.h, n)

    resid = cx.sb("resid", [128, NT * 1024], F32)
    stats1 = cx.sb("stats1", [128, NT * 2], F32)
    idf = cx.sb("idf", [128, 128], F32)
    idb = cx.sb("idb", [128, 128], BF16)
    onesf = cx.sb("onesf", [128, 128], F32)
    onesb = cx.sb("onesb", [128, 128], BF16)
    mg0i = cx.sb("mg0i", [128, 128], F32)
    inclb = cx.sb("inclb", [128, 128], BF16)
    iotac = cx.sb("iotac", [128, 1], F32)
    i127c = cx.sb("i127c", [128, 1], F32)

    t0 = cx.sb("t0", [128, 512], F32)
    t1 = cx.sb("t1", [128, 512], F32)
    t2 = cx.sb("t2", [128, 512], F32)
    b0 = cx.sb("b0", [128, 512], BF16)
    b1 = cx.sb("b1", [128, 512], BF16)
    b2 = cx.sb("b2", [128, 512], BF16)
    b3 = cx.sb("b3", [128, 512], BF16)
    cc = [cx.sb(f"cc{i}", [128, 1], F32) for i in range(8)]
    wtmp = cx.sb("wtmp", [128, 1024], F32)
    wtmpb = cx.sb("wtmpb", [128, 1024], BF16)
    w64 = wtmp[0:64, :]
    w64b = wtmpb[0:64, :]
    brow = wtmp[0:1, :]
    browb = wtmpb[0:1, :]
    rankc = cx.sb("rankc", [128, 8], F32)
    u0 = cx.sb("u0", [128, 1024], F32)
    u1 = cx.sb("u1", [128, 1024], F32)
    mgc = cx.sb("mgc", [128, 1], F32)
    nmgc = cx.sb("nmgc", [128, 1], F32)

    aa = []
    xfmb = cx.sb("xfmb", [128, ND * 128], BF16, pool=aa)
    maskX = {m: cx.sb(f"maskX{m}", [128, 128], F32, pool=aa) for m in "fs"}
    maskS = {m: cx.sb(f"maskS{m}", [128, 128], BF16, pool=aa) for m in "fs"}
    maskI = {m: cx.sb(f"maskI{m}", [128, 128], BF16, pool=aa) for m in "fs"}
    d1col = {m: cx.sb(f"d1col{m}", [128, 1], F32, pool=aa) for m in "fs"}
    d2bcol = {m: cx.sb(f"d2bcol{m}", [128, 1], F32, pool=aa) for m in "fs"}
    lamC = {m: cx.sb(f"lamC{m}", [128, 1], F32, pool=aa) for m in "fs"}
    betac = {m: cx.sb(f"betac{m}", [128, 1], F32, pool=aa) for m in "fs"}
    lamc = {m: cx.sb(f"lamc{m}", [128, 1], F32, pool=aa) for m in "fs"}

    wall = cx.sb("wall", [128, ND * 392], BF16, pool=aa)
    wbarr = cx.sb("wbarr", [128, 392], F32, pool=aa)
    nmb1r = cx.sb("nmb1r", [128, 128], F32, pool=aa)
    nmb2r = cx.sb("nmb2r", [128, 64], F32, pool=aa)
    irb1r = cx.sb("irb1r", [128, 64], F32, pool=aa)
    ir2r = cx.sb("ir2r", [128, 64], F32, pool=aa)
    irb2c = cx.sb("irb2c", [128, 1], F32, pool=aa)
    nwr = cx.sb("nwr", [128, 64], F32, pool=aa)
    nm2T = cx.sb("nm2T", [128, 64], BF16, pool=aa)
    woT = cx.sb("woT", [64, 1024], BF16, pool=aa)

    kbf = cx.sb("kbf", [128, NT * 64], BF16, pool=aa)
    qbf = cx.sb("qbf", [128, NT * 64], BF16, pool=aa)
    vsb = cx.sb("vsb", [128, NT * 64], F32, pool=aa)
    nusb = cx.sb("nusb", [128, NT * 64], BF16, pool=aa)
    pwtsb = cx.sb("pwtsb", [128, NT * 4], F32, pool=aa)
    ucol = cx.sb("ucol", [128, NT], F32, pool=aa)
    oprt = cx.sb("oprt", [128, NT * 64], BF16, pool=aa)
    nsloc = cx.sb("nsloc", [128, NT * 64], BF16, pool=aa)
    pcol = cx.sb("pcol", [128, NT], F32, pool=aa)
    pcr = cx.sb("pcr", [1, NT], F32, pool=aa)
    rcr = cx.sb("rcr", [1, NT * 64], BF16, pool=aa)
    mn = {m: cx.sb(f"mn{m}", [64, 128], F32, pool=aa) for m in "fs"}
    mnb = {m: cx.sb(f"mnb{m}", [64, 128], BF16, pool=aa) for m in "fs"}
    s0f = {m: cx.sb(f"s0f{m}", [64, 64], F32, pool=aa) for m in "fs"}
    s0b = {m: cx.sb(f"s0b{m}", [64, 64], BF16, pool=aa) for m in "fs"}

    proj = u0[:, 0:392]
    solR = u1[:, 0:128]
    gsf = u1[:, 128:256]
    solRb = cx.sb("solRb", [128, 128], BF16, pool=aa)
    xrb = [wtmpb[:, j * 128:(j + 1) * 128] for j in range(2)]
    _wtb = wtmp[:].bitcast(BF16)
    xcp = [wtmpb[:, 256 + j * 128:384 + j * 128] for j in range(6)] + [_wtb[:, 0:128]]

    pb = [cx.ps(f"pb{i}", [128, 512], F32) for i in range(6)]
    pf = [cx.ps(f"pf{i}", [128, 1024], BF16) for i in range(2)]
    with nc.Block() as block:
        xst = xs.rearrange("(n p) d -> n p d", p=128)
        for i in range(NT):
            SY.dma(resid[:, i * 1024:(i + 1) * 1024], xst[i])
        ld_resid = (SY.sem, SY.sem.n)

        e_str = t0[:, 0:128]
        mg0s = t0[:, 128:256]
        GP.op("iota", t1[:, 0:128].bitcast(I32), pattern=[[1, 128]], base=-1, channel_multiplier=-1)
        GP.op("iota", t1[:, 128:129].bitcast(I32), pattern=[[0, 1]], base=0, channel_multiplier=1)
        GP.op("iota", t1[:, 129:130].bitcast(I32), pattern=[[0, 1]], base=127, channel_multiplier=-1)
        VE.wait(GP)
        VE.op("tensor_copy", e_str, t1[:, 0:128].bitcast(I32))
        VE.op("tensor_copy", iotac[:], t1[:, 128:129].bitcast(I32))
        VE.op("tensor_copy", i127c[:], t1[:, 129:130].bitcast(I32))
        VE.wait(VE)
        VE.op("tensor_scalar", mg0s, e_str, 0.0, None, OP.is_ge)
        VE.op("tensor_scalar", mg0i[:], e_str, -1.0, None, OP.is_ge)
        VE.wait(VE)
        VE.op("tensor_copy", inclb[:], mg0i[:])
        VE.op("tensor_scalar", idf[:], e_str, -1.0, None, OP.is_equal)
        VE.wait(VE)
        VE.op("tensor_copy", idb[:], idf[:])
        VE.op("memset", onesf[:], 1.0)
        VE.wait(VE)
        VE.op("tensor_copy", onesb[:], onesf[:])

        alin = {"f": cc[0], "s": cc[1]}
        alln = {"f": cc[2], "s": cc[3]}
        alb = {"f": cc[4], "s": cc[5]}
        GP.dma(cc[0][0:1, 0:1], fa_in)
        GP.dma(cc[1][0:1, 0:1], sa_in)
        GP.dma(rankc[0:1, :], rankm)
        SC.wait(GP)
        for m in "fs":
            SC.op("activation", alln[m][0:1, 0:1], alin[m][0:1, 0:1], AF.Ln, bias=0.0, scale=1.0)
            SC.op("activation", alb[m][0:1, 0:1], alin[m][0:1, 0:1], AF.Identity, bias=1.0, scale=-1.0)
        TE.wait(SC, VE)
        for i, m in enumerate("fs"):
            TE.op("matmul", pb[0][:, i:i + 1], onesf[0:1, :], alln[m][0:1, 0:1], start=True, stop=True)
            TE.op("matmul", pb[0][:, 2 + i:3 + i], onesf[0:1, :], alb[m][0:1, 0:1], start=True, stop=True)
        TE.op("matmul", pb[0][:, 4:12], onesf[0:1, :], rankc[0:1, :], start=True, stop=True)
        VE.wait(TE)
        lnrep = {"f": cc[6], "s": cc[7]}
        for i, m in enumerate("fs"):
            VE.op("tensor_copy", lnrep[m][:], pb[0][:, i:i + 1])
            VE.op("tensor_copy", betac[m][:], pb[0][:, 2 + i:3 + i])
        VE.op("tensor_copy", rankc[:], pb[0][:, 4:12])
        for m in "fs":
            VE.op("tensor_scalar", lamc[m][:], betac[m][:], -1.0, 1.0, OP.mult, OP.add)
            VE.op("tensor_tensor", t2[:, 0:1], iotac[:], lnrep[m][:], OP.mult)
            VE.op("tensor_tensor", t2[:, 1:2], i127c[:], lnrep[m][:], OP.mult)
            VE.op("tensor_scalar", t2[:, 2:3], lnrep[m][:], 128.0, None, OP.mult)
            VE.op("tensor_scalar", t2[:, 128:256], e_str, lnrep[m][:], None, OP.mult)
            SC.wait(VE)
            SC.op("activation", d1col[m][:], t2[:, 0:1], AF.Exp, bias=0.0, scale=1.0)
            SC.op("activation", cc[3][:], t2[:, 1:2], AF.Exp, bias=0.0, scale=1.0)
            SC.op("activation", lamC[m][:], t2[:, 2:3], AF.Exp, bias=0.0, scale=1.0)
            SC.op("activation", t1[:, 0:128], t2[:, 128:256], AF.Exp, bias=0.0, scale=1.0)
            VE.wait(SC)
            VE.op("tensor_tensor", d2bcol[m][:], cc[3][:], betac[m][:], OP.mult)
            VE.op("tensor_tensor", t1[:, 128:256], t1[:, 0:128], mg0s, OP.mult)
            VE.wait(VE)
            VE.op("tensor_copy", maskS[m][:], t1[:, 128:256])
            VE.op("tensor_scalar", maskX[m][:], t1[:, 128:256], betac[m][:], None, OP.mult)
            VE.op("tensor_tensor", t1[:, 256:384], t1[:, 0:128], mg0i[:], OP.mult)
            VE.wait(VE)
            VE.op("tensor_scalar", t1[:, 384:512], t1[:, 256:384], lamc[m][:], None, OP.mult)
            VE.wait(VE)
            VE.op("tensor_copy", maskI[m][:], t1[:, 384:512])

        def load_t_cols(wap, col, rows=64):
            SY.wait(TE, VE)
            SY.dma(w64[0:rows, :], wap)
            VE.wait(SY)
            VE.op("tensor_copy", w64b[0:rows, :], w64[0:rows, :])
            TE.wait(VE)
            for d in range(ND):
                TE.op("transpose", pf[d % 2][:, 0:rows], w64b[0:rows, d * 128:(d + 1) * 128], idb[0:rows, 0:rows])
                VE.wait(TE)
                VE.op("tensor_copy", wall[:, d * 392 + col: d * 392 + col + rows], pf[d % 2][:, 0:rows])
                TE.wait(VE)

        load_t_cols(aWk, 0)
        load_t_cols(aWq, 64)
        load_t_cols(aWv, 128)
        load_t_cols(ir_w1, 192)
        load_t_cols(pw, 256, rows=3)
        SY.wait(TE, VE)
        SY.dma(t2[:, 0:64], nm_w1)
        SY.dma(w64, aWv)
        VE.wait(SY)
        VE.op("tensor_copy", w64b, w64)
        VE.op("tensor_copy", b0[:, 0:64], t2[:, 0:64])
        TE.wait(VE)
        TE.op("transpose", pf[0][0:64, 0:128], b0[:, 0:64], idb[:])
        VE.wait(TE)
        VE.op("tensor_copy", b1[0:64, 0:128], pf[0][0:64, 0:128])
        TE.wait(VE)
        TE.op("matmul", pb[0][:], b1[0:64, 0:128], w64b[:, 0:512], start=True, stop=True)
        TE.op("matmul", pb[1][:], b1[0:64, 0:128], w64b[:, 512:1024], start=True, stop=True)
        VE.wait(TE)
        VE.op("tensor_copy", wtmpb[:, 0:512], pb[0][:])
        VE.op("tensor_copy", wtmpb[:, 512:1024], pb[1][:])
        TE.wait(VE)
        for d in range(ND):
            TE.op("transpose", pf[d % 2][:, 0:128], wtmpb[:, d * 128:(d + 1) * 128], idb[:])
            VE.wait(TE)
            VE.op("tensor_copy", wall[:, d * 392 + 259: d * 392 + 387], pf[d % 2][:, 0:128])
            TE.wait(VE)
        for d in range(ND):
            VE.op("memset", wall[:, d * 392 + 387:(d + 1) * 392], 0.0)
        TE.wait(VE)
        for d in range(ND):
            TE.op("matmul", pb[2][0:1, 0:392], onesb[:, 0:1], wall[:, d * 392: d * 392 + 392],
                  start=(d == 0), stop=(d == ND - 1))
        VE.wait(TE)
        VE.op("tensor_copy", b0[0:1, 0:392], pb[2][0:1, 0:392])
        TE.wait(VE)
        TE.op("matmul", pb[2][:, 0:392], onesb[0:1, :], b0[0:1, 0:392], start=True, stop=True)
        VE.wait(TE)
        VE.op("tensor_copy", wbarr[:], pb[2][:, 0:392])

        def replicate(dst, src_ap, width):
            SY.wait(TE, VE)
            SY.dma(brow[0:1, 0:width], src_ap)
            VE.wait(SY)
            VE.op("tensor_copy", browb[0:1, 0:width], brow[0:1, 0:width])
            TE.wait(VE)
            TE.op("matmul", pb[3][:, 0:width], onesb[0:1, :], browb[0:1, 0:width], start=True, stop=True)
            VE.wait(TE)
            VE.op("tensor_copy", dst, pb[3][:, 0:width])
            TE.wait(VE)

        replicate(nmb1r[:], nm_b1, 128)
        replicate(nmb2r[:], nm_b2, 64)
        replicate(irb1r[:], ir_b1, 64)
        replicate(ir2r[:], ir_w2, 64)
        replicate(irb2c[:], ir_b2, 1)
        replicate(nwr[:], norm_w, 64)
        SY.wait(TE, VE)
        SY.dma(w64[:, 0:128], nm_w2)
        VE.wait(SY)
        VE.op("tensor_copy", w64b[:, 0:128], w64[:, 0:128])
        TE.wait(VE)
        TE.op("transpose", pf[0][:, 0:64], w64b[0:64, 0:128], idb[0:64, 0:64])
        VE.wait(TE)
        VE.op("tensor_copy", nm2T[:], pf[0][:, 0:64])
        for d in range(ND):
            SY.wait(TE, VE)
            SY.dma(wtmp[:, 0:64], aWo[d * 128:(d + 1) * 128, :])
            VE.wait(SY)
            VE.op("tensor_copy", b0[:, 0:64], wtmp[:, 0:64])
            TE.wait(VE)
            TE.op("transpose", pf[d % 2][0:64, 0:128], b0[:, 0:64], idb[:])
            VE.wait(TE)
            VE.op("tensor_copy", woT[:, d * 128:(d + 1) * 128], pf[d % 2][0:64, 0:128])
            TE.wait(VE)

        SC.wait_tok(ld_resid)
        VE.wait_tok(ld_resid)
        for i in range(NT):
            xt = resid[:, i * 1024:(i + 1) * 1024]
            SC.op("activation", t0[:, 0:512], xt[:, 0:512], AF.Square, bias=0.0, scale=1.0, accum_out=cc[0][:])
            SC.op("activation", t1[:, 0:512], xt[:, 512:1024], AF.Square, bias=0.0, scale=1.0, accum_out=cc[1][:])
            VE.op("tensor_reduce", cc[2][:], xt, axis=AX, op=OP.add)
            VE.wait(SC)
            VE.wait(VE)
            VE.op("tensor_tensor", cc[3][:], cc[0][:], cc[1][:], OP.add)
            VE.op("tensor_scalar", cc[2][:], cc[2][:], 1.0 / 1024.0, None, OP.mult)
            SC.wait(VE)
            SC.op("activation", cc[4][:], cc[2][:], AF.Square, bias=0.0, scale=1.0)
            VE.wait(SC)
            VE.op("tensor_scalar", cc[3][:], cc[3][:], 1.0 / 1024.0, None, OP.mult)
            VE.wait(VE)
            VE.op("tensor_tensor", cc[3][:], cc[3][:], cc[4][:], OP.subtract)
            VE.wait(VE)
            VE.op("tensor_scalar", cc[5][:], cc[3][:], 1e-5, None, OP.add)
            SC.wait(VE)
            SC.op("activation", cc[4][:], cc[5][:], AF.Sqrt, bias=0.0, scale=1.0)
            VE.wait(SC)
            VE.op("reciprocal", stats1[:, 2 * i:2 * i + 1], cc[4][:])
            VE.wait(VE)
            VE.op("tensor_tensor", cc[4][:], cc[2][:], stats1[:, 2 * i:2 * i + 1], OP.mult)
            VE.wait(VE)
            VE.op("tensor_scalar", stats1[:, 2 * i + 1:2 * i + 2], cc[4][:], -1.0, None, OP.mult)
        if "attn_a" in PHASES:
            for i in range(NT):
                xt = resid[:, i * 1024:(i + 1) * 1024]
                rstd = stats1[:, 2 * i:2 * i + 1]
                nmrs = stats1[:, 2 * i + 1:2 * i + 2]
                TE.wait(VE)
                for d in range(ND):
                    TE.op("transpose", pb[1][:, 0:128], xt[:, d * 128:(d + 1) * 128], idf[:])
                    VE.wait(TE)
                    VE.op("tensor_copy", xfmb[:, d * 128:(d + 1) * 128], pb[1][:, 0:128])
                    TE.wait(VE)
                for d in range(ND):
                    TE.op("matmul", pb[0][:, 0:392], xfmb[:, d * 128:(d + 1) * 128],
                          wall[:, d * 392:(d + 1) * 392], start=(d == 0), stop=(d == ND - 1))
                VE.wait(TE)
                VE.op("tensor_scalar", t0[:, 0:392], pb[0][:, 0:392], rstd, None, OP.mult)
                VE.op("scalar_tensor_tensor", proj, wbarr[:], nmrs, t0[:, 0:392], OP.mult, OP.add)
                VE.wait(VE)

                SC.wait(VE)
                SC.op("activation", t0[:, 0:64], proj[:, 0:64], AF.Square, bias=0.0, scale=1.0, accum_out=cc[0][:])
                SC.op("activation", t0[:, 64:128], proj[:, 64:128], AF.Square, bias=0.0, scale=1.0, accum_out=cc[1][:])
                SC.wait(SC)
                SC.op("activation", cc[2][:], cc[0][:], AF.Sqrt, bias=0.0, scale=1.0)
                SC.op("activation", cc[3][:], cc[1][:], AF.Sqrt, bias=0.0, scale=1.0)
                VE.wait(SC)
                VE.op("tensor_scalar", cc[2][:], cc[2][:], 1e-12, None, OP.max)
                VE.op("tensor_scalar", cc[3][:], cc[3][:], 1e-12, None, OP.max)
                VE.wait(VE)
                VE.op("reciprocal", cc[0][:], cc[2][:])
                VE.op("reciprocal", cc[1][:], cc[3][:])
                VE.wait(VE)
                VE.op("tensor_scalar", kbf[:, i * 64:(i + 1) * 64], proj[:, 0:64], cc[0][:], None, OP.mult)
                VE.op("tensor_scalar", qbf[:, i * 64:(i + 1) * 64], proj[:, 64:128], cc[1][:], None, OP.mult)

                VE.op("tensor_copy", vsb[:, i * 64:(i + 1) * 64], proj[:, 128:192])
                VE.op("tensor_tensor", t0[:, 0:128], proj[:, 259:387], nmb1r[:], OP.add)
                SC.wait(VE)
                SC.op("activation", t1[:, 0:128], t0[:, 0:128], AF.Gelu, bias=0.0, scale=1.0)
                VE.wait(SC)
                VE.op("tensor_copy", b0[:, 0:128], t1[:, 0:128])
                TE.wait(VE)
                TE.op("transpose", pf[0][:, 0:128], b0[:, 0:128], idb[:])
                VE.wait(TE)
                VE.op("tensor_copy", b1[:, 0:128], pf[0][:, 0:128])
                TE.wait(VE)
                TE.op("matmul", pb[1][:, 0:64], b1[:, 0:128], nm2T[:], start=True, stop=True)
                VE.wait(TE)
                VE.op("tensor_tensor", nusb[:, i * 64:(i + 1) * 64], pb[1][:, 0:64], nmb2r[:], OP.add)

                VE.op("tensor_tensor", t0[:, 192:256], proj[:, 192:256], irb1r[:], OP.add)
                SC.wait(VE)
                SC.op("activation", t0[:, 128:192], t0[:, 192:256], AF.Gelu, bias=0.0, scale=1.0)
                VE.wait(SC)
                VE.op("tensor_tensor", t1[:, 0:64], t0[:, 128:192], ir2r[:], OP.mult)
                VE.wait(VE)
                VE.op("tensor_reduce", cc[0][:], t1[:, 0:64], axis=AX, op=OP.add)
                VE.wait(VE)
                VE.op("tensor_tensor", cc[1][:], cc[0][:], irb2c[:], OP.add)
                VE.wait(VE)
                VE.op("tensor_scalar", ucol[:, i:i + 1], cc[1][:], 0.0, None, OP.is_gt)

                VE.op("tensor_reduce", cc[0][:], proj[:, 256:259], axis=AX, op=OP.max)
                VE.wait(VE)
                VE.op("tensor_scalar", cc[1][:], cc[0][:], -1.0, None, OP.mult)
                SC.wait(VE)
                SC.op("activation", t0[:, 0:3], proj[:, 256:259], AF.Exp, bias=cc[1][:], scale=1.0)
                VE.wait(SC)
                VE.op("tensor_reduce", cc[2][:], t0[:, 0:3], axis=AX, op=OP.add)
                VE.wait(VE)
                VE.op("reciprocal", cc[2][:], cc[2][:])
                VE.wait(VE)
                VE.op("tensor_scalar", pwtsb[:, i * 4:i * 4 + 3], t0[:, 0:3], cc[2][:], None, OP.mult)

                TE.wait(VE)
                TE.op("transpose", pf[0][0:64, 0:128], kbf[:, i * 64:(i + 1) * 64], idb[:])
                TE.op("transpose", pf[1][0:64, 0:128], qbf[:, i * 64:(i + 1) * 64], idb[:])
                VE.wait(TE)
                VE.op("tensor_copy", b0[0:64, 0:128], pf[0][0:64, 0:128])
                VE.op("tensor_copy", b1[0:64, 0:128], pf[1][0:64, 0:128])
                TE.wait(VE)
                TE.op("matmul", pb[2][:, 0:128], b0[0:64, 0:128], b0[0:64, 0:128], start=True, stop=True)
                TE.op("matmul", pb[3][:, 0:128], b0[0:64, 0:128], b1[0:64, 0:128], start=True, stop=True)
                VE.wait(TE)
                VE.wait(GP)
                VE.op("tensor_copy", gsf, pb[2][:, 0:128])
                VE.op("tensor_copy", b3[:, 0:128], pb[3][:, 0:128])
                GP.wait(VE)
                GP.dma(spillQ[i], b3[:, 0:128])

                for mi, m in enumerate("fs"):
                    VE.wait(GP)
                    if m == "f":
                        VE.op("tensor_tensor", xcp[0], gsf, maskX[m][:], OP.mult)
                    else:
                        VE.op("scalar_tensor_tensor", xcp[0], gsf, ucol[:, i:i + 1], maskX[m][:], OP.mult, OP.mult)
                    TE.wait(VE)
                    TE.op("transpose", pf[0][:, 0:128], xcp[0], idb[:])
                    VE.wait(TE)
                    VE.op("tensor_copy", xrb[0], pf[0][:, 0:128])
                    VE.op("tensor_copy", solR[:, 0:64], vsb[:, i * 64:(i + 1) * 64])
                    VE.op("tensor_scalar", solR[:, 64:128], kbf[:, i * 64:(i + 1) * 64], d1col[m][:], None, OP.mult)
                    VE.wait(VE)
                    VE.op("tensor_copy", solRb[:], solR)
                    TE.wait(VE)
                    for lev in range(6):
                        TE.op("matmul", pb[1][:, 0:128], xcp[lev], xrb[lev % 2], start=True, stop=True)
                        VE.wait(TE)
                        VE.op("tensor_copy", xrb[(lev + 1) % 2], pb[1][:, 0:128])
                        TE.wait(VE)
                        TE.op("transpose", pf[0][:, 0:128], xrb[(lev + 1) % 2], idb[:])
                        VE.wait(TE)
                        VE.op("tensor_copy", xcp[lev + 1], pf[0][:, 0:128])
                        TE.wait(VE)
                    for j in range(6, -1, -1):
                        TE.op("matmul", pb[3][:, 0:128], xcp[j], solRb[:], start=True, stop=True)
                        VE.wait(TE)
                        VE.op("tensor_tensor", solR, solR, pb[3][:, 0:128],
                              OP.add if j > 0 else OP.subtract)
                        VE.wait(VE)
                        VE.op("tensor_copy", solRb[:], solR)
                        TE.wait(VE)
                    if m == "f":
                        VE.op("tensor_scalar", b2[:, 0:64], kbf[:, i * 64:(i + 1) * 64], d2bcol[m][:], None, OP.mult)
                    else:
                        VE.op("tensor_scalar", b2[:, 0:64], kbf[:, i * 64:(i + 1) * 64], d2bcol[m][:], ucol[:, i:i + 1], OP.mult, OP.mult)
                    GP.wait(VE)
                    GP.dma(spillA[2 * i + mi][:, 0:128], solRb[:])
                    GP.dma(spillA[2 * i + mi][:, 128:192], b2[:, 0:64])
                    TE.wait(VE)
                    TE.op("matmul", pb[4][0:64, 0:128], b2[:, 0:64], solRb[:], start=True, stop=True)
                    VE.wait(TE)
                    VE.op("scalar_tensor_tensor", t0[0:64, 0:64], idf[0:64, 0:64], lamC[m][0:64, :], pb[4][0:64, 64:128], OP.mult, OP.subtract)
                    VE.op("tensor_copy", t1[0:64, 64:128], pb[4][0:64, 0:64])
                    VE.wait(VE)
                    VE.op("tensor_copy", b2[0:64, 64:128], t0[0:64, 0:64])
                    if i == 0:
                        VE.op("tensor_copy", mn[m][0:64, 64:128], t0[0:64, 0:64])
                        VE.op("tensor_copy", mn[m][0:64, 0:64], pb[4][0:64, 0:64])
                        VE.wait(VE)
                        VE.op("tensor_copy", mnb[m][:], mn[m][:])
                    else:
                        TE.wait(VE)
                        TE.op("transpose", pf[0][0:64, 0:64], b2[0:64, 64:128], idb[0:64, 0:64])
                        VE.wait(TE)
                        VE.op("tensor_copy", b3[0:64, 128:192], pf[0][0:64, 0:64])
                        TE.wait(VE)
                        TE.op("matmul", pb[5][0:64, 0:128], b3[0:64, 128:192], mnb[m][:], start=True, stop=True)
                        VE.wait(TE)
                        VE.op("tensor_tensor", mn[m][0:64, 0:64], pb[5][0:64, 0:64], t1[0:64, 64:128], OP.add)
                        VE.op("tensor_copy", mn[m][0:64, 64:128], pb[5][0:64, 64:128])
                        VE.wait(VE)
                        VE.op("tensor_copy", mnb[m][:], mn[m][:])
                    TE.wait(VE)

            GP.wait(VE)
            tap(GP, "stats1", stats1[:])
            tap(GP, "wall", wall[:])
            tap(GP, "xfmbL", xfmb[:])
            tap(GP, "wbarr", wbarr[:])
            tap(GP, "kbf", kbf[:])
            tap(GP, "qbf", qbf[:])
            tap(GP, "vsb", vsb[:])
            tap(GP, "nusb", nusb[:])
            tap(GP, "ucol", ucol[:])
            tap(GP, "pwtsb", pwtsb[:])
            tap(GP, "mnf", mn["f"][0:64, :])
            tap(GP, "mns", mn["s"][0:64, :])
            tap(GP, "solA_d", spillA)
            GP.wait(VE)
            GP.dma(ccA_in[:, 0:64], mn["f"][0:64, 0:64])
            GP.dma(ccA_in[:, 64:128], mn["f"][0:64, 64:128])
            GP.dma(ccA_in[:, 128:192], mn["s"][0:64, 0:64])
            GP.dma(ccA_in[:, 192:256], mn["s"][0:64, 64:128])
            GP.wait(GP)
            tokA = GP.coll("AllGather", OP.bypass,
                           replica_groups=[[0, 1, 2, 3], [4, 5, 6, 7]],
                           ins=[ccA_in.opt()], outs=[ccA_out.opt()])
            gat = wtmp[0:64, :]
            GP.wait_tok(tokA)
            GP.wait(TE, VE)
            for r in range(4):
                GP.dma(gat[:, r * 256:(r + 1) * 256], ccA_out[r * 64:(r + 1) * 64, :])
            VE.wait(GP)
            VE.op("tensor_scalar", t2[:, 200:208], rankc[:], -1.0, 1.0, OP.mult, OP.add)
            for mi, m in enumerate("fs"):
                VE.op("memset", s0f[m][:], 0.0)
                VE.op("memset", s0b[m][:], 0.0)
            VE.wait(VE)
            for r in range(4):
                for mi, m in enumerate("fs"):
                    Nj = gat[0:64, r * 256 + mi * 128: r * 256 + mi * 128 + 64]
                    Mj = gat[0:64, r * 256 + mi * 128 + 64: r * 256 + mi * 128 + 128]
                    rj = rankc[0:64, r:r + 1]
                    rnj = t2[0:64, 200 + r:201 + r]
                    VE.op("tensor_scalar", t0[0:64, 0:64], Mj, rj, None, OP.mult)
                    VE.wait(VE)
                    VE.op("scalar_tensor_tensor", t0[0:64, 0:64], idf[0:64, 0:64], rnj, t0[0:64, 0:64], OP.mult, OP.add)
                    VE.wait(VE)
                    VE.op("tensor_copy", b2[0:64, 0:64], t0[0:64, 0:64])
                    VE.op("tensor_scalar", t1[0:64, 0:64], Nj, rj, None, OP.mult)
                    TE.wait(VE)
                    TE.op("transpose", pf[0][0:64, 0:64], b2[0:64, 0:64], idb[0:64, 0:64])
                    VE.wait(TE)
                    VE.op("tensor_copy", b3[0:64, 0:64], pf[0][0:64, 0:64])
                    TE.wait(VE)
                    TE.op("matmul", pb[0][0:64, 0:64], b3[0:64, 0:64], s0b[m][:], start=True, stop=True)
                    VE.wait(TE)
                    VE.op("tensor_tensor", s0f[m][:], pb[0][0:64, 0:64], t1[0:64, 0:64], OP.add)
                    VE.wait(VE)
                    VE.op("tensor_copy", s0b[m][:], s0f[m][:])

            GP.wait(VE)
            tap(GP, "gatA2", gat[0:64, :])
            tap(GP, "rankcT", rankc[:])
            tap(GP, "s0fA2", s0f["f"][:])
            tap(GP, "s0sA2", s0f["s"][:])
            solA = cx.sb("solA", [128, 192 * 2], BF16, pool=aa)
            qgr = cx.sb("qgr", [128, 128], BF16, pool=aa)
            pre = {"f": u1[:, 128:192], "s": u1[:, 192:256]}
            post = {"f": u1[:, 256:320], "s": u1[:, 320:384]}
            utl = u1[:, 64:128]
            t0sb = u1[:, 0:64]
            utlb = cx.sb("utlb", [128, 64], BF16, pool=aa)
            ub = cx.sb("ub", [128, 64], BF16, pool=aa)
            aqm = cx.sb("aqm", [128, 128], BF16, pool=aa)
            qcs = b3[0:64, 256:384]

            SY.wait(GP)
            for i in range(NT):
                SY.wait(TE, VE)
                SY.dma(solA[:, 0:192], spillA[2 * i])
                SY.dma(solA[:, 192:384], spillA[2 * i + 1])
                SY.dma(qgr[:], spillQ[i])
                TE.wait(SY)
                VE.wait(SY)
                for mi, m in enumerate("fs"):
                    sol = solA[:, mi * 192: mi * 192 + 128]
                    k2b = solA[:, mi * 192 + 128: mi * 192 + 192]
                    TE.op("transpose", pf[0][0:64, 0:128], sol[:, 64:128], idb[:])
                    VE.op("tensor_scalar", b0[:, 0:64], qbf[:, i * 64:(i + 1) * 64], d1col[m][:], None, OP.mult)
                    TE.wait(VE)
                    TE.op("transpose", pf[1][0:64, 0:128], b0[:, 0:64], idb[:])
                    VE.wait(TE)
                    VE.op("tensor_copy", b1[0:64, 0:128], pf[0][0:64, 0:128])
                    VE.op("tensor_copy", qcs, pf[1][0:64, 0:128])
                    TE.wait(VE)
                    TE.op("matmul", pb[0][:, 0:64], b1[0:64, 0:128], s0b[m][:], start=True, stop=True)
                    TE.op("matmul", pb[1][:, 0:64], qcs, s0b[m][:], start=True, stop=True)
                    VE.wait(TE)
                    VE.op("tensor_tensor", utl, sol[:, 0:64], pb[0][:, 0:64], OP.subtract)
                    VE.op("tensor_copy", t0sb, pb[1][:, 0:64])
                    VE.wait(VE)
                    VE.op("tensor_copy", utlb[:], utl)
                    if m == "f":
                        VE.op("tensor_scalar", t0[:, 0:64], utl, betac[m][:], None, OP.mult)
                    else:
                        VE.op("tensor_scalar", t0[:, 0:64], utl, betac[m][:], ucol[:, i:i + 1], OP.mult, OP.mult)
                    VE.wait(VE)
                    VE.op("tensor_copy", ub[:], t0[:, 0:64])
                    VE.op("tensor_tensor", aqm[:], qgr[:], maskS[m][:], OP.mult)
                    TE.wait(VE)
                    TE.op("matmul", pb[2][:, 0:64], aqm[:], ub[:], start=True, stop=True)
                    VE.wait(TE)
                    VE.op("tensor_tensor", pre[m], pb[2][:, 0:64], t0sb, OP.add)
                    VE.op("tensor_tensor", aqm[:], qgr[:], maskI[m][:], OP.mult)
                    TE.wait(VE)
                    TE.op("matmul", pb[2][:, 0:64], aqm[:], ub[:], start=True, stop=True)
                    TE.op("matmul", pb[3][0:64, 0:64], k2b[:], utlb[:], start=True, stop=True)
                    VE.wait(TE)
                    VE.op("scalar_tensor_tensor", post[m], t0sb, lamc[m][:], pb[2][:, 0:64], OP.mult, OP.add)
                    VE.op("scalar_tensor_tensor", s0f[m][:], s0f[m][:], lamC[m][0:64, :], pb[3][0:64, 0:64], OP.mult, OP.add)
                    VE.wait(VE)
                    VE.op("tensor_copy", s0b[m][:], s0f[m][:])
                    TE.wait(VE)

                VE.op("tensor_tensor", t0[:, 0:64], pre["f"], pre["s"], OP.add)
                VE.wait(VE)
                VE.op("tensor_scalar", t0[:, 64:128], t0[:, 0:64], -0.5, None, OP.mult)
                VE.wait(VE)
                VE.op("tensor_tensor", t1[:, 0:64], vsb[:, i * 64:(i + 1) * 64], t0[:, 64:128], OP.add)
                SC.wait(VE)
                SC.op("activation", t1[:, 64:128], t1[:, 0:64], AF.Square, bias=0.0, scale=1.0, accum_out=cc[0][:])
                VE.wait(SC)
                VE.op("tensor_scalar", cc[1][:], cc[0][:], 1.0 / (1.0 + 1e-6), LN7_3, OP.mult, OP.is_gt)
                VE.wait(VE)
                VE.op("tensor_copy", b0[:, 0:1], cc[1][:])
                TE.wait(VE)
                TE.op("matmul", pb[0][:, 0:1], inclb[:], b0[:, 0:1], start=True, stop=True)
                VE.wait(TE)
                VE.op("tensor_scalar", t0[:, 0:1], pb[0][:, 0:1], LN09, None, OP.mult)
                VE.wait(VE)
                SC.wait(VE)
                SC.op("activation", pcol[:, i:i + 1], t0[:, 0:1], AF.Exp, bias=0.0, scale=1.0)
                VE.op("tensor_scalar", t0[:, 1:2], t0[:, 0:1], -1.0, None, OP.mult)
                TE.wait(VE)
                TE.op("transpose", pb[1][0:1, 0:128], t0[:, 0:1], idf[:])
                VE.wait(TE)
                VE.op("tensor_copy", t2[0:1, 0:128], pb[1][0:1, 0:128])
                VE.wait(VE)
                VE.op("tensor_copy", b0[0:1, 0:128], t2[0:1, 0:128])
                TE.wait(VE)
                TE.op("matmul", pb[2][:, 0:128], onesb[0:1, :], b0[0:1, 0:128], start=True, stop=True)
                SC.wait(TE)
                SC.op("activation", t2[:, 256:384], pb[2][:, 0:128], AF.Exp, bias=t0[:, 1:2], scale=1.0)
                VE.wait(SC)
                VE.op("tensor_tensor", t2[:, 128:256], t2[:, 256:384], mg0i[:], OP.mult)
                VE.op("tensor_scalar", t1[:, 0:64], nusb[:, i * 64:(i + 1) * 64], cc[1][:], 0.1, OP.mult, OP.mult)
                VE.wait(VE)
                VE.op("tensor_copy", b1[:, 0:128], t2[:, 128:256])
                VE.op("tensor_copy", b2[:, 0:64], t1[:, 0:64])
                TE.wait(VE)
                TE.op("matmul", pb[3][:, 0:64], b1[:, 0:128], b2[:, 0:64], start=True, stop=True)
                TE.op("matmul", pb[4][0:1, 0:64], b1[:, 127:128], b2[:, 0:64], start=True, stop=True)
                VE.wait(TE)
                VE.op("tensor_copy", nsloc[:, i * 64:(i + 1) * 64], pb[3][:, 0:64])
                VE.op("tensor_copy", rcr[0:1, i * 64:(i + 1) * 64], pb[4][0:1, 0:64])
                SC.op("activation", pcr[0:1, i:i + 1], t2[0:1, 127:128], AF.Exp, bias=0.0, scale=1.0)
                VE.op("tensor_scalar", t0[:, 0:64], post["f"], pwtsb[:, i * 4:i * 4 + 1], None, OP.mult)
                VE.wait(VE)
                VE.op("scalar_tensor_tensor", oprt[:, i * 64:(i + 1) * 64], post["s"], pwtsb[:, i * 4 + 1:i * 4 + 2], t0[:, 0:64], OP.mult, OP.add)

            GP.wait(VE, SC)
            tap(GP, "prefL", pre["f"])
            tap(GP, "presL", pre["s"])
            tap(GP, "postfL", post["f"])
            tap(GP, "postsL", post["s"])
            tap(GP, "pcr", pcr[0:1, :])
            tap(GP, "rcr", rcr[0:1, :])
            tap(GP, "pcol", pcol[:])
            tap(GP, "nsloc", nsloc[:])
            tap(GP, "oprt", oprt[:])
            tap(GP, "s0fA3", s0f["f"][:])
            tap(GP, "s0sA3", s0f["s"][:])
            VE.wait(SC)
            VE.op("memset", t2[0:1, 0:64], 0.0)
            VE.op("memset", t2[0:1, 64:65], 1.0)
            VE.wait(VE)
            for i in range(NT):
                VE.op("scalar_tensor_tensor", t2[0:1, 0:64], t2[0:1, 0:64], pcr[0:1, i:i + 1], rcr[0:1, i * 64:(i + 1) * 64], OP.mult, OP.add)
                VE.op("tensor_tensor", t2[0:1, 64:65], t2[0:1, 64:65], pcr[0:1, i:i + 1], OP.mult)
                VE.wait(VE)
            GP.wait(VE)
            GP.dma(ccN_in[0:1, 0:1], t2[0:1, 64:65])
            GP.dma(ccN_in[0:1, 1:65], t2[0:1, 0:64])
            GP.wait(GP)
            tokN = GP.coll("AllGather", OP.bypass,
                           replica_groups=[[0, 1, 2, 3], [4, 5, 6, 7]],
                           ins=[ccN_in.opt()], outs=[ccN_out.opt()])
            gat2 = wtmp[0:1, 0:264]
            GP.wait_tok(tokN)
            GP.wait(TE, VE)
            GP.dma(gat2[0:1, :], ccN_out.rearrange("(o a) b -> o (a b)", o=1))
            VE.wait(GP)
            VE.op("memset", t2[0:1, 128:192], 0.0)
            VE.wait(VE)
            for r in range(4):
                Pj = gat2[0:1, r * 66: r * 66 + 1]
                rjv = gat2[0:1, r * 66 + 1: r * 66 + 65]
                rj = rankc[0:1, r:r + 1]
                VE.op("tensor_scalar", t0[0:1, 0:1], Pj, rj, None, OP.mult)
                VE.op("tensor_scalar", t0[0:1, 1:2], rj, -1.0, 1.0, OP.mult, OP.add)
                VE.wait(VE)
                VE.op("tensor_tensor", t0[0:1, 0:1], t0[0:1, 0:1], t0[0:1, 1:2], OP.add)
                VE.op("tensor_scalar", t0[0:1, 2:66], rjv, rj, None, OP.mult)
                VE.wait(VE)
                VE.op("scalar_tensor_tensor", t2[0:1, 128:192], t2[0:1, 128:192], t0[0:1, 0:1], t0[0:1, 2:66], OP.mult, OP.add)
                VE.wait(VE)
            ns0c = cx.sb("ns0c", [1, 64], F32, pool=aa)
            VE.op("tensor_copy", ns0c[0:1, :], t2[0:1, 128:192])
            VE.wait(VE)
            GP.wait(VE)
            tap(GP, "nscs", ns0c[0:1, :])

            for i in range(NT):
                VE.op("tensor_copy", browb[0:1, 0:64], ns0c[0:1, :])
                TE.wait(VE)
                TE.op("matmul", pb[0][:, 0:64], onesb[0:1, :], browb[0:1, 0:64], start=True, stop=True)
                VE.wait(TE)
                VE.op("scalar_tensor_tensor", t0[:, 0:64], pb[0][:, 0:64], pcol[:, i:i + 1], nsloc[:, i * 64:(i + 1) * 64], OP.mult, OP.add)
                VE.wait(VE)
                VE.op("scalar_tensor_tensor", t1[:, 0:64], t0[:, 0:64], pwtsb[:, i * 4 + 2:i * 4 + 3], oprt[:, i * 64:(i + 1) * 64], OP.mult, OP.add)
                SC.wait(VE)
                SC.op("activation", t2[:, 0:64], t1[:, 0:64], AF.Square, bias=0.0, scale=1.0, accum_out=cc[0][:])
                VE.wait(SC)
                VE.op("tensor_scalar", cc[1][:], cc[0][:], 1.0 / 64.0, 1e-6, OP.mult, OP.add)
                SC.wait(VE)
                SC.op("activation", cc[2][:], cc[1][:], AF.Sqrt, bias=0.0, scale=1.0)
                VE.wait(SC)
                VE.op("reciprocal", cc[3][:], cc[2][:])
                VE.wait(VE)
                VE.op("tensor_scalar", t2[:, 0:64], t1[:, 0:64], cc[3][:], None, OP.mult)
                VE.wait(VE)
                VE.op("tensor_tensor", t2[:, 64:128], t2[:, 0:64], nwr[:], OP.mult)
                VE.wait(VE)
                VE.op("tensor_copy", b0[:, 0:64], t2[:, 64:128])
                TE.wait(VE)
                TE.op("transpose", pf[0][0:64, 0:128], b0[:, 0:64], idb[:])
                VE.wait(TE)
                VE.op("tensor_copy", b1[0:64, 0:128], pf[0][0:64, 0:128])
                TE.wait(VE)
                TE.op("matmul", pb[1][:, 0:512], b1[0:64, 0:128], woT[:, 0:512], start=True, stop=True)
                TE.op("matmul", pb[2][:, 0:512], b1[0:64, 0:128], woT[:, 512:1024], start=True, stop=True)
                VE.wait(TE)
                VE.op("tensor_tensor", resid[:, i * 1024:i * 1024 + 512], resid[:, i * 1024:i * 1024 + 512], pb[1][:, 0:512], OP.add)
                VE.op("tensor_tensor", resid[:, i * 1024 + 512:i * 1024 + 1024], resid[:, i * 1024 + 512:i * 1024 + 1024], pb[2][:, 0:512], OP.add)
                VE.op("scalar_tensor_tensor", ns0c[0:1, :], ns0c[0:1, :], pcr[0:1, i:i + 1], rcr[0:1, i * 64:(i + 1) * 64], OP.mult, OP.add)
                VE.wait(VE)
                TE.wait(VE)

        full_barrier()
        cx.free_pool(aa)
        def attn_b_emit():
            SCALE_QK = float(64.0 ** -0.25)
            bbP = []
            Gtot = cx.sb("Gtot", [128, H * 66], F32, pool=bbP)
            Sext = cx.sb("Sext", [128, H * 66], F32, pool=bbP)
            Sexb = cx.sb("Sexb", [128, 66], BF16, pool=bbP)
            gamr = cx.sb("gamr", [128, 1024], BF16, pool=bbP)
            betr = cx.sb("betr", [128, 1024], BF16, pool=bbP)
            b2r = cx.sb("b2r", [128, 1024], F32, pool=bbP)
            b1c = cx.sb("b1c", [128, 32], F32, pool=bbP)
            omgT = cx.sb("omgT", [128, 128], BF16, pool=bbP)

            for src, dst, nr, ncs in ((bWq, bWq_sc, 8, 1), (bWk, bWk_sc, 8, 1), (bWv, bWv_sc, 8, 1),
                                      (bWo, bWo_sc, 8, 1), (bW1, bW1_sc, 8, 4), (bW2, bW2_sc, 32, 1)):
                w = src.rearrange("(a p) (b q) -> a b p q", p=128, q=1024)
                d2 = dst.rearrange("(a p) (b q) -> a b p q", p=128, q=1024)
                for r in range(nr):
                    for c_ in range(ncs):
                        SY.wait(VE, GP)
                        SY.dma(wtmp[:], w[r, c_])
                        VE.wait(SY)
                        VE.op("tensor_copy", wtmpb[:], wtmp[:])
                        GP.wait(VE)
                        GP.dma(d2[r, c_], wtmpb[:])
            SY.wait(VE, GP, TE)
            SY.dma(wtmp[:, 0:64], omega)
            VE.wait(SY)
            VE.wait(GP)
            VE.op("tensor_copy", wtmpb[:, 0:64], wtmp[:, 0:64])
            TE.wait(VE)
            TE.op("transpose", pf[0][0:64, 0:128], wtmpb[:, 0:64], idb[:])
            VE.wait(TE)
            VE.op("memset", omgT[:], 0.0)
            VE.wait(VE)
            VE.op("tensor_copy", omgT[0:64, :], pf[0][0:64, 0:128])
            VE.op("memset", omgT[64:65, :], -0.5)
            replicate(gamr[:, 0:512], gam[0:1, 0:512], 512)
            replicate(gamr[:, 512:1024], gam[0:1, 512:1024], 512)
            replicate(betr[:, 0:512], bet[0:1, 0:512], 512)
            replicate(betr[:, 512:1024], bet[0:1, 512:1024], 512)
            replicate(b2r[:, 0:512], bb2[0:1, 0:512], 512)
            replicate(b2r[:, 512:1024], bb2[0:1, 512:1024], 512)
            for rr in range(4):
                SY.wait(VE, TE)
                SY.dma(wtmp[0:1, :], bb1[0:1, rr * 1024:(rr + 1) * 1024])
                TE.wait(SY)
                for c8 in range(8):
                    ch = rr * 8 + c8
                    TE.op("transpose", pb[5][:, 0:1], wtmp[0:1, c8 * 128:(c8 + 1) * 128], idf[0:1, 0:1])
                    VE.wait(TE)
                    VE.op("tensor_copy", b1c[:, ch:ch + 1], pb[5][:, 0:1])
                    TE.wait(VE)
            full_barrier()
            for i in range(NT):
                xt = resid[:, i * 1024:(i + 1) * 1024]
                SC.wait(VE)
                SC.op("activation", t0[:, 0:512], xt[:, 0:512], AF.Square, bias=0.0, scale=1.0, accum_out=cc[0][:])
                SC.op("activation", t1[:, 0:512], xt[:, 512:1024], AF.Square, bias=0.0, scale=1.0, accum_out=cc[1][:])
                VE.op("tensor_reduce", cc[2][:], xt, axis=AX, op=OP.add)
                VE.wait(SC)
                VE.wait(VE)
                VE.op("tensor_tensor", cc[3][:], cc[0][:], cc[1][:], OP.add)
                VE.op("tensor_scalar", cc[2][:], cc[2][:], 1.0 / 1024.0, None, OP.mult)
                SC.wait(VE)
                SC.op("activation", cc[4][:], cc[2][:], AF.Square, bias=0.0, scale=1.0)
                VE.wait(SC)
                VE.op("tensor_scalar", cc[3][:], cc[3][:], 1.0 / 1024.0, None, OP.mult)
                VE.wait(VE)
                VE.op("tensor_tensor", cc[3][:], cc[3][:], cc[4][:], OP.subtract)
                VE.wait(VE)
                VE.op("tensor_scalar", cc[5][:], cc[3][:], 1e-5, None, OP.add)
                SC.wait(VE)
                SC.op("activation", cc[4][:], cc[5][:], AF.Sqrt, bias=0.0, scale=1.0)
                VE.wait(SC)
                VE.op("reciprocal", stats1[:, 2 * i:2 * i + 1], cc[4][:])
                VE.wait(VE)
                VE.op("tensor_tensor", cc[4][:], cc[2][:], stats1[:, 2 * i:2 * i + 1], OP.mult)
                VE.wait(VE)
                VE.op("tensor_scalar", stats1[:, 2 * i + 1:2 * i + 2], cc[4][:], -1.0, None, OP.mult)
            VE.op("memset", Gtot[:], 0.0)

            bb1p = []
            l3f = cx.sb("l3f", [128, 8 * 1024], BF16, pool=bb1p)
            qaug = cx.sb("qaug", [128, 1024], BF16, pool=bb1p)
            wqk8 = [cx.sb(f"wqk8{j}", [128, 1024], BF16, pool=bb1p) for j in range(2)]
            vext = cx.sb("vext", [128, 66], BF16, pool=bb1p)
            vtm = cx.sb("vtm", [128, 128], BF16, pool=bb1p)
            VE.op("memset", vext[:], 0.0)
            VE.wait(VE)
            VE.op("memset", vext[:, 64:65], 1.0)
            for g in range(4):
                for ii in range(8):
                    i = g * 8 + ii
                    xt = resid[:, i * 1024:(i + 1) * 1024]
                    VE.wait(TE)
                    VE.op("tensor_scalar", u0[:], xt, stats1[:, 2 * i:2 * i + 1], stats1[:, 2 * i + 1:2 * i + 2], OP.mult, OP.add)
                    VE.wait(VE)
                    VE.op("tensor_copy", wtmpb[:], u0[:])
                    TE.wait(VE)
                    for d in range(ND):
                        TE.op("transpose", pf[d % 2][:, 0:128], wtmpb[:, d * 128:(d + 1) * 128], idb[:])
                        VE.wait(TE)
                        VE.op("tensor_copy", l3f[:, d * 1024 + ii * 128: d * 1024 + ii * 128 + 128], pf[d % 2][:, 0:128])
                        TE.wait(VE)
                for wi, (wsc, kind) in enumerate(((bWv_sc, "v"), (bWk_sc, "k"), (bWq_sc, "q"))):
                    for o8 in range(8):
                        pA = pb[0] if o8 % 2 == 0 else pb[2]
                        pB = pb[1] if o8 % 2 == 0 else pb[3]
                        wpar = o8 % 2
                        SY.wait(TE)
                        SY.dma(wqk8[wpar][:].rearrange("p (a q) -> p a q", q=128),
                               wsc[:, o8 * 128:(o8 + 1) * 128].rearrange("(a p) q -> p a q", p=128))
                        TE.wait(SY)
                        for d in range(ND):
                            TE.op("matmul", pA[:, 0:512], wqk8[wpar][:, d * 128:(d + 1) * 128], l3f[:, d * 1024:d * 1024 + 512], start=(d == 0), stop=(d == ND - 1))
                            TE.op("matmul", pB[:, 0:512], wqk8[wpar][:, d * 128:(d + 1) * 128], l3f[:, d * 1024 + 512:d * 1024 + 1024], start=(d == 0), stop=(d == ND - 1))
                        VE.wait(TE)
                        if kind == "v":
                            VE.op("tensor_copy", b0[:, 0:512], pA[:, 0:512])
                            VE.op("tensor_copy", b1[:, 0:512], pB[:, 0:512])
                            TE.wait(VE)
                            for tt in range(8):
                                src = b0 if tt < 4 else b1
                                TE.op("transpose", pf[tt % 2][:, 0:128], src[:, (tt % 4) * 128:(tt % 4) * 128 + 128], idb[:])
                                VE.wait(TE)
                                VE.op("tensor_copy", vtm[:], pf[tt % 2][:, 0:128])
                                GP.wait(VE)
                                GP.dma(v_sc[g * 1024 + tt * 128: g * 1024 + (tt + 1) * 128, o8 * 128:(o8 + 1) * 128], vtm[:])
                                VE.wait(GP)
                                TE.wait(VE)
                        else:
                            VE.op("tensor_scalar", u0[:, 0:512], pA[:, 0:512], SCALE_QK, None, OP.mult)
                            VE.op("tensor_scalar", u0[:, 512:1024], pB[:, 0:512], SCALE_QK, None, OP.mult)
                            SC.wait(VE)
                            SC.op("activation", u1[:], u0[:], AF.Square, bias=0.0, scale=1.0)
                            VE.wait(SC)
                            VE.op("tensor_copy", b0[:, 0:512], u1[:, 0:512])
                            VE.op("tensor_copy", b1[:, 0:512], u1[:, 512:1024])
                            TE.wait(VE)
                            for hh in range(2):
                                TE.op("matmul", pb[4][0:1, 0:512], onesb[hh * 64:hh * 64 + 64, 0:1], b0[hh * 64:hh * 64 + 64, 0:512], start=True, stop=True)
                                TE.op("matmul", pb[5][0:1, 0:512], onesb[hh * 64:hh * 64 + 64, 0:1], b1[hh * 64:hh * 64 + 64, 0:512], start=True, stop=True)
                                VE.wait(TE)
                                VE.op("tensor_copy", qaug[0:64, :], u0[hh * 64:hh * 64 + 64, :])
                                VE.wait(VE)
                                VE.op("tensor_copy", qaug[64:65, 0:512], pb[4][0:1, 0:512])
                                VE.op("tensor_copy", qaug[64:65, 512:1024], pb[5][0:1, 0:512])
                                VE.wait(VE)
                                TE.wait(VE)
                                h = o8 * 2 + hh
                                for tt in range(8):
                                    TE.op("matmul", pb[4 if tt % 2 == 0 else 5][:, 0:128], qaug[0:65, tt * 128:(tt + 1) * 128], omgT[0:65, :], start=True, stop=True)
                                    VE.wait(TE)
                                    VE.op("tensor_reduce", cc[0][:], pb[4 if tt % 2 == 0 else 5][:, 0:128], axis=AX, op=OP.max)
                                    VE.wait(VE)
                                    VE.op("tensor_scalar", cc[1][:], cc[0][:], -1.0, None, OP.mult)
                                    SC.wait(VE)
                                    SC.op("activation", u1[:, tt * 128:(tt + 1) * 128], pb[4 if tt % 2 == 0 else 5][:, 0:128], AF.Exp, bias=cc[1][:], scale=1.0)
                                    TE.wait(SC)
                                VE.wait(SC)
                                VE.op("tensor_scalar", u1[:], u1[:], float(1.0 / np.sqrt(128.0)), 1e-6, OP.mult, OP.add)
                                VE.wait(VE)
                                VE.wait(GP)
                                VE.op("tensor_copy", wtmpb[:], u1[:])
                                GP.wait(VE)
                                psc = kphi_sc if kind == "k" else qphi_sc
                                for tt in range(8):
                                    GP.dma(psc[h, g * 1024 + tt * 128: g * 1024 + (tt + 1) * 128, :], wtmpb[:, tt * 128:(tt + 1) * 128])
                                VE.wait(GP)
                                if kind == "k":
                                    for tt in range(8):
                                        SY.wait(TE, VE)
                                        SY.dma(vtm[:, 0:64], v_sc[g * 1024 + tt * 128: g * 1024 + (tt + 1) * 128, h * 64:(h + 1) * 64])
                                        VE.wait(SY)
                                        VE.op("tensor_copy", vext[:, 0:64], vtm[:, 0:64])
                                        VE.wait(VE)
                                        TE.wait(VE)
                                        TE.op("matmul", pb[4][:, 0:66], wtmpb[:, tt * 128:(tt + 1) * 128], vext[:], start=True, stop=True)
                                        VE.wait(TE)
                                        VE.op("tensor_tensor", Gtot[:, h * 66:(h + 1) * 66], Gtot[:, h * 66:(h + 1) * 66], pb[4][:, 0:66], OP.add)
                                        VE.wait(VE)
                                        TE.wait(VE)
            full_barrier()
            cx.free_pool(bb1p)
            GP.wait(VE)
            tap(GP, "qph0", qphi_sc[0])
            tap(GP, "kph0", kphi_sc[0])
            tap(GP, "qph7", qphi_sc[7])
            tap(GP, "vsc", v_sc)
            tap(GP, "gtotT", Gtot[:])

            GP.wait(VE)
            GP.dma(ccG_in, Gtot[:])
            GP.wait(GP)
            tokG = GP.coll("AllGather", OP.bypass,
                           replica_groups=[[0, 1, 2, 3], [4, 5, 6, 7]],
                           ins=[ccG_in.opt()], outs=[ccG_out.opt()])
            GP.wait_tok(tokG)
            VE.op("memset", Sext[:], 0.0)
            bb2p = []
            for r in range(4):
                GP.wait(VE, TE)
                GP.dma(wtmp[:], ccG_out[r * 128:(r + 1) * 128, 0:1024])
                GP.dma(t2[:, 0:32], ccG_out[r * 128:(r + 1) * 128, 1024:1056])
                VE.wait(GP)
                VE.op("scalar_tensor_tensor", Sext[:, 0:1024], wtmp[:], rankc[:, r:r + 1], Sext[:, 0:1024], OP.mult, OP.add)
                VE.op("scalar_tensor_tensor", Sext[:, 1024:1056], t2[:, 0:32], rankc[:, r:r + 1], Sext[:, 1024:1056], OP.mult, OP.add)
                VE.wait(VE)

            GP.wait(VE)
            tap(GP, "sextT", Sext[:])
            qpb = cx.sb("qpb", [128, 2 * 128], BF16, pool=bb2p)
            kpb = cx.sb("kpb", [128, 2 * 128], BF16, pool=bb2p)
            vtb = cx.sb("vtb", [128, 1024], BF16, pool=bb2p)
            woB = cx.sb("woB", [128, 8 * 1024], BF16, pool=bb2p)
            hfm = cx.sb("hfm", [128, 8 * 512], BF16, pool=bb2p)
            mid = cx.sb("mid", [128, 4 * 256], BF16, pool=bb2p)
            vexb = b2[:, 384:450]
            qpM = b2[:, 0:128]
            kpM = b2[:, 128:256]
            amb = b2[:, 256:384]
            w2t = [cx.sb(f"w2t{j}", [128, 1024], BF16, pool=bb2p) for j in range(2)]
            VE.op("memset", vexb, 0.0)
            VE.wait(VE)
            VE.op("memset", vexb[:, 64:65], 1.0)
            for r in range(8):
                SY.wait(VE, GP)
                SY.dma(woB[:, r * 1024:(r + 1) * 1024], bWo_sc[r * 128:(r + 1) * 128, :])

            for j in range(NT):
                SY.wait(TE, VE)
                SY.dma(qpb[:, 0:128], qphi_sc[0, j * 128:(j + 1) * 128, :])
                SY.dma(kpb[:, 0:128], kphi_sc[0, j * 128:(j + 1) * 128, :])
                SY.dma(vtb[:], v_sc[j * 128:(j + 1) * 128, :])
                TE.wait(SY)
                VE.wait(SY)
                for h in range(H):
                    hp = h % 2
                    if h + 1 < H:
                        SY.wait(TE)
                        SY.dma(qpb[:, (1 - hp) * 128:(2 - hp) * 128], qphi_sc[h + 1, j * 128:(j + 1) * 128, :])
                        SY.dma(kpb[:, (1 - hp) * 128:(2 - hp) * 128], kphi_sc[h + 1, j * 128:(j + 1) * 128, :])
                    TE.wait(SY)
                    TE.op("transpose", pf[0][:, 0:128], qpb[:, hp * 128:(hp + 1) * 128], idb[:])
                    TE.op("transpose", pf[1][:, 0:128], kpb[:, hp * 128:(hp + 1) * 128], idb[:])
                    VE.wait(TE)
                    VE.op("tensor_copy", qpM, pf[0][:, 0:128])
                    VE.op("tensor_copy", kpM, pf[1][:, 0:128])
                    VE.op("tensor_copy", vexb[:, 0:64], vtb[:, h * 64:(h + 1) * 64])
                    VE.op("tensor_copy", Sexb[:], Sext[:, h * 66:(h + 1) * 66])
                    VE.wait(VE)
                    TE.wait(VE)
                    TE.op("matmul", pb[2][:, 0:128], kpM, qpM, start=True, stop=True)
                    VE.wait(TE)
                    VE.op("tensor_tensor", amb, pb[2][:, 0:128], mg0i[:], OP.mult)
                    VE.wait(VE)
                    TE.wait(VE)
                    TE.op("matmul", pb[3][:, 0:66], amb, vexb, start=True, stop=False)
                    TE.op("matmul", pb[3][:, 0:66], qpM, Sexb[:], start=False, stop=True)
                    TE.op("matmul", pb[4][:, 0:66], kpb[:, hp * 128:(hp + 1) * 128], vexb, start=True, stop=True)
                    VE.wait(TE)
                    VE.op("tensor_scalar", cc[0][:], pb[3][:, 64:65], 1e-6, None, OP.add)
                    VE.op("tensor_tensor", Sext[:, h * 66:(h + 1) * 66], Sext[:, h * 66:(h + 1) * 66], pb[4][:, 0:66], OP.add)
                    VE.wait(VE)
                    VE.op("reciprocal", cc[1][:], cc[0][:])
                    VE.wait(VE)
                    VE.op("tensor_scalar", t0[:, 0:64], pb[3][:, 0:64], cc[1][:], None, OP.mult)
                    VE.wait(VE)
                    VE.op("tensor_copy", b0[:, 0:64], t0[:, 0:64])
                    TE.wait(VE)
                    TE.op("transpose", pf[0][0:64, 0:128], b0[:, 0:64], idb[:])
                    VE.wait(TE)
                    hb = (h % 2) * 64
                    VE.op("tensor_copy", b1[hb:hb + 64, 0:128], pf[0][0:64, 0:128])
                    TE.wait(VE)
                    rhsWo = woB[:, (h // 2) * 1024:(h // 2 + 1) * 1024][hb:hb + 64, :]
                    TE.op("matmul", pb[0][:, 0:512], b1[hb:hb + 64, 0:128], rhsWo[:, 0:512], start=(h == 0), stop=(h == H - 1))
                    TE.op("matmul", pb[1][:, 0:512], b1[hb:hb + 64, 0:128], rhsWo[:, 512:1024], start=(h == 0), stop=(h == H - 1))
                VE.wait(TE)
                xt = resid[:, j * 1024:(j + 1) * 1024]
                VE.op("tensor_scalar", u0[:], xt, stats1[:, 2 * j:2 * j + 1], stats1[:, 2 * j + 1:2 * j + 2], OP.mult, OP.add)
                VE.wait(VE)
                VE.op("tensor_tensor", u0[:, 0:512], u0[:, 0:512], pb[0][:, 0:512], OP.add)
                VE.op("tensor_tensor", u0[:, 512:1024], u0[:, 512:1024], pb[1][:, 0:512], OP.add)
                VE.wait(VE)
                TE.wait(VE)
                SC.wait(VE)
                SC.op("activation", t0[:, 0:512], u0[:, 0:512], AF.Square, bias=0.0, scale=1.0, accum_out=cc[0][:])
                SC.op("activation", t1[:, 0:512], u0[:, 512:1024], AF.Square, bias=0.0, scale=1.0, accum_out=cc[1][:])
                VE.op("tensor_reduce", cc[2][:], u0[:], axis=AX, op=OP.add)
                VE.wait(SC)
                VE.wait(VE)
                VE.op("tensor_tensor", cc[3][:], cc[0][:], cc[1][:], OP.add)
                VE.op("tensor_scalar", cc[2][:], cc[2][:], 1.0 / 1024.0, None, OP.mult)
                SC.wait(VE)
                SC.op("activation", cc[4][:], cc[2][:], AF.Square, bias=0.0, scale=1.0)
                VE.wait(SC)
                VE.op("tensor_scalar", cc[3][:], cc[3][:], 1.0 / 1024.0, None, OP.mult)
                VE.wait(VE)
                VE.op("tensor_tensor", cc[3][:], cc[3][:], cc[4][:], OP.subtract)
                VE.wait(VE)
                VE.op("tensor_scalar", cc[5][:], cc[3][:], 1e6, None, OP.add)
                SC.wait(VE)
                SC.op("activation", cc[4][:], cc[5][:], AF.Sqrt, bias=0.0, scale=1.0)
                VE.wait(SC)
                VE.op("reciprocal", cc[5][:], cc[4][:])
                VE.wait(VE)
                VE.op("tensor_tensor", cc[3][:], cc[2][:], cc[5][:], OP.mult)
                VE.wait(VE)
                VE.op("tensor_scalar", cc[3][:], cc[3][:], -1.0, None, OP.mult)
                VE.wait(VE)
                VE.op("tensor_scalar", u1[:], u0[:], cc[5][:], cc[3][:], OP.mult, OP.add)
                VE.wait(VE)
                VE.op("tensor_tensor", u1[:], u1[:], gamr[:], OP.mult)
                VE.wait(VE)
                VE.op("tensor_tensor", u1[:], u1[:], betr[:], OP.add)
                VE.wait(VE)
                VE.op("tensor_copy", wtmpb[:], u1[:])
                TE.wait(VE)
                for d in range(ND):
                    TE.op("transpose", pf[d % 2][:, 0:128], wtmpb[:, d * 128:(d + 1) * 128], idb[:])
                    VE.wait(TE)
                    VE.op("tensor_copy", hfm[:, d * 512 + (j % 4) * 128: d * 512 + (j % 4) * 128 + 128], pf[d % 2][:, 0:128])
                    TE.wait(VE)
                if j % 4 == 3:
                    gp512 = j // 4
                    for th in range(2):
                        tb = th * 256
                        for q4 in range(8):
                            for fc8 in range(4):
                                fch = q4 * 4 + fc8
                                wpar = fch % 2
                                SY.wait(TE)
                                SY.dma(w2t[wpar][:].rearrange("p (a q) -> p a q", q=128),
                                       bW1_sc[:, fch * 128:(fch + 1) * 128].rearrange("(a p) q -> p a q", p=128))
                                TE.wait(SY)
                                for d in range(ND):
                                    TE.op("matmul", pb[2][:, 0:256], w2t[wpar][:, d * 128:(d + 1) * 128], hfm[:, d * 512 + tb: d * 512 + tb + 256], start=(d == 0), stop=(d == ND - 1))
                                SC.wait(TE)
                                SC.op("activation", t0[:, 0:256], pb[2][:, 0:256], AF.Relu, bias=b1c[:, fch:fch + 1], scale=1.0)
                                VE.wait(SC)
                                VE.op("tensor_copy", mid[:, fc8 * 256:(fc8 + 1) * 256], t0[:, 0:256])
                                TE.wait(VE)
                            for fc8 in range(4):
                                fch = q4 * 4 + fc8
                                wpar = fch % 2
                                SY.wait(TE)
                                SY.dma(w2t[wpar][:], bW2_sc[fch * 128:(fch + 1) * 128, :])
                                TE.wait(SY)
                                for ts in range(2):
                                    pA = pb[3] if ts == 0 else pb[4]
                                    pB = pb[5] if ts == 0 else pf[0][:].bitcast(F32)
                                    TE.op("matmul", pA[:, 0:512],
                                          mid[:, fc8 * 256 + ts * 128: fc8 * 256 + (ts + 1) * 128], w2t[wpar][:, 0:512],
                                          start=(fch == 0), stop=(fch == 31))
                                    TE.op("matmul", pB[:, 0:512],
                                          mid[:, fc8 * 256 + ts * 128: fc8 * 256 + (ts + 1) * 128], w2t[wpar][:, 512:1024],
                                          start=(fch == 0), stop=(fch == 31))
                        VE.wait(TE)
                        pmap = [(pb[3], pb[5]), (pb[4], pf[0][:].bitcast(F32))]
                        for ts in range(2):
                            ti_ = gp512 * 4 + th * 2 + ts
                            r0 = resid[:, ti_ * 1024: ti_ * 1024 + 512]
                            r1 = resid[:, ti_ * 1024 + 512: (ti_ + 1) * 1024]
                            pA, pB = pmap[ts]
                            VE.op("tensor_tensor", t0[:, 0:512], pA[:, 0:512], b2r[:, 0:512], OP.add)
                            VE.op("tensor_tensor", t1[:, 0:512], pB[:, 0:512], b2r[:, 512:1024], OP.add)
                            VE.wait(VE)
                            VE.op("tensor_tensor", r0, r0, t0[:, 0:512], OP.add)
                            VE.op("tensor_tensor", r1, r1, t1[:, 0:512], OP.add)
                            VE.wait(VE)
                        TE.wait(VE)
            full_barrier()
            cx.free_pool(bb2p)
            cx.free_pool(bbP)
        if "ffn1" in PHASES or "ffn2" in PHASES:
            VE.op("memset", mgc[:], MAGIC)
            VE.op("memset", nmgc[:], -MAGIC)

            def dequant(src, nrows, ncols, trans_dst):
                nb = 32
                for r in range(nrows // 128):
                    for cs in range(ncols // 1024):
                        SY.wait(TE, VE, SC)
                        SY.dma(wtmp[:], src[r * 128:(r + 1) * 128, cs * 1024:(cs + 1) * 1024])
                        VE.wait(SY)
                        w3 = wtmp[:].rearrange("p (g k) -> p g k", g=nb)
                        VE.op("tensor_reduce", t0[:, 0:nb], w3, axis=AX, op=OP.max)
                        VE.op("tensor_reduce", t0[:, nb:2 * nb], w3, axis=AX, op=OP.min)
                        VE.wait(VE)
                        VE.op("tensor_scalar", t0[:, nb:2 * nb], t0[:, nb:2 * nb], -1.0, None, OP.mult)
                        VE.wait(VE)
                        VE.op("tensor_tensor", t0[:, 0:nb], t0[:, 0:nb], t0[:, nb:2 * nb], OP.max)
                        VE.wait(VE)
                        VE.op("tensor_scalar", t0[:, 0:nb], t0[:, 0:nb], 1.0 / 7.0, 1e-12, OP.mult, OP.max)
                        VE.wait(VE)
                        VE.op("reciprocal", t0[:, nb:2 * nb], t0[:, 0:nb])
                        VE.wait(VE)
                        s3d = t0[:, nb:2 * nb].unsqueeze(-1).broadcast_to([128, nb, 32])
                        VE.op("tensor_tensor", u0[:].rearrange("p (g k) -> p g k", g=nb), w3, s3d, OP.mult)
                        SC.wait(VE)
                        SC.op("activation", u0[:], u0[:], AF.Identity, bias=mgc[:], scale=1.0)
                        SC.wait(SC)
                        SC.op("activation", u0[:], u0[:], AF.Identity, bias=nmgc[:], scale=1.0)
                        VE.wait(SC)
                        s3e = t0[:, 0:nb].unsqueeze(-1).broadcast_to([128, nb, 32])
                        VE.op("tensor_tensor", u1[:].rearrange("p (g k) -> p g k", g=nb), u0[:].rearrange("p (g k) -> p g k", g=nb), s3e, OP.mult)
                        VE.wait(VE)
                        VE.op("tensor_copy", wtmpb[:], u1[:])
                        TE.wait(VE)
                        for dch in range(8):
                            TE.op("transpose", pf[dch % 2][:, 0:128], wtmpb[:, dch * 128:(dch + 1) * 128], idb[:])
                            VE.wait(TE)
                            VE.op("tensor_copy", b0[:, 0:128], pf[dch % 2][:, 0:128])
                            GP.wait(VE)
                            GP.dma(trans_dst[cs * 1024 + dch * 128: cs * 1024 + (dch + 1) * 128, r * 128:(r + 1) * 128], b0[:, 0:128])
                            VE.wait(GP)
                            TE.wait(VE)

            dequant(w_up, FQ, D, upT_sc)
            dequant(w_gate, FQ, D, gateT_sc)
            dequant(w_down, D, FQ, wdT_sc)
            full_barrier()

            def k1rep(dst, src11):
                VE.op("tensor_copy", b0[0:1, 0:1], src11)
                TE.wait(VE)
                TE.op("matmul", pb[5][:, 0:1], onesb[0:1, :], b0[0:1, 0:1], start=True, stop=True)
                VE.wait(TE)
                VE.op("tensor_copy", dst, pb[5][:, 0:1])
                TE.wait(VE)

            def cross_part_max(col, dst11):
                TE.wait(VE)
                TE.op("transpose", pb[5][0:1, 0:128], col, idf[:])
                VE.wait(TE)
                VE.op("tensor_reduce", dst11, pb[5][0:1, 0:128], axis=AX, op=OP.max)
                VE.wait(VE)
                TE.wait(VE)

            def ffn_emit(tag):
                ff = []
                ofm = cx.sb("ofm" + tag, [128, 24 * 512], BF16, pool=ff)
                wt = [cx.sb(f"wt{tag}{j}", [128, 1024], BF16, pool=ff) for j in range(4)]
                mxc = cx.sb("mxc" + tag, [128, 4], F32, pool=ff)
                scl = cx.sb("scl" + tag, [1, 16], F32, pool=ff)
                sclr = [cx.sb(f"sclr{tag}{j}", [128, 1], F32, pool=ff) for j in range(6)]
                ffx = []
                xqf = cx.sb("xqf" + tag, [128, 8 * 1024], BF16, pool=ffx)
                co = 0 if tag == "f1" else 4
                VE.op("memset", mxc[:], 0.0)
                for i in range(NT):
                    xt = resid[:, i * 1024:(i + 1) * 1024]
                    SC.wait(VE)
                    SC.op("activation", t0[:, 0:512], xt[:, 0:512], AF.Square, bias=0.0, scale=1.0, accum_out=cc[0][:])
                    SC.op("activation", t1[:, 0:512], xt[:, 512:1024], AF.Square, bias=0.0, scale=1.0, accum_out=cc[1][:])
                    VE.op("tensor_reduce", cc[2][:], xt, axis=AX, op=OP.add)
                    VE.wait(SC)
                    VE.wait(VE)
                    VE.op("tensor_tensor", cc[3][:], cc[0][:], cc[1][:], OP.add)
                    VE.op("tensor_scalar", cc[2][:], cc[2][:], 1.0 / 1024.0, None, OP.mult)
                    SC.wait(VE)
                    SC.op("activation", cc[4][:], cc[2][:], AF.Square, bias=0.0, scale=1.0)
                    VE.wait(SC)
                    VE.op("tensor_scalar", cc[3][:], cc[3][:], 1.0 / 1024.0, None, OP.mult)
                    VE.wait(VE)
                    VE.op("tensor_tensor", cc[3][:], cc[3][:], cc[4][:], OP.subtract)
                    VE.wait(VE)
                    VE.op("tensor_scalar", cc[5][:], cc[3][:], 1e-5, None, OP.add)
                    SC.wait(VE)
                    SC.op("activation", cc[4][:], cc[5][:], AF.Sqrt, bias=0.0, scale=1.0)
                    VE.wait(SC)
                    VE.op("reciprocal", stats1[:, 2 * i:2 * i + 1], cc[4][:])
                    VE.wait(VE)
                    VE.op("tensor_tensor", cc[4][:], cc[2][:], stats1[:, 2 * i:2 * i + 1], OP.mult)
                    VE.wait(VE)
                    VE.op("tensor_scalar", stats1[:, 2 * i + 1:2 * i + 2], cc[4][:], -1.0, None, OP.mult)
                    VE.op("tensor_scalar", u0[:], xt, stats1[:, 2 * i:2 * i + 1], stats1[:, 2 * i + 1:2 * i + 2], OP.mult, OP.add)
                    VE.wait(VE)
                    VE.op("tensor_reduce", cc[0][:], u0[:], axis=AX, op=OP.max)
                    VE.op("tensor_reduce", cc[1][:], u0[:], axis=AX, op=OP.min)
                    VE.wait(VE)
                    VE.op("tensor_scalar", cc[1][:], cc[1][:], -1.0, None, OP.mult)
                    VE.wait(VE)
                    VE.op("tensor_tensor", cc[0][:], cc[0][:], cc[1][:], OP.max)
                    VE.wait(VE)
                    VE.op("tensor_tensor", mxc[:, 0:1], mxc[:, 0:1], cc[0][:], OP.max)
                    VE.wait(VE)
                cross_part_max(mxc[:, 0:1], scl[0:1, 0:1])
                GP.wait(VE)
                GP.dma(cc_mx_in[0:1, 0:1], scl[0:1, 0:1])
                GP.wait(GP)
                tokM = GP.coll("AllReduce", OP.max, replica_groups=[list(range(8))],
                               ins=[cc_mx_in.opt()], outs=[cc_mx_out[co].opt()])
                GP.wait_tok(tokM)
                GP.dma(scl[0:1, 0:1], cc_mx_out[co][0:1, 0:1])
                VE.wait(GP)
                VE.op("tensor_scalar", scl[0:1, 1:2], scl[0:1, 0:1], 1.0 / 32767.0, 1e-12, OP.mult, OP.max)
                VE.wait(VE)
                VE.op("reciprocal", scl[0:1, 2:3], scl[0:1, 1:2])
                VE.wait(VE)
                k1rep(sclr[0][:], scl[0:1, 2:3])

                VE.op("memset", mxc[:, 1:3], 0.0)
                for g in range(4):
                    for ii in range(8):
                        i = g * 8 + ii
                        xt = resid[:, i * 1024:(i + 1) * 1024]
                        VE.wait(TE)
                        VE.op("tensor_scalar", u0[:], xt, stats1[:, 2 * i:2 * i + 1], stats1[:, 2 * i + 1:2 * i + 2], OP.mult, OP.add)
                        SC.wait(VE)
                        SC.op("activation", u1[:], u0[:], AF.Identity, bias=mgc[:], scale=sclr[0][:])
                        SC.wait(SC)
                        SC.op("activation", u1[:], u1[:], AF.Identity, bias=nmgc[:], scale=1.0)
                        VE.wait(SC)
                        VE.op("tensor_copy", wtmpb[:], u1[:])
                        TE.wait(VE)
                        for d in range(ND):
                            TE.op("transpose", pf[d % 2][:, 0:128], wtmpb[:, d * 128:(d + 1) * 128], idb[:])
                            VE.wait(TE)
                            VE.op("tensor_copy", xqf[:, d * 1024 + ii * 128: d * 1024 + ii * 128 + 128], pf[d % 2][:, 0:128])
                            TE.wait(VE)
                    for ht in range(NF):
                        pu = [pb[0], pb[1]] if ht % 2 == 0 else [pb[4], pb[5]]
                        pg_ = [pb[2], pb[3]] if ht % 2 == 0 else [pf[0][:].bitcast(F32), pf[1][:].bitcast(F32)]
                        wpar = ht % 2
                        SY.wait(TE)
                        SY.dma(wt[wpar][:].rearrange("p (a q) -> p a q", q=128),
                               upT_sc[:, ht * 128:(ht + 1) * 128].rearrange("(a p) q -> p a q", p=128))
                        SY.dma(wt[2 + wpar][:].rearrange("p (a q) -> p a q", q=128),
                               gateT_sc[:, ht * 128:(ht + 1) * 128].rearrange("(a p) q -> p a q", p=128))
                        TE.wait(SY)
                        for d in range(ND):
                            TE.op("matmul", pu[0][:, 0:512], wt[wpar][:, d * 128:(d + 1) * 128], xqf[:, d * 1024:d * 1024 + 512], start=(d == 0), stop=(d == ND - 1))
                            TE.op("matmul", pu[1][:, 0:512], wt[wpar][:, d * 128:(d + 1) * 128], xqf[:, d * 1024 + 512:d * 1024 + 1024], start=(d == 0), stop=(d == ND - 1))
                        for d in range(ND):
                            TE.op("matmul", pg_[0][:, 0:512], wt[2 + wpar][:, d * 128:(d + 1) * 128], xqf[:, d * 1024:d * 1024 + 512], start=(d == 0), stop=(d == ND - 1))
                            TE.op("matmul", pg_[1][:, 0:512], wt[2 + wpar][:, d * 128:(d + 1) * 128], xqf[:, d * 1024 + 512:d * 1024 + 1024], start=(d == 0), stop=(d == ND - 1))
                        VE.wait(TE)
                        VE.wait(GP)
                        VE.op("tensor_copy", u0[:, 0:512], pu[0][:, 0:512])
                        VE.op("tensor_copy", u0[:, 512:1024], pu[1][:, 0:512])
                        VE.op("tensor_copy", u1[:, 0:512], pg_[0][:, 0:512])
                        VE.op("tensor_copy", u1[:, 512:1024], pg_[1][:, 0:512])
                        VE.wait(VE)
                        GP.wait(VE)
                        GP.dma(up_sc[ht * 128:(ht + 1) * 128, g * 1024:(g + 1) * 1024], u0[:])
                        GP.dma(gate_sc[ht * 128:(ht + 1) * 128, g * 1024:(g + 1) * 1024], u1[:])
                        for (src, cix) in ((u0, 1), (u1, 2)):
                            VE.op("tensor_reduce", cc[0][:], src[:], axis=AX, op=OP.max)
                            VE.op("tensor_reduce", cc[1][:], src[:], axis=AX, op=OP.min)
                            VE.wait(VE)
                            VE.op("tensor_scalar", cc[1][:], cc[1][:], -1.0, None, OP.mult)
                            VE.wait(VE)
                            VE.op("tensor_tensor", cc[0][:], cc[0][:], cc[1][:], OP.max)
                            VE.wait(VE)
                            VE.op("tensor_tensor", mxc[:, cix:cix + 1], mxc[:, cix:cix + 1], cc[0][:], OP.max)
                            VE.wait(VE)
                cross_part_max(mxc[:, 1:2], scl[0:1, 3:4])
                cross_part_max(mxc[:, 2:3], scl[0:1, 4:5])
                GP.wait(VE)
                GP.dma(cc_mx_in[0:1, 0:2], scl[0:1, 3:5])
                GP.wait(GP)
                tokU = GP.coll("AllReduce", OP.max, replica_groups=[list(range(8))],
                               ins=[cc_mx_in.opt()], outs=[cc_mx_out[co + 1].opt()])
                GP.wait_tok(tokU)
                GP.dma(scl[0:1, 3:5], cc_mx_out[co + 1][0:1, 0:2])
                VE.wait(GP)
                VE.op("tensor_scalar", scl[0:1, 5:7], scl[0:1, 3:5], scl[0:1, 1:2], None, OP.mult)
                VE.wait(VE)
                VE.op("tensor_scalar", scl[0:1, 5:7], scl[0:1, 5:7], 1.0 / 32767.0, 1e-12, OP.mult, OP.max)
                VE.wait(VE)
                VE.op("reciprocal", scl[0:1, 7:9], scl[0:1, 5:7])
                VE.wait(VE)
                VE.op("tensor_scalar", scl[0:1, 9:11], scl[0:1, 7:9], scl[0:1, 1:2], None, OP.mult)
                VE.wait(VE)
                k1rep(sclr[1][:], scl[0:1, 9:10])
                k1rep(sclr[2][:], scl[0:1, 10:11])
                k1rep(sclr[3][:], scl[0:1, 6:7])

                full_barrier()
                cx.free_pool(ffx)
                ua = [cx.sb(f"ua{tag}{j}", [128, 1024], F32, pool=ff) for j in range(2)]
                ub_ = [cx.sb(f"ub{tag}{j}", [128, 1024], F32, pool=ff) for j in range(2)]
                VE.op("memset", mxc[:, 3:4], 0.0)
                iters = [(ht, g) for ht in range(NF) for g in range(4)]
                def c1_load(ix):
                    ht, g = iters[ix]
                    p = ix % 2
                    SY.dma(ub_[p][:], gate_sc[ht * 128:(ht + 1) * 128, g * 1024:(g + 1) * 1024])
                def c1_sc(ix):
                    p = ix % 2
                    SC.wait(SY) if False else None
                    SC.op("activation", ua[p][:], ub_[p][:], AF.Identity, bias=mgc[:], scale=sclr[2][:])
                    SC.wait(SC)
                    SC.op("activation", ua[p][:], ua[p][:], AF.Identity, bias=nmgc[:], scale=1.0)
                    SC.wait(SC)
                    SC.op("activation", ub_[p][:], ua[p][:], AF.Sigmoid, bias=0.0, scale=sclr[3][:])
                    SC.wait(SC)
                    SC.op("activation", ub_[p][:], ub_[p][:], AF.Identity, bias=mgc[:], scale=65536.0)
                    SC.wait(SC)
                    SC.op("activation", ub_[p][:], ub_[p][:], AF.Identity, bias=nmgc[:], scale=1.0)
                def c1_ve(ix):
                    p = ix % 2
                    VE.op("tensor_scalar", ub_[p][:], ub_[p][:], 65535.0, None, OP.min)
                    VE.wait(VE)
                    VE.op("tensor_tensor", ua[p][:], ua[p][:], ub_[p][:], OP.mult)
                    VE.wait(VE)
                    VE.op("tensor_reduce", cc[0][:], ua[p][:], axis=AX, op=OP.max)
                    VE.op("tensor_reduce", cc[1][:], ua[p][:], axis=AX, op=OP.min)
                    VE.wait(VE)
                    VE.op("tensor_tensor", mxc[:, 3:4], mxc[:, 3:4], cc[0][:], OP.max)
                    VE.op("scalar_tensor_tensor", mxc[:, 3:4], cc[1][:], -1.0, mxc[:, 3:4], OP.mult, OP.max)
                    VE.wait(VE)
                c1_load(0)
                for ix in range(len(iters)):
                    if ix + 1 < len(iters):
                        SY.wait(SC)
                        c1_load(ix + 1)
                    SC.wait(SY)
                    SC.wait(VE)
                    c1_sc(ix)
                    VE.wait(SC)
                    c1_ve(ix)
                cross_part_max(mxc[:, 3:4], scl[0:1, 11:12])
                GP.wait(VE)
                GP.dma(cc_mx_in[0:1, 0:1], scl[0:1, 11:12])
                GP.wait(GP)
                tok3 = GP.coll("AllReduce", OP.max, replica_groups=[list(range(8))],
                               ins=[cc_mx_in.opt()], outs=[cc_mx_out[co + 2].opt()])
                GP.wait_tok(tok3)
                GP.dma(scl[0:1, 11:12], cc_mx_out[co + 2][0:1, 0:1])
                VE.wait(GP)
                VE.op("tensor_tensor", scl[0:1, 12:13], scl[0:1, 11:12], scl[0:1, 6:7], OP.mult)
                VE.wait(VE)
                VE.op("tensor_scalar", scl[0:1, 12:13], scl[0:1, 12:13], 1.0 / 65536.0 / 32767.0, 1e-12, OP.mult, OP.max)
                VE.wait(VE)
                VE.op("reciprocal", scl[0:1, 13:14], scl[0:1, 12:13])
                VE.wait(VE)
                VE.op("tensor_tensor", scl[0:1, 13:14], scl[0:1, 13:14], scl[0:1, 6:7], OP.mult)
                VE.wait(VE)
                VE.op("tensor_scalar", scl[0:1, 13:14], scl[0:1, 13:14], 1.0 / 65536.0, None, OP.mult)
                VE.wait(VE)
                k1rep(sclr[4][:], scl[0:1, 13:14])

                VE.op("memset", mxc[:, 0:1], 0.0)
                for ht in range(NF):
                    for g in range(4):
                        SY.wait(VE, SC, GP)
                        SY.dma(u1[:], gate_sc[ht * 128:(ht + 1) * 128, g * 1024:(g + 1) * 1024])
                        SY.dma(t0[:], up_sc[ht * 128:(ht + 1) * 128, g * 1024:g * 1024 + 512])
                        SY.dma(t1[:], up_sc[ht * 128:(ht + 1) * 128, g * 1024 + 512:(g + 1) * 1024])
                        SC.wait(SY)
                        SC.op("activation", u0[:], u1[:], AF.Identity, bias=mgc[:], scale=sclr[2][:])
                        SC.wait(SC)
                        SC.op("activation", u0[:], u0[:], AF.Identity, bias=nmgc[:], scale=1.0)
                        SC.wait(SC)
                        SC.op("activation", u1[:], u0[:], AF.Sigmoid, bias=0.0, scale=sclr[3][:])
                        SC.wait(SC)
                        SC.op("activation", u1[:], u1[:], AF.Identity, bias=mgc[:], scale=65536.0)
                        SC.wait(SC)
                        SC.op("activation", u1[:], u1[:], AF.Identity, bias=nmgc[:], scale=1.0)
                        VE.wait(SC)
                        VE.op("tensor_scalar", u1[:], u1[:], 65535.0, None, OP.min)
                        VE.wait(VE)
                        VE.op("tensor_tensor", u0[:], u0[:], u1[:], OP.mult)
                        SC.wait(VE)
                        SC.op("activation", u0[:], u0[:], AF.Identity, bias=mgc[:], scale=sclr[4][:])
                        SC.wait(SC)
                        SC.op("activation", u0[:], u0[:], AF.Identity, bias=nmgc[:], scale=1.0)
                        SC.op("activation", u1[:, 0:512], t0[:], AF.Identity, bias=mgc[:], scale=sclr[1][:])
                        SC.op("activation", u1[:, 512:1024], t1[:], AF.Identity, bias=mgc[:], scale=sclr[1][:])
                        SC.wait(SC)
                        SC.op("activation", u1[:], u1[:], AF.Identity, bias=nmgc[:], scale=1.0)
                        VE.wait(SC)
                        VE.op("tensor_tensor", u0[:], u0[:], u1[:], OP.mult)
                        VE.wait(VE)
                        VE.op("tensor_reduce", cc[0][:], u0[:], axis=AX, op=OP.max)
                        VE.op("tensor_reduce", cc[1][:], u0[:], axis=AX, op=OP.min)
                        VE.wait(VE)
                        VE.op("tensor_scalar", cc[1][:], cc[1][:], -1.0, None, OP.mult)
                        VE.wait(VE)
                        VE.op("tensor_tensor", cc[0][:], cc[0][:], cc[1][:], OP.max)
                        VE.wait(VE)
                        VE.op("tensor_tensor", mxc[:, 0:1], mxc[:, 0:1], cc[0][:], OP.max)
                        VE.wait(VE)
                cross_part_max(mxc[:, 0:1], scl[0:1, 14:15])
                GP.wait(VE)
                GP.dma(cc_mx_in[0:1, 0:1], scl[0:1, 14:15])
                GP.wait(GP)
                tok4 = GP.coll("AllReduce", OP.max, replica_groups=[list(range(8))],
                               ins=[cc_mx_in.opt()], outs=[cc_mx_out[co + 3].opt()])
                GP.wait_tok(tok4)
                GP.dma(scl[0:1, 14:15], cc_mx_out[co + 3][0:1, 0:1])
                VE.wait(GP)
                VE.op("tensor_tensor", scl[0:1, 15:16], scl[0:1, 12:13], scl[0:1, 5:6], OP.mult)
                VE.wait(VE)
                VE.op("tensor_tensor", scl[0:1, 14:15], scl[0:1, 14:15], scl[0:1, 15:16], OP.mult)
                VE.wait(VE)
                VE.op("tensor_scalar", scl[0:1, 14:15], scl[0:1, 14:15], 1.0 / 32767.0, 1e-12, OP.mult, OP.max)
                VE.wait(VE)
                VE.op("reciprocal", scl[0:1, 13:14], scl[0:1, 14:15])
                VE.wait(VE)
                VE.op("tensor_tensor", scl[0:1, 13:14], scl[0:1, 13:14], scl[0:1, 15:16], OP.mult)
                VE.wait(VE)
                k1rep(sclr[5][:], scl[0:1, 13:14])
                k1rep(sclr[0][:], scl[0:1, 14:15])

                if tag == "f1":
                    GP.wait(VE)
                    tap(GP, "sclT", scl[0:1, :])
                    tap(GP, "upsp", up_sc[0:1024, 0:64])
                    tap(GP, "gtsp", gate_sc[0:1024, 0:64])
                    tap(GP, "uqT", upT_sc[0:256, 0:256])
                    tap(GP, "wdT", wdT_sc[0:256, 0:256])
                    tap(GP, "statsF", stats1[:])
                for gp_ in range(8):
                    for ht in range(NF):
                        SY.wait(VE, SC, GP, TE)
                        SY.dma(u1[:, 0:512], gate_sc[ht * 128:(ht + 1) * 128, gp_ * 512:(gp_ + 1) * 512])
                        SY.dma(t0[:], up_sc[ht * 128:(ht + 1) * 128, gp_ * 512:(gp_ + 1) * 512])
                        SC.wait(SY)
                        SC.op("activation", u0[:, 0:512], u1[:, 0:512], AF.Identity, bias=mgc[:], scale=sclr[2][:])
                        SC.wait(SC)
                        SC.op("activation", u0[:, 0:512], u0[:, 0:512], AF.Identity, bias=nmgc[:], scale=1.0)
                        SC.wait(SC)
                        SC.op("activation", u1[:, 0:512], u0[:, 0:512], AF.Sigmoid, bias=0.0, scale=sclr[3][:])
                        SC.wait(SC)
                        SC.op("activation", u1[:, 0:512], u1[:, 0:512], AF.Identity, bias=mgc[:], scale=65536.0)
                        SC.wait(SC)
                        SC.op("activation", u1[:, 0:512], u1[:, 0:512], AF.Identity, bias=nmgc[:], scale=1.0)
                        VE.wait(SC)
                        VE.op("tensor_scalar", u1[:, 0:512], u1[:, 0:512], 65535.0, None, OP.min)
                        VE.wait(VE)
                        VE.op("tensor_tensor", u0[:, 0:512], u0[:, 0:512], u1[:, 0:512], OP.mult)
                        SC.wait(VE)
                        SC.op("activation", u0[:, 0:512], u0[:, 0:512], AF.Identity, bias=mgc[:], scale=sclr[4][:])
                        SC.wait(SC)
                        SC.op("activation", u0[:, 0:512], u0[:, 0:512], AF.Identity, bias=nmgc[:], scale=1.0)
                        SC.op("activation", u1[:, 0:512], t0[:], AF.Identity, bias=mgc[:], scale=sclr[1][:])
                        SC.wait(SC)
                        SC.op("activation", u1[:, 0:512], u1[:, 0:512], AF.Identity, bias=nmgc[:], scale=1.0)
                        VE.wait(SC)
                        VE.op("tensor_tensor", u0[:, 0:512], u0[:, 0:512], u1[:, 0:512], OP.mult)
                        SC.wait(VE)
                        SC.op("activation", u0[:, 0:512], u0[:, 0:512], AF.Identity, bias=mgc[:], scale=sclr[5][:])
                        SC.wait(SC)
                        SC.op("activation", u0[:, 0:512], u0[:, 0:512], AF.Identity, bias=nmgc[:], scale=1.0)
                        VE.wait(SC)
                        VE.op("tensor_copy", ofm[:, ht * 512:(ht + 1) * 512], u0[:, 0:512])
                        VE.wait(VE)
                    TE.wait(VE)
                    for h in range(NF):
                        wpar = h % 2
                        SY.wait(TE)
                        SY.dma(wt[wpar][:], wdT_sc[h * 128:(h + 1) * 128, :])
                        TE.wait(SY)
                        for ts in range(4):
                            pA = pb[ts]
                            pB = pb[4 + ts] if ts < 2 else pf[ts - 2][:].bitcast(F32)
                            TE.op("matmul", pA[:, 0:512],
                                  ofm[:, h * 512 + ts * 128: h * 512 + (ts + 1) * 128], wt[wpar][:, 0:512],
                                  start=(h == 0), stop=(h == NF - 1))
                            TE.op("matmul", pB[:, 0:512],
                                  ofm[:, h * 512 + ts * 128: h * 512 + (ts + 1) * 128], wt[wpar][:, 512:1024],
                                  start=(h == 0), stop=(h == NF - 1))
                    VE.wait(TE)
                    base = gp_ * 4
                    for ts in range(4):
                        i = base + ts
                        r0 = resid[:, i * 1024: i * 1024 + 512]
                        r1 = resid[:, i * 1024 + 512: (i + 1) * 1024]
                        pA = pb[ts]
                        pB = pb[4 + ts] if ts < 2 else pf[ts - 2][:].bitcast(F32)
                        VE.op("scalar_tensor_tensor", r0, pA[:, 0:512], sclr[0][:], r0, OP.mult, OP.add)
                        VE.op("scalar_tensor_tensor", r1, pB[:, 0:512], sclr[0][:], r1, OP.mult, OP.add)
                        VE.wait(VE)
                    TE.wait(VE)
                full_barrier()
                cx.free_pool(ff)

            if "ffn1" in PHASES:
                ffn_emit("f1")
            if "attn_b" in PHASES:
                attn_b_emit()
            if "ffn2" in PHASES:
                ffn_emit("f2")
        elif "attn_b" in PHASES:
            attn_b_emit()

        SY.wait(VE, SC, TE, GP)
        for i in range(NT):
            ins = nc.sync.dma_start(out=outt[i], in_=resid[:, i * 1024:(i + 1) * 1024])
            ins.then_inc(souts.h, 16)
            souts.n += 16
        nc.sync.wait_ge(souts.h, souts.n)

    cx.close()
    return nc


_NC_CACHE = {}


def kernel(L0_attn_a_x, L0_attn_a_Wk, L0_attn_a_Wv, L0_attn_a_Wq, L0_attn_a_fast_alpha,
           L0_attn_a_slow_alpha, L0_attn_a_nm_w1, L0_attn_a_nm_b1, L0_attn_a_nm_w2,
           L0_attn_a_nm_b2, L0_attn_a_ir_w1, L0_attn_a_ir_b1, L0_attn_a_ir_w2,
           L0_attn_a_ir_b2, L0_attn_a_pw, L0_attn_a_Wo, L0_attn_a_norm_w,
           L0_attn_b_Wq, L0_attn_b_Wk, L0_attn_b_Wv, L0_attn_b_Wo, L0_attn_b_omega,
           L0_attn_b_W1, L0_attn_b_b1, L0_attn_b_W2, L0_attn_b_b2, L0_attn_b_gamma,
           L0_attn_b_beta, L0_ffn_w_gate, L0_ffn_w_up, L0_ffn_w_down):
    key = (PHASES, tuple(sorted(DEBUG_TAP.keys())))
    if key not in _NC_CACHE:
        _NC_CACHE[key] = build()
    nc = _NC_CACHE[key]

    x = np.ascontiguousarray(np.asarray(L0_attn_a_x, np.float32).reshape(B * S, D))
    r2 = lambda a, shp: np.ascontiguousarray(np.asarray(a, np.float32).reshape(shp))
    in_maps = []
    for c in range(NCORES):
        im = {
            "xs": x[c * TPC:(c + 1) * TPC],
            "aWk": r2(L0_attn_a_Wk, (64, D)), "aWv": r2(L0_attn_a_Wv, (64, D)),
            "aWq": r2(L0_attn_a_Wq, (64, D)),
            "fa": r2(L0_attn_a_fast_alpha, (1, 1)), "sa": r2(L0_attn_a_slow_alpha, (1, 1)),
            "nm_w1": r2(L0_attn_a_nm_w1, (128, 64)), "nm_b1": r2(L0_attn_a_nm_b1, (1, 128)),
            "nm_w2": r2(L0_attn_a_nm_w2, (64, 128)), "nm_b2": r2(L0_attn_a_nm_b2, (1, 64)),
            "ir_w1": r2(L0_attn_a_ir_w1, (64, D)), "ir_b1": r2(L0_attn_a_ir_b1, (1, 64)),
            "ir_w2": r2(L0_attn_a_ir_w2, (1, 64)), "ir_b2": r2(L0_attn_a_ir_b2, (1, 1)),
            "pw": r2(L0_attn_a_pw, (3, D)), "aWo": r2(L0_attn_a_Wo, (D, 64)),
            "norm_w": r2(L0_attn_a_norm_w, (1, 64)),
            "rankm": np.asarray([[1.0 if jj < (c % 4) else 0.0 for jj in range(8)]], np.float32),
        }
        if "attn_b" in PHASES:
            im.update({
                "bWq": r2(L0_attn_b_Wq, (D, D)), "bWk": r2(L0_attn_b_Wk, (D, D)),
                "bWv": r2(L0_attn_b_Wv, (D, D)), "bWo": r2(L0_attn_b_Wo, (D, D)),
                "omega": r2(L0_attn_b_omega, (MPH, 64)),
                "bW1": r2(L0_attn_b_W1, (D, MQ)), "bb1": r2(L0_attn_b_b1, (1, MQ)),
                "bW2": r2(L0_attn_b_W2, (MQ, D)), "bb2": r2(L0_attn_b_b2, (1, D)),
                "gamma": r2(L0_attn_b_gamma, (1, D)), "beta": r2(L0_attn_b_beta, (1, D)),
            })
        if "ffn1" in PHASES or "ffn2" in PHASES:
            im.update({
                "w_gate": r2(L0_ffn_w_gate, (FQ, D)), "w_up": r2(L0_ffn_w_up, (FQ, D)),
                "w_down": r2(L0_ffn_w_down, (D, FQ)),
            })
        in_maps.append(im)

    res = run_bass_kernel_spmd(nc, in_maps, core_ids=list(range(NCORES)))
    outp = np.concatenate([np.asarray(res.results[c]["out"]) for c in range(NCORES)], axis=0)
    if DEBUG_TAP:
        return outp.reshape(B, S, D), [
            {nm: np.asarray(res.results[c]["tp_" + nm]) for nm in DEBUG_TAP} for c in range(NCORES)]
    return outp.reshape(B, S, D)


# version 05
# speedup vs baseline: 1.2725x; 1.1165x over previous
import numpy as np
import concourse.bass as bass
import concourse.mybir as mybir
from concourse.mybir import AxisListType
from concourse.alu_op_type import AluOpType as OP
from concourse.bass_utils import run_bass_kernel_spmd

F32 = mybir.dt.float32
BF16 = mybir.dt.bfloat16
I32 = mybir.dt.int32
AF = mybir.ActivationFunctionType
AX = AxisListType.X

NCORES = 8
B, S, D = 2, 16384, 1024
TPC = S * B // NCORES
NT = TPC // 128
ND = D // 128
MAGIC = 12582912.0
LN7_3 = float(np.log(7.0) - np.log(3.0))
LN09 = float(np.log(0.9))
FQ = 3072
NF = FQ // 128
MQ = 4096
H = 16
MPH = 128

PHASES = ("attn_a", "ffn1", "attn_b", "ffn2")
DEBUG_TAP = {}


class _Sem:
    __slots__ = ("h", "n")

    def __init__(self, h):
        self.h = h
        self.n = 0


class Eng:

    def __init__(self, cx, name, raw, dma_step=1):
        self._cx = cx
        self.name = name
        self.raw = raw
        self.sem = cx.mksem("S" + name)
        self._pend = None
        self._step = dma_step
        self._seen = {}

    def op(self, meth, *a, **kw):
        ins = getattr(self.raw, meth)(*a, **kw)
        self._pend = ins
        return ins

    def dma(self, out, in_):
        ins = self.raw.dma_start(out=out, in_=in_)
        ins.then_inc(self.sem.h, 16)
        self.sem.n += 16
        self._pend = None
        return ins

    def seal(self):
        if self._pend is not None:
            self._pend.then_inc(self.sem.h, 1)
            self.sem.n += 1
            self._pend = None
        return (self.sem, self.sem.n)

    def wait(self, *others):
        for o in others:
            s, n = o.seal()
            if n > 0 and self._seen.get(id(s), -1) < n:
                self.raw.wait_ge(s.h, n)
                self._seen[id(s)] = n

    def wait_tok(self, tok):
        s, n = tok
        self.raw.wait_ge(s.h, n)

    def coll(self, *a, **kw):
        s = self._cx.mksem(f"Scc{self._cx.ncc}")
        self._cx.ncc += 1
        ins = self.raw.collective_compute(*a, **kw)
        ins.then_inc(s.h, 1)
        return (s, 1)


class Ctx:
    def __init__(self):
        self.nc = bass.Bass("TRN2", target_bir_lowering=False, debug=False)
        self._mgrs = []
        self.ncc = 0

    def mksem(self, name):
        m = self.nc.semaphore(name)
        h = m.__enter__()
        self._mgrs.append(m)
        return _Sem(h)

    def sb(self, name, shape, dt, pool=None):
        m = self.nc.sbuf_tensor(name, shape, dt)
        t = m.__enter__()
        if pool is not None:
            pool.append(m)
        else:
            self._mgrs.append(m)
        return t

    def free_pool(self, pool):
        for m in reversed(pool):
            m.__exit__(None, None, None)
        pool.clear()

    def ps(self, name, shape, dt):
        m = self.nc.psum_tensor(name, shape, dt)
        t = m.__enter__()
        self._mgrs.append(m)
        return t

    def close(self):
        for m in reversed(self._mgrs):
            m.__exit__(None, None, None)


def build():
    cx = Ctx()
    nc = cx.nc

    def din(name, shape):
        return nc.dram_tensor(name, shape, F32, kind="ExternalInput").ap()

    xs = din("xs", [TPC, D])
    aWk = din("aWk", [64, D]); aWv = din("aWv", [64, D]); aWq = din("aWq", [64, D])
    fa_in = din("fa", [1, 1]); sa_in = din("sa", [1, 1])
    nm_w1 = din("nm_w1", [128, 64]); nm_b1 = din("nm_b1", [1, 128])
    nm_w2 = din("nm_w2", [64, 128]); nm_b2 = din("nm_b2", [1, 64])
    ir_w1 = din("ir_w1", [64, D]); ir_b1 = din("ir_b1", [1, 64])
    ir_w2 = din("ir_w2", [1, 64]); ir_b2 = din("ir_b2", [1, 1])
    pw = din("pw", [3, D]); aWo = din("aWo", [D, 64]); norm_w = din("norm_w", [1, 64])
    rankm = din("rankm", [1, 8])
    if "attn_b" in PHASES:
        bWq = din("bWq", [D, D]); bWk = din("bWk", [D, D]); bWv = din("bWv", [D, D])
        bWo = din("bWo", [D, D]); omega = din("omega", [MPH, 64])
        bW1 = din("bW1", [D, MQ]); bb1 = din("bb1", [1, MQ])
        bW2 = din("bW2", [MQ, D]); bb2 = din("bb2", [1, D])
        gam = din("gamma", [1, D]); bet = din("beta", [1, D])
    if "ffn1" in PHASES or "ffn2" in PHASES:
        w_gate = din("w_gate", [FQ, D]); w_up = din("w_up", [FQ, D]); w_down = din("w_down", [D, FQ])

    out = nc.dram_tensor("out", [TPC, D], F32, kind="ExternalOutput").ap()
    outt = out.rearrange("(n p) d -> n p d", p=128)

    taps = {nm: nc.dram_tensor("tp_" + nm, shape, F32 if dt == "f32" else BF16, kind="ExternalOutput").ap()
            for nm, (shape, dt) in DEBUG_TAP.items()}

    def tap(GPeng, name, src):
        if name in taps:
            GPeng.dma(taps[name], src)

    spillA = nc.dram_tensor("spillA", [NT * 2, 128, 192], BF16).ap()
    spillQ = nc.dram_tensor("spillQ", [NT, 128, 128], BF16).ap()
    ccA_in = nc.dram_tensor("ccA_in", [64, 256], F32).ap()
    ccA_out = nc.dram_tensor("ccA_out", [256, 256], F32).ap()
    ccN_in = nc.dram_tensor("ccN_in", [1, 66], F32).ap()
    ccN_out = nc.dram_tensor("ccN_out", [4, 66], F32).ap()
    if "attn_b" in PHASES:
        qphi_sc = nc.dram_tensor("qphi_sc", [H, TPC, MPH], BF16).ap()
        kphi_sc = nc.dram_tensor("kphi_sc", [H, TPC, MPH], BF16).ap()
        v_sc = nc.dram_tensor("v_sc", [TPC, D], BF16).ap()
        bWq_sc = nc.dram_tensor("bWq_sc", [D, D], BF16).ap()
        bWk_sc = nc.dram_tensor("bWk_sc", [D, D], BF16).ap()
        bWv_sc = nc.dram_tensor("bWv_sc", [D, D], BF16).ap()
        bWo_sc = nc.dram_tensor("bWo_sc", [D, D], BF16).ap()
        bW1_sc = nc.dram_tensor("bW1_sc", [D, MQ], BF16).ap()
        bW2_sc = nc.dram_tensor("bW2_sc", [MQ, D], BF16).ap()
        ccG_in = nc.dram_tensor("ccG_in", [128, H * 66], F32).ap()
        ccG_out = nc.dram_tensor("ccG_out", [512, H * 66], F32).ap()
    if "ffn1" in PHASES or "ffn2" in PHASES:
        upT_sc = nc.dram_tensor("upT_sc", [D, FQ], BF16).ap()
        gateT_sc = nc.dram_tensor("gateT_sc", [D, FQ], BF16).ap()
        wdT_sc = nc.dram_tensor("wdT_sc", [FQ, D], BF16).ap()
        up_sc = nc.dram_tensor("up_sc", [FQ, TPC], F32).ap()
        gate_sc = nc.dram_tensor("gate_sc", [FQ, TPC], F32).ap()
        cc_mx_in = nc.dram_tensor("cc_mx_in", [1, 2], F32).ap()
        cc_mx_out = [nc.dram_tensor(f"cc_mx_out{i}", [1, 2], F32).ap() for i in range(8)]

    TE = Eng(cx, "te", nc.tensor)
    VE = Eng(cx, "ve", nc.vector)
    SC = Eng(cx, "sc", nc.scalar)
    GP = Eng(cx, "gp", nc.gpsimd)
    SY = Eng(cx, "sy", nc.sync)
    souts = cx.mksem("souts")

    def full_barrier():
        toks = [(E, E.seal()) for E in (TE, VE, SC, GP, SY)]
        for E, _ in toks:
            for E2, (sh, n) in toks:
                if E2 is not E and n > 0:
                    E.raw.wait_ge(sh.h, n)

    resid = cx.sb("resid", [128, NT * 1024], F32)
    stats1 = cx.sb("stats1", [128, NT * 2], F32)
    idf = cx.sb("idf", [128, 128], F32)
    idb = cx.sb("idb", [128, 128], BF16)
    onesf = cx.sb("onesf", [128, 128], F32)
    onesb = cx.sb("onesb", [128, 128], BF16)
    mg0i = cx.sb("mg0i", [128, 128], F32)
    inclb = cx.sb("inclb", [128, 128], BF16)
    iotac = cx.sb("iotac", [128, 1], F32)
    i127c = cx.sb("i127c", [128, 1], F32)

    t0 = cx.sb("t0", [128, 512], F32)
    t1 = cx.sb("t1", [128, 512], F32)
    t2 = cx.sb("t2", [128, 512], F32)
    b0 = cx.sb("b0", [128, 512], BF16)
    b1 = cx.sb("b1", [128, 512], BF16)
    b2 = cx.sb("b2", [128, 512], BF16)
    b3 = cx.sb("b3", [128, 512], BF16)
    cc = [cx.sb(f"cc{i}", [128, 1], F32) for i in range(8)]
    wtmp = cx.sb("wtmp", [128, 1024], F32)
    wtmpb = cx.sb("wtmpb", [128, 1024], BF16)
    w64 = wtmp[0:64, :]
    w64b = wtmpb[0:64, :]
    brow = wtmp[0:1, :]
    browb = wtmpb[0:1, :]
    rankc = cx.sb("rankc", [128, 8], F32)
    u0 = cx.sb("u0", [128, 1024], F32)
    u1 = cx.sb("u1", [128, 1024], F32)
    mgc = cx.sb("mgc", [128, 1], F32)
    nmgc = cx.sb("nmgc", [128, 1], F32)

    aa = []
    xfmb = cx.sb("xfmb", [128, ND * 128], BF16, pool=aa)
    maskX = {m: cx.sb(f"maskX{m}", [128, 128], F32, pool=aa) for m in "fs"}
    maskS = {m: cx.sb(f"maskS{m}", [128, 128], BF16, pool=aa) for m in "fs"}
    maskI = {m: cx.sb(f"maskI{m}", [128, 128], BF16, pool=aa) for m in "fs"}
    d1col = {m: cx.sb(f"d1col{m}", [128, 1], F32, pool=aa) for m in "fs"}
    d2bcol = {m: cx.sb(f"d2bcol{m}", [128, 1], F32, pool=aa) for m in "fs"}
    lamC = {m: cx.sb(f"lamC{m}", [128, 1], F32, pool=aa) for m in "fs"}
    betac = {m: cx.sb(f"betac{m}", [128, 1], F32, pool=aa) for m in "fs"}
    lamc = {m: cx.sb(f"lamc{m}", [128, 1], F32, pool=aa) for m in "fs"}

    wall = cx.sb("wall", [128, ND * 392], BF16, pool=aa)
    wbarr = cx.sb("wbarr", [128, 392], F32, pool=aa)
    nmb1r = cx.sb("nmb1r", [128, 128], F32, pool=aa)
    nmb2r = cx.sb("nmb2r", [128, 64], F32, pool=aa)
    irb1r = cx.sb("irb1r", [128, 64], F32, pool=aa)
    ir2r = cx.sb("ir2r", [128, 64], F32, pool=aa)
    irb2c = cx.sb("irb2c", [128, 1], F32, pool=aa)
    nwr = cx.sb("nwr", [128, 64], F32, pool=aa)
    nm2T = cx.sb("nm2T", [128, 64], BF16, pool=aa)
    woT = cx.sb("woT", [64, 1024], BF16, pool=aa)

    kbf = cx.sb("kbf", [128, NT * 64], BF16, pool=aa)
    qbf = cx.sb("qbf", [128, NT * 64], BF16, pool=aa)
    vsb = cx.sb("vsb", [128, NT * 64], F32, pool=aa)
    nusb = cx.sb("nusb", [128, NT * 64], BF16, pool=aa)
    pwtsb = cx.sb("pwtsb", [128, NT * 4], F32, pool=aa)
    ucol = cx.sb("ucol", [128, NT], F32, pool=aa)
    oprt = cx.sb("oprt", [128, NT * 64], BF16, pool=aa)
    nsloc = cx.sb("nsloc", [128, NT * 64], BF16, pool=aa)
    pcol = cx.sb("pcol", [128, NT], F32, pool=aa)
    pcr = cx.sb("pcr", [1, NT], F32, pool=aa)
    rcr = cx.sb("rcr", [1, NT * 64], BF16, pool=aa)
    mn = {m: cx.sb(f"mn{m}", [64, 128], F32, pool=aa) for m in "fs"}
    mnb = {m: cx.sb(f"mnb{m}", [64, 128], BF16, pool=aa) for m in "fs"}
    s0f = {m: cx.sb(f"s0f{m}", [64, 64], F32, pool=aa) for m in "fs"}
    s0b = {m: cx.sb(f"s0b{m}", [64, 64], BF16, pool=aa) for m in "fs"}

    proj = u0[:, 0:392]
    solR = u1[:, 0:128]
    gsf = u1[:, 128:256]
    solRb = cx.sb("solRb", [128, 128], BF16, pool=aa)
    xrb = [wtmpb[:, j * 128:(j + 1) * 128] for j in range(2)]
    _wtb = wtmp[:].bitcast(BF16)
    xcp = [wtmpb[:, 256 + j * 128:384 + j * 128] for j in range(6)] + [_wtb[:, 0:128]]

    pb = [cx.ps(f"pb{i}", [128, 512], F32) for i in range(6)]
    pf = [cx.ps(f"pf{i}", [128, 1024], BF16) for i in range(2)]
    with nc.Block() as block:
        xst = xs.rearrange("(n p) d -> n p d", p=128)
        for i in range(NT):
            SY.dma(resid[:, i * 1024:(i + 1) * 1024], xst[i])
        ld_resid = (SY.sem, SY.sem.n)

        e_str = t0[:, 0:128]
        mg0s = t0[:, 128:256]
        GP.op("iota", t1[:, 0:128].bitcast(I32), pattern=[[1, 128]], base=-1, channel_multiplier=-1)
        GP.op("iota", t1[:, 128:129].bitcast(I32), pattern=[[0, 1]], base=0, channel_multiplier=1)
        GP.op("iota", t1[:, 129:130].bitcast(I32), pattern=[[0, 1]], base=127, channel_multiplier=-1)
        VE.wait(GP)
        VE.op("tensor_copy", e_str, t1[:, 0:128].bitcast(I32))
        VE.op("tensor_copy", iotac[:], t1[:, 128:129].bitcast(I32))
        VE.op("tensor_copy", i127c[:], t1[:, 129:130].bitcast(I32))
        VE.wait(VE)
        VE.op("tensor_scalar", mg0s, e_str, 0.0, None, OP.is_ge)
        VE.op("tensor_scalar", mg0i[:], e_str, -1.0, None, OP.is_ge)
        VE.wait(VE)
        VE.op("tensor_copy", inclb[:], mg0i[:])
        VE.op("tensor_scalar", idf[:], e_str, -1.0, None, OP.is_equal)
        VE.wait(VE)
        VE.op("tensor_copy", idb[:], idf[:])
        VE.op("memset", onesf[:], 1.0)
        VE.wait(VE)
        VE.op("tensor_copy", onesb[:], onesf[:])

        alin = {"f": cc[0], "s": cc[1]}
        alln = {"f": cc[2], "s": cc[3]}
        alb = {"f": cc[4], "s": cc[5]}
        GP.dma(cc[0][0:1, 0:1], fa_in)
        GP.dma(cc[1][0:1, 0:1], sa_in)
        GP.dma(rankc[0:1, :], rankm)
        SC.wait(GP)
        for m in "fs":
            SC.op("activation", alln[m][0:1, 0:1], alin[m][0:1, 0:1], AF.Ln, bias=0.0, scale=1.0)
            SC.op("activation", alb[m][0:1, 0:1], alin[m][0:1, 0:1], AF.Identity, bias=1.0, scale=-1.0)
        TE.wait(SC, VE)
        for i, m in enumerate("fs"):
            TE.op("matmul", pb[0][:, i:i + 1], onesf[0:1, :], alln[m][0:1, 0:1], start=True, stop=True)
            TE.op("matmul", pb[0][:, 2 + i:3 + i], onesf[0:1, :], alb[m][0:1, 0:1], start=True, stop=True)
        TE.op("matmul", pb[0][:, 4:12], onesf[0:1, :], rankc[0:1, :], start=True, stop=True)
        VE.wait(TE)
        lnrep = {"f": cc[6], "s": cc[7]}
        for i, m in enumerate("fs"):
            VE.op("tensor_copy", lnrep[m][:], pb[0][:, i:i + 1])
            VE.op("tensor_copy", betac[m][:], pb[0][:, 2 + i:3 + i])
        VE.op("tensor_copy", rankc[:], pb[0][:, 4:12])
        for m in "fs":
            VE.op("tensor_scalar", lamc[m][:], betac[m][:], -1.0, 1.0, OP.mult, OP.add)
            VE.op("tensor_tensor", t2[:, 0:1], iotac[:], lnrep[m][:], OP.mult)
            VE.op("tensor_tensor", t2[:, 1:2], i127c[:], lnrep[m][:], OP.mult)
            VE.op("tensor_scalar", t2[:, 2:3], lnrep[m][:], 128.0, None, OP.mult)
            VE.op("tensor_scalar", t2[:, 128:256], e_str, lnrep[m][:], None, OP.mult)
            SC.wait(VE)
            SC.op("activation", d1col[m][:], t2[:, 0:1], AF.Exp, bias=0.0, scale=1.0)
            SC.op("activation", cc[3][:], t2[:, 1:2], AF.Exp, bias=0.0, scale=1.0)
            SC.op("activation", lamC[m][:], t2[:, 2:3], AF.Exp, bias=0.0, scale=1.0)
            SC.op("activation", t1[:, 0:128], t2[:, 128:256], AF.Exp, bias=0.0, scale=1.0)
            VE.wait(SC)
            VE.op("tensor_tensor", d2bcol[m][:], cc[3][:], betac[m][:], OP.mult)
            VE.op("tensor_tensor", t1[:, 128:256], t1[:, 0:128], mg0s, OP.mult)
            VE.wait(VE)
            VE.op("tensor_copy", maskS[m][:], t1[:, 128:256])
            VE.op("tensor_scalar", maskX[m][:], t1[:, 128:256], betac[m][:], None, OP.mult)
            VE.op("tensor_tensor", t1[:, 256:384], t1[:, 0:128], mg0i[:], OP.mult)
            VE.wait(VE)
            VE.op("tensor_scalar", t1[:, 384:512], t1[:, 256:384], lamc[m][:], None, OP.mult)
            VE.wait(VE)
            VE.op("tensor_copy", maskI[m][:], t1[:, 384:512])

        def load_t_cols(wap, col, rows=64):
            SY.wait(TE, VE)
            SY.dma(w64[0:rows, :], wap)
            VE.wait(SY)
            VE.op("tensor_copy", w64b[0:rows, :], w64[0:rows, :])
            TE.wait(VE)
            for d in range(ND):
                TE.op("transpose", pf[d % 2][:, 0:rows], w64b[0:rows, d * 128:(d + 1) * 128], idb[0:rows, 0:rows])
                VE.wait(TE)
                VE.op("tensor_copy", wall[:, d * 392 + col: d * 392 + col + rows], pf[d % 2][:, 0:rows])
                TE.wait(VE)

        load_t_cols(aWk, 0)
        load_t_cols(aWq, 64)
        load_t_cols(aWv, 128)
        load_t_cols(ir_w1, 192)
        load_t_cols(pw, 256, rows=3)
        SY.wait(TE, VE)
        SY.dma(t2[:, 0:64], nm_w1)
        SY.dma(w64, aWv)
        VE.wait(SY)
        VE.op("tensor_copy", w64b, w64)
        VE.op("tensor_copy", b0[:, 0:64], t2[:, 0:64])
        TE.wait(VE)
        TE.op("transpose", pf[0][0:64, 0:128], b0[:, 0:64], idb[:])
        VE.wait(TE)
        VE.op("tensor_copy", b1[0:64, 0:128], pf[0][0:64, 0:128])
        TE.wait(VE)
        TE.op("matmul", pb[0][:], b1[0:64, 0:128], w64b[:, 0:512], start=True, stop=True)
        TE.op("matmul", pb[1][:], b1[0:64, 0:128], w64b[:, 512:1024], start=True, stop=True)
        VE.wait(TE)
        VE.op("tensor_copy", wtmpb[:, 0:512], pb[0][:])
        VE.op("tensor_copy", wtmpb[:, 512:1024], pb[1][:])
        TE.wait(VE)
        for d in range(ND):
            TE.op("transpose", pf[d % 2][:, 0:128], wtmpb[:, d * 128:(d + 1) * 128], idb[:])
            VE.wait(TE)
            VE.op("tensor_copy", wall[:, d * 392 + 259: d * 392 + 387], pf[d % 2][:, 0:128])
            TE.wait(VE)
        for d in range(ND):
            VE.op("memset", wall[:, d * 392 + 387:(d + 1) * 392], 0.0)
        TE.wait(VE)
        for d in range(ND):
            TE.op("matmul", pb[2][0:1, 0:392], onesb[:, 0:1], wall[:, d * 392: d * 392 + 392],
                  start=(d == 0), stop=(d == ND - 1))
        VE.wait(TE)
        VE.op("tensor_copy", b0[0:1, 0:392], pb[2][0:1, 0:392])
        TE.wait(VE)
        TE.op("matmul", pb[2][:, 0:392], onesb[0:1, :], b0[0:1, 0:392], start=True, stop=True)
        VE.wait(TE)
        VE.op("tensor_copy", wbarr[:], pb[2][:, 0:392])

        def replicate(dst, src_ap, width):
            SY.wait(TE, VE)
            SY.dma(brow[0:1, 0:width], src_ap)
            VE.wait(SY)
            VE.op("tensor_copy", browb[0:1, 0:width], brow[0:1, 0:width])
            TE.wait(VE)
            TE.op("matmul", pb[3][:, 0:width], onesb[0:1, :], browb[0:1, 0:width], start=True, stop=True)
            VE.wait(TE)
            VE.op("tensor_copy", dst, pb[3][:, 0:width])
            TE.wait(VE)

        replicate(nmb1r[:], nm_b1, 128)
        replicate(nmb2r[:], nm_b2, 64)
        replicate(irb1r[:], ir_b1, 64)
        replicate(ir2r[:], ir_w2, 64)
        replicate(irb2c[:], ir_b2, 1)
        replicate(nwr[:], norm_w, 64)
        SY.wait(TE, VE)
        SY.dma(w64[:, 0:128], nm_w2)
        VE.wait(SY)
        VE.op("tensor_copy", w64b[:, 0:128], w64[:, 0:128])
        TE.wait(VE)
        TE.op("transpose", pf[0][:, 0:64], w64b[0:64, 0:128], idb[0:64, 0:64])
        VE.wait(TE)
        VE.op("tensor_copy", nm2T[:], pf[0][:, 0:64])
        for d in range(ND):
            SY.wait(TE, VE)
            SY.dma(wtmp[:, 0:64], aWo[d * 128:(d + 1) * 128, :])
            VE.wait(SY)
            VE.op("tensor_copy", b0[:, 0:64], wtmp[:, 0:64])
            TE.wait(VE)
            TE.op("transpose", pf[d % 2][0:64, 0:128], b0[:, 0:64], idb[:])
            VE.wait(TE)
            VE.op("tensor_copy", woT[:, d * 128:(d + 1) * 128], pf[d % 2][0:64, 0:128])
            TE.wait(VE)

        SC.wait_tok(ld_resid)
        VE.wait_tok(ld_resid)
        for i in range(NT):
            xt = resid[:, i * 1024:(i + 1) * 1024]
            SC.op("activation", t0[:, 0:512], xt[:, 0:512], AF.Square, bias=0.0, scale=1.0, accum_out=cc[0][:])
            SC.op("activation", t1[:, 0:512], xt[:, 512:1024], AF.Square, bias=0.0, scale=1.0, accum_out=cc[1][:])
            VE.op("tensor_reduce", cc[2][:], xt, axis=AX, op=OP.add)
            VE.wait(SC)
            VE.wait(VE)
            VE.op("tensor_tensor", cc[3][:], cc[0][:], cc[1][:], OP.add)
            VE.op("tensor_scalar", cc[2][:], cc[2][:], 1.0 / 1024.0, None, OP.mult)
            SC.wait(VE)
            SC.op("activation", cc[4][:], cc[2][:], AF.Square, bias=0.0, scale=1.0)
            VE.wait(SC)
            VE.op("tensor_scalar", cc[3][:], cc[3][:], 1.0 / 1024.0, None, OP.mult)
            VE.wait(VE)
            VE.op("tensor_tensor", cc[3][:], cc[3][:], cc[4][:], OP.subtract)
            VE.wait(VE)
            VE.op("tensor_scalar", cc[5][:], cc[3][:], 1e-5, None, OP.add)
            SC.wait(VE)
            SC.op("activation", cc[4][:], cc[5][:], AF.Sqrt, bias=0.0, scale=1.0)
            VE.wait(SC)
            VE.op("reciprocal", stats1[:, 2 * i:2 * i + 1], cc[4][:])
            VE.wait(VE)
            VE.op("tensor_tensor", cc[4][:], cc[2][:], stats1[:, 2 * i:2 * i + 1], OP.mult)
            VE.wait(VE)
            VE.op("tensor_scalar", stats1[:, 2 * i + 1:2 * i + 2], cc[4][:], -1.0, None, OP.mult)
        if "attn_a" in PHASES:
            for i in range(NT):
                xt = resid[:, i * 1024:(i + 1) * 1024]
                rstd = stats1[:, 2 * i:2 * i + 1]
                nmrs = stats1[:, 2 * i + 1:2 * i + 2]
                TE.wait(VE)
                for d in range(ND):
                    TE.op("transpose", pb[1][:, 0:128], xt[:, d * 128:(d + 1) * 128], idf[:])
                    VE.wait(TE)
                    VE.op("tensor_copy", xfmb[:, d * 128:(d + 1) * 128], pb[1][:, 0:128])
                    TE.wait(VE)
                for d in range(ND):
                    TE.op("matmul", pb[0][:, 0:392], xfmb[:, d * 128:(d + 1) * 128],
                          wall[:, d * 392:(d + 1) * 392], start=(d == 0), stop=(d == ND - 1))
                VE.wait(TE)
                VE.op("tensor_scalar", t0[:, 0:392], pb[0][:, 0:392], rstd, None, OP.mult)
                VE.op("scalar_tensor_tensor", proj, wbarr[:], nmrs, t0[:, 0:392], OP.mult, OP.add)
                VE.wait(VE)

                SC.wait(VE)
                SC.op("activation", t0[:, 0:64], proj[:, 0:64], AF.Square, bias=0.0, scale=1.0, accum_out=cc[0][:])
                SC.op("activation", t0[:, 64:128], proj[:, 64:128], AF.Square, bias=0.0, scale=1.0, accum_out=cc[1][:])
                SC.wait(SC)
                SC.op("activation", cc[2][:], cc[0][:], AF.Sqrt, bias=0.0, scale=1.0)
                SC.op("activation", cc[3][:], cc[1][:], AF.Sqrt, bias=0.0, scale=1.0)
                VE.wait(SC)
                VE.op("tensor_scalar", cc[2][:], cc[2][:], 1e-12, None, OP.max)
                VE.op("tensor_scalar", cc[3][:], cc[3][:], 1e-12, None, OP.max)
                VE.wait(VE)
                VE.op("reciprocal", cc[0][:], cc[2][:])
                VE.op("reciprocal", cc[1][:], cc[3][:])
                VE.wait(VE)
                VE.op("tensor_scalar", kbf[:, i * 64:(i + 1) * 64], proj[:, 0:64], cc[0][:], None, OP.mult)
                VE.op("tensor_scalar", qbf[:, i * 64:(i + 1) * 64], proj[:, 64:128], cc[1][:], None, OP.mult)

                VE.op("tensor_copy", vsb[:, i * 64:(i + 1) * 64], proj[:, 128:192])
                VE.op("tensor_tensor", t0[:, 0:128], proj[:, 259:387], nmb1r[:], OP.add)
                SC.wait(VE)
                SC.op("activation", t1[:, 0:128], t0[:, 0:128], AF.Gelu, bias=0.0, scale=1.0)
                VE.wait(SC)
                VE.op("tensor_copy", b0[:, 0:128], t1[:, 0:128])
                TE.wait(VE)
                TE.op("transpose", pf[0][:, 0:128], b0[:, 0:128], idb[:])
                VE.wait(TE)
                VE.op("tensor_copy", b1[:, 0:128], pf[0][:, 0:128])
                TE.wait(VE)
                TE.op("matmul", pb[1][:, 0:64], b1[:, 0:128], nm2T[:], start=True, stop=True)
                VE.wait(TE)
                VE.op("tensor_tensor", nusb[:, i * 64:(i + 1) * 64], pb[1][:, 0:64], nmb2r[:], OP.add)

                VE.op("tensor_tensor", t0[:, 192:256], proj[:, 192:256], irb1r[:], OP.add)
                SC.wait(VE)
                SC.op("activation", t0[:, 128:192], t0[:, 192:256], AF.Gelu, bias=0.0, scale=1.0)
                VE.wait(SC)
                VE.op("tensor_tensor", t1[:, 0:64], t0[:, 128:192], ir2r[:], OP.mult)
                VE.wait(VE)
                VE.op("tensor_reduce", cc[0][:], t1[:, 0:64], axis=AX, op=OP.add)
                VE.wait(VE)
                VE.op("tensor_tensor", cc[1][:], cc[0][:], irb2c[:], OP.add)
                VE.wait(VE)
                VE.op("tensor_scalar", ucol[:, i:i + 1], cc[1][:], 0.0, None, OP.is_gt)

                VE.op("tensor_reduce", cc[0][:], proj[:, 256:259], axis=AX, op=OP.max)
                VE.wait(VE)
                VE.op("tensor_scalar", cc[1][:], cc[0][:], -1.0, None, OP.mult)
                SC.wait(VE)
                SC.op("activation", t0[:, 0:3], proj[:, 256:259], AF.Exp, bias=cc[1][:], scale=1.0)
                VE.wait(SC)
                VE.op("tensor_reduce", cc[2][:], t0[:, 0:3], axis=AX, op=OP.add)
                VE.wait(VE)
                VE.op("reciprocal", cc[2][:], cc[2][:])
                VE.wait(VE)
                VE.op("tensor_scalar", pwtsb[:, i * 4:i * 4 + 3], t0[:, 0:3], cc[2][:], None, OP.mult)

                TE.wait(VE)
                TE.op("transpose", pf[0][0:64, 0:128], kbf[:, i * 64:(i + 1) * 64], idb[:])
                TE.op("transpose", pf[1][0:64, 0:128], qbf[:, i * 64:(i + 1) * 64], idb[:])
                VE.wait(TE)
                VE.op("tensor_copy", b0[0:64, 0:128], pf[0][0:64, 0:128])
                VE.op("tensor_copy", b1[0:64, 0:128], pf[1][0:64, 0:128])
                TE.wait(VE)
                TE.op("matmul", pb[2][:, 0:128], b0[0:64, 0:128], b0[0:64, 0:128], start=True, stop=True)
                TE.op("matmul", pb[3][:, 0:128], b0[0:64, 0:128], b1[0:64, 0:128], start=True, stop=True)
                VE.wait(TE)
                VE.wait(GP)
                VE.op("tensor_copy", gsf, pb[2][:, 0:128])
                VE.op("tensor_copy", b3[:, 0:128], pb[3][:, 0:128])
                GP.wait(VE)
                GP.dma(spillQ[i], b3[:, 0:128])

                for mi, m in enumerate("fs"):
                    VE.wait(GP)
                    if m == "f":
                        VE.op("tensor_tensor", xcp[0], gsf, maskX[m][:], OP.mult)
                    else:
                        VE.op("scalar_tensor_tensor", xcp[0], gsf, ucol[:, i:i + 1], maskX[m][:], OP.mult, OP.mult)
                    TE.wait(VE)
                    TE.op("transpose", pf[0][:, 0:128], xcp[0], idb[:])
                    VE.wait(TE)
                    VE.op("tensor_copy", xrb[0], pf[0][:, 0:128])
                    VE.op("tensor_copy", solR[:, 0:64], vsb[:, i * 64:(i + 1) * 64])
                    VE.op("tensor_scalar", solR[:, 64:128], kbf[:, i * 64:(i + 1) * 64], d1col[m][:], None, OP.mult)
                    VE.wait(VE)
                    VE.op("tensor_copy", solRb[:], solR)
                    TE.wait(VE)
                    for lev in range(6):
                        TE.op("matmul", pb[1][:, 0:128], xcp[lev], xrb[lev % 2], start=True, stop=True)
                        VE.wait(TE)
                        VE.op("tensor_copy", xrb[(lev + 1) % 2], pb[1][:, 0:128])
                        TE.wait(VE)
                        TE.op("transpose", pf[0][:, 0:128], xrb[(lev + 1) % 2], idb[:])
                        VE.wait(TE)
                        VE.op("tensor_copy", xcp[lev + 1], pf[0][:, 0:128])
                        TE.wait(VE)
                    for j in range(6, -1, -1):
                        TE.op("matmul", pb[3][:, 0:128], xcp[j], solRb[:], start=True, stop=True)
                        VE.wait(TE)
                        VE.op("tensor_tensor", solR, solR, pb[3][:, 0:128],
                              OP.add if j > 0 else OP.subtract)
                        VE.wait(VE)
                        VE.op("tensor_copy", solRb[:], solR)
                        TE.wait(VE)
                    if m == "f":
                        VE.op("tensor_scalar", b2[:, 0:64], kbf[:, i * 64:(i + 1) * 64], d2bcol[m][:], None, OP.mult)
                    else:
                        VE.op("tensor_scalar", b2[:, 0:64], kbf[:, i * 64:(i + 1) * 64], d2bcol[m][:], ucol[:, i:i + 1], OP.mult, OP.mult)
                    GP.wait(VE)
                    GP.dma(spillA[2 * i + mi][:, 0:128], solRb[:])
                    GP.dma(spillA[2 * i + mi][:, 128:192], b2[:, 0:64])
                    TE.wait(VE)
                    TE.op("matmul", pb[4][0:64, 0:128], b2[:, 0:64], solRb[:], start=True, stop=True)
                    VE.wait(TE)
                    VE.op("scalar_tensor_tensor", t0[0:64, 0:64], idf[0:64, 0:64], lamC[m][0:64, :], pb[4][0:64, 64:128], OP.mult, OP.subtract)
                    VE.op("tensor_copy", t1[0:64, 64:128], pb[4][0:64, 0:64])
                    VE.wait(VE)
                    VE.op("tensor_copy", b2[0:64, 64:128], t0[0:64, 0:64])
                    if i == 0:
                        VE.op("tensor_copy", mn[m][0:64, 64:128], t0[0:64, 0:64])
                        VE.op("tensor_copy", mn[m][0:64, 0:64], pb[4][0:64, 0:64])
                        VE.wait(VE)
                        VE.op("tensor_copy", mnb[m][:], mn[m][:])
                    else:
                        TE.wait(VE)
                        TE.op("transpose", pf[0][0:64, 0:64], b2[0:64, 64:128], idb[0:64, 0:64])
                        VE.wait(TE)
                        VE.op("tensor_copy", b3[0:64, 128:192], pf[0][0:64, 0:64])
                        TE.wait(VE)
                        TE.op("matmul", pb[5][0:64, 0:128], b3[0:64, 128:192], mnb[m][:], start=True, stop=True)
                        VE.wait(TE)
                        VE.op("tensor_tensor", mn[m][0:64, 0:64], pb[5][0:64, 0:64], t1[0:64, 64:128], OP.add)
                        VE.op("tensor_copy", mn[m][0:64, 64:128], pb[5][0:64, 64:128])
                        VE.wait(VE)
                        VE.op("tensor_copy", mnb[m][:], mn[m][:])
                    TE.wait(VE)

            GP.wait(VE)
            tap(GP, "stats1", stats1[:])
            tap(GP, "wall", wall[:])
            tap(GP, "xfmbL", xfmb[:])
            tap(GP, "wbarr", wbarr[:])
            tap(GP, "kbf", kbf[:])
            tap(GP, "qbf", qbf[:])
            tap(GP, "vsb", vsb[:])
            tap(GP, "nusb", nusb[:])
            tap(GP, "ucol", ucol[:])
            tap(GP, "pwtsb", pwtsb[:])
            tap(GP, "mnf", mn["f"][0:64, :])
            tap(GP, "mns", mn["s"][0:64, :])
            tap(GP, "solA_d", spillA)
            GP.wait(VE)
            GP.dma(ccA_in[:, 0:64], mn["f"][0:64, 0:64])
            GP.dma(ccA_in[:, 64:128], mn["f"][0:64, 64:128])
            GP.dma(ccA_in[:, 128:192], mn["s"][0:64, 0:64])
            GP.dma(ccA_in[:, 192:256], mn["s"][0:64, 64:128])
            GP.wait(GP)
            tokA = GP.coll("AllGather", OP.bypass,
                           replica_groups=[[0, 1, 2, 3], [4, 5, 6, 7]],
                           ins=[ccA_in.opt()], outs=[ccA_out.opt()])
            gat = wtmp[0:64, :]
            GP.wait_tok(tokA)
            GP.wait(TE, VE)
            for r in range(4):
                GP.dma(gat[:, r * 256:(r + 1) * 256], ccA_out[r * 64:(r + 1) * 64, :])
            VE.wait(GP)
            VE.op("tensor_scalar", t2[:, 200:208], rankc[:], -1.0, 1.0, OP.mult, OP.add)
            for mi, m in enumerate("fs"):
                VE.op("memset", s0f[m][:], 0.0)
                VE.op("memset", s0b[m][:], 0.0)
            VE.wait(VE)
            for r in range(4):
                for mi, m in enumerate("fs"):
                    Nj = gat[0:64, r * 256 + mi * 128: r * 256 + mi * 128 + 64]
                    Mj = gat[0:64, r * 256 + mi * 128 + 64: r * 256 + mi * 128 + 128]
                    rj = rankc[0:64, r:r + 1]
                    rnj = t2[0:64, 200 + r:201 + r]
                    VE.op("tensor_scalar", t0[0:64, 0:64], Mj, rj, None, OP.mult)
                    VE.wait(VE)
                    VE.op("scalar_tensor_tensor", t0[0:64, 0:64], idf[0:64, 0:64], rnj, t0[0:64, 0:64], OP.mult, OP.add)
                    VE.wait(VE)
                    VE.op("tensor_copy", b2[0:64, 0:64], t0[0:64, 0:64])
                    VE.op("tensor_scalar", t1[0:64, 0:64], Nj, rj, None, OP.mult)
                    TE.wait(VE)
                    TE.op("transpose", pf[0][0:64, 0:64], b2[0:64, 0:64], idb[0:64, 0:64])
                    VE.wait(TE)
                    VE.op("tensor_copy", b3[0:64, 0:64], pf[0][0:64, 0:64])
                    TE.wait(VE)
                    TE.op("matmul", pb[0][0:64, 0:64], b3[0:64, 0:64], s0b[m][:], start=True, stop=True)
                    VE.wait(TE)
                    VE.op("tensor_tensor", s0f[m][:], pb[0][0:64, 0:64], t1[0:64, 0:64], OP.add)
                    VE.wait(VE)
                    VE.op("tensor_copy", s0b[m][:], s0f[m][:])

            GP.wait(VE)
            tap(GP, "gatA2", gat[0:64, :])
            tap(GP, "rankcT", rankc[:])
            tap(GP, "s0fA2", s0f["f"][:])
            tap(GP, "s0sA2", s0f["s"][:])
            solA = cx.sb("solA", [128, 192 * 2], BF16, pool=aa)
            qgr = cx.sb("qgr", [128, 128], BF16, pool=aa)
            pre = {"f": u1[:, 128:192], "s": u1[:, 192:256]}
            post = {"f": u1[:, 256:320], "s": u1[:, 320:384]}
            utl = u1[:, 64:128]
            t0sb = u1[:, 0:64]
            utlb = cx.sb("utlb", [128, 64], BF16, pool=aa)
            ub = cx.sb("ub", [128, 64], BF16, pool=aa)
            aqm = cx.sb("aqm", [128, 128], BF16, pool=aa)
            qcs = b3[0:64, 256:384]

            SY.wait(GP)
            for i in range(NT):
                SY.wait(TE, VE)
                SY.dma(solA[:, 0:192], spillA[2 * i])
                SY.dma(solA[:, 192:384], spillA[2 * i + 1])
                SY.dma(qgr[:], spillQ[i])
                TE.wait(SY)
                VE.wait(SY)
                for mi, m in enumerate("fs"):
                    sol = solA[:, mi * 192: mi * 192 + 128]
                    k2b = solA[:, mi * 192 + 128: mi * 192 + 192]
                    TE.op("transpose", pf[0][0:64, 0:128], sol[:, 64:128], idb[:])
                    VE.op("tensor_scalar", b0[:, 0:64], qbf[:, i * 64:(i + 1) * 64], d1col[m][:], None, OP.mult)
                    TE.wait(VE)
                    TE.op("transpose", pf[1][0:64, 0:128], b0[:, 0:64], idb[:])
                    VE.wait(TE)
                    VE.op("tensor_copy", b1[0:64, 0:128], pf[0][0:64, 0:128])
                    VE.op("tensor_copy", qcs, pf[1][0:64, 0:128])
                    TE.wait(VE)
                    TE.op("matmul", pb[0][:, 0:64], b1[0:64, 0:128], s0b[m][:], start=True, stop=True)
                    TE.op("matmul", pb[1][:, 0:64], qcs, s0b[m][:], start=True, stop=True)
                    VE.wait(TE)
                    VE.op("tensor_tensor", utl, sol[:, 0:64], pb[0][:, 0:64], OP.subtract)
                    VE.op("tensor_copy", t0sb, pb[1][:, 0:64])
                    VE.wait(VE)
                    VE.op("tensor_copy", utlb[:], utl)
                    if m == "f":
                        VE.op("tensor_scalar", t0[:, 0:64], utl, betac[m][:], None, OP.mult)
                    else:
                        VE.op("tensor_scalar", t0[:, 0:64], utl, betac[m][:], ucol[:, i:i + 1], OP.mult, OP.mult)
                    VE.wait(VE)
                    VE.op("tensor_copy", ub[:], t0[:, 0:64])
                    VE.op("tensor_tensor", aqm[:], qgr[:], maskS[m][:], OP.mult)
                    TE.wait(VE)
                    TE.op("matmul", pb[2][:, 0:64], aqm[:], ub[:], start=True, stop=True)
                    VE.wait(TE)
                    VE.op("tensor_tensor", pre[m], pb[2][:, 0:64], t0sb, OP.add)
                    VE.op("tensor_tensor", aqm[:], qgr[:], maskI[m][:], OP.mult)
                    TE.wait(VE)
                    TE.op("matmul", pb[2][:, 0:64], aqm[:], ub[:], start=True, stop=True)
                    TE.op("matmul", pb[3][0:64, 0:64], k2b[:], utlb[:], start=True, stop=True)
                    VE.wait(TE)
                    VE.op("scalar_tensor_tensor", post[m], t0sb, lamc[m][:], pb[2][:, 0:64], OP.mult, OP.add)
                    VE.op("scalar_tensor_tensor", s0f[m][:], s0f[m][:], lamC[m][0:64, :], pb[3][0:64, 0:64], OP.mult, OP.add)
                    VE.wait(VE)
                    VE.op("tensor_copy", s0b[m][:], s0f[m][:])
                    TE.wait(VE)

                VE.op("tensor_tensor", t0[:, 0:64], pre["f"], pre["s"], OP.add)
                VE.wait(VE)
                VE.op("tensor_scalar", t0[:, 64:128], t0[:, 0:64], -0.5, None, OP.mult)
                VE.wait(VE)
                VE.op("tensor_tensor", t1[:, 0:64], vsb[:, i * 64:(i + 1) * 64], t0[:, 64:128], OP.add)
                SC.wait(VE)
                SC.op("activation", t1[:, 64:128], t1[:, 0:64], AF.Square, bias=0.0, scale=1.0, accum_out=cc[0][:])
                VE.wait(SC)
                VE.op("tensor_scalar", cc[1][:], cc[0][:], 1.0 / (1.0 + 1e-6), LN7_3, OP.mult, OP.is_gt)
                VE.wait(VE)
                VE.op("tensor_copy", b0[:, 0:1], cc[1][:])
                TE.wait(VE)
                TE.op("matmul", pb[0][:, 0:1], inclb[:], b0[:, 0:1], start=True, stop=True)
                VE.wait(TE)
                VE.op("tensor_scalar", t0[:, 0:1], pb[0][:, 0:1], LN09, None, OP.mult)
                VE.wait(VE)
                SC.wait(VE)
                SC.op("activation", pcol[:, i:i + 1], t0[:, 0:1], AF.Exp, bias=0.0, scale=1.0)
                VE.op("tensor_scalar", t0[:, 1:2], t0[:, 0:1], -1.0, None, OP.mult)
                TE.wait(VE)
                TE.op("transpose", pb[1][0:1, 0:128], t0[:, 0:1], idf[:])
                VE.wait(TE)
                VE.op("tensor_copy", t2[0:1, 0:128], pb[1][0:1, 0:128])
                VE.wait(VE)
                VE.op("tensor_copy", b0[0:1, 0:128], t2[0:1, 0:128])
                TE.wait(VE)
                TE.op("matmul", pb[2][:, 0:128], onesb[0:1, :], b0[0:1, 0:128], start=True, stop=True)
                SC.wait(TE)
                SC.op("activation", t2[:, 256:384], pb[2][:, 0:128], AF.Exp, bias=t0[:, 1:2], scale=1.0)
                VE.wait(SC)
                VE.op("tensor_tensor", t2[:, 128:256], t2[:, 256:384], mg0i[:], OP.mult)
                VE.op("tensor_scalar", t1[:, 0:64], nusb[:, i * 64:(i + 1) * 64], cc[1][:], 0.1, OP.mult, OP.mult)
                VE.wait(VE)
                VE.op("tensor_copy", b1[:, 0:128], t2[:, 128:256])
                VE.op("tensor_copy", b2[:, 0:64], t1[:, 0:64])
                TE.wait(VE)
                TE.op("matmul", pb[3][:, 0:64], b1[:, 0:128], b2[:, 0:64], start=True, stop=True)
                TE.op("matmul", pb[4][0:1, 0:64], b1[:, 127:128], b2[:, 0:64], start=True, stop=True)
                VE.wait(TE)
                VE.op("tensor_copy", nsloc[:, i * 64:(i + 1) * 64], pb[3][:, 0:64])
                VE.op("tensor_copy", rcr[0:1, i * 64:(i + 1) * 64], pb[4][0:1, 0:64])
                SC.op("activation", pcr[0:1, i:i + 1], t2[0:1, 127:128], AF.Exp, bias=0.0, scale=1.0)
                VE.op("tensor_scalar", t0[:, 0:64], post["f"], pwtsb[:, i * 4:i * 4 + 1], None, OP.mult)
                VE.wait(VE)
                VE.op("scalar_tensor_tensor", oprt[:, i * 64:(i + 1) * 64], post["s"], pwtsb[:, i * 4 + 1:i * 4 + 2], t0[:, 0:64], OP.mult, OP.add)

            GP.wait(VE, SC)
            tap(GP, "prefL", pre["f"])
            tap(GP, "presL", pre["s"])
            tap(GP, "postfL", post["f"])
            tap(GP, "postsL", post["s"])
            tap(GP, "pcr", pcr[0:1, :])
            tap(GP, "rcr", rcr[0:1, :])
            tap(GP, "pcol", pcol[:])
            tap(GP, "nsloc", nsloc[:])
            tap(GP, "oprt", oprt[:])
            tap(GP, "s0fA3", s0f["f"][:])
            tap(GP, "s0sA3", s0f["s"][:])
            VE.wait(SC)
            VE.op("memset", t2[0:1, 0:64], 0.0)
            VE.op("memset", t2[0:1, 64:65], 1.0)
            VE.wait(VE)
            for i in range(NT):
                VE.op("scalar_tensor_tensor", t2[0:1, 0:64], t2[0:1, 0:64], pcr[0:1, i:i + 1], rcr[0:1, i * 64:(i + 1) * 64], OP.mult, OP.add)
                VE.op("tensor_tensor", t2[0:1, 64:65], t2[0:1, 64:65], pcr[0:1, i:i + 1], OP.mult)
                VE.wait(VE)
            GP.wait(VE)
            GP.dma(ccN_in[0:1, 0:1], t2[0:1, 64:65])
            GP.dma(ccN_in[0:1, 1:65], t2[0:1, 0:64])
            GP.wait(GP)
            tokN = GP.coll("AllGather", OP.bypass,
                           replica_groups=[[0, 1, 2, 3], [4, 5, 6, 7]],
                           ins=[ccN_in.opt()], outs=[ccN_out.opt()])
            gat2 = wtmp[0:1, 0:264]
            GP.wait_tok(tokN)
            GP.wait(TE, VE)
            GP.dma(gat2[0:1, :], ccN_out.rearrange("(o a) b -> o (a b)", o=1))
            VE.wait(GP)
            VE.op("memset", t2[0:1, 128:192], 0.0)
            VE.wait(VE)
            for r in range(4):
                Pj = gat2[0:1, r * 66: r * 66 + 1]
                rjv = gat2[0:1, r * 66 + 1: r * 66 + 65]
                rj = rankc[0:1, r:r + 1]
                VE.op("tensor_scalar", t0[0:1, 0:1], Pj, rj, None, OP.mult)
                VE.op("tensor_scalar", t0[0:1, 1:2], rj, -1.0, 1.0, OP.mult, OP.add)
                VE.wait(VE)
                VE.op("tensor_tensor", t0[0:1, 0:1], t0[0:1, 0:1], t0[0:1, 1:2], OP.add)
                VE.op("tensor_scalar", t0[0:1, 2:66], rjv, rj, None, OP.mult)
                VE.wait(VE)
                VE.op("scalar_tensor_tensor", t2[0:1, 128:192], t2[0:1, 128:192], t0[0:1, 0:1], t0[0:1, 2:66], OP.mult, OP.add)
                VE.wait(VE)
            ns0c = cx.sb("ns0c", [1, 64], F32, pool=aa)
            VE.op("tensor_copy", ns0c[0:1, :], t2[0:1, 128:192])
            VE.wait(VE)
            GP.wait(VE)
            tap(GP, "nscs", ns0c[0:1, :])

            for i in range(NT):
                VE.op("tensor_copy", browb[0:1, 0:64], ns0c[0:1, :])
                TE.wait(VE)
                TE.op("matmul", pb[0][:, 0:64], onesb[0:1, :], browb[0:1, 0:64], start=True, stop=True)
                VE.wait(TE)
                VE.op("scalar_tensor_tensor", t0[:, 0:64], pb[0][:, 0:64], pcol[:, i:i + 1], nsloc[:, i * 64:(i + 1) * 64], OP.mult, OP.add)
                VE.wait(VE)
                VE.op("scalar_tensor_tensor", t1[:, 0:64], t0[:, 0:64], pwtsb[:, i * 4 + 2:i * 4 + 3], oprt[:, i * 64:(i + 1) * 64], OP.mult, OP.add)
                SC.wait(VE)
                SC.op("activation", t2[:, 0:64], t1[:, 0:64], AF.Square, bias=0.0, scale=1.0, accum_out=cc[0][:])
                VE.wait(SC)
                VE.op("tensor_scalar", cc[1][:], cc[0][:], 1.0 / 64.0, 1e-6, OP.mult, OP.add)
                SC.wait(VE)
                SC.op("activation", cc[2][:], cc[1][:], AF.Sqrt, bias=0.0, scale=1.0)
                VE.wait(SC)
                VE.op("reciprocal", cc[3][:], cc[2][:])
                VE.wait(VE)
                VE.op("tensor_scalar", t2[:, 0:64], t1[:, 0:64], cc[3][:], None, OP.mult)
                VE.wait(VE)
                VE.op("tensor_tensor", t2[:, 64:128], t2[:, 0:64], nwr[:], OP.mult)
                VE.wait(VE)
                VE.op("tensor_copy", b0[:, 0:64], t2[:, 64:128])
                TE.wait(VE)
                TE.op("transpose", pf[0][0:64, 0:128], b0[:, 0:64], idb[:])
                VE.wait(TE)
                VE.op("tensor_copy", b1[0:64, 0:128], pf[0][0:64, 0:128])
                TE.wait(VE)
                TE.op("matmul", pb[1][:, 0:512], b1[0:64, 0:128], woT[:, 0:512], start=True, stop=True)
                TE.op("matmul", pb[2][:, 0:512], b1[0:64, 0:128], woT[:, 512:1024], start=True, stop=True)
                VE.wait(TE)
                VE.op("tensor_tensor", resid[:, i * 1024:i * 1024 + 512], resid[:, i * 1024:i * 1024 + 512], pb[1][:, 0:512], OP.add)
                VE.op("tensor_tensor", resid[:, i * 1024 + 512:i * 1024 + 1024], resid[:, i * 1024 + 512:i * 1024 + 1024], pb[2][:, 0:512], OP.add)
                VE.op("scalar_tensor_tensor", ns0c[0:1, :], ns0c[0:1, :], pcr[0:1, i:i + 1], rcr[0:1, i * 64:(i + 1) * 64], OP.mult, OP.add)
                VE.wait(VE)
                TE.wait(VE)

        full_barrier()
        cx.free_pool(aa)
        def attn_b_emit():
            SCALE_QK = float(64.0 ** -0.25)
            bbP = []
            Gtot = cx.sb("Gtot", [128, H * 66], F32, pool=bbP)
            Sext = cx.sb("Sext", [128, H * 66], F32, pool=bbP)
            Sexb = cx.sb("Sexb", [128, 66], BF16, pool=bbP)
            gamr = cx.sb("gamr", [128, 1024], BF16, pool=bbP)
            betr = cx.sb("betr", [128, 1024], BF16, pool=bbP)
            b2r = cx.sb("b2r", [128, 1024], F32, pool=bbP)
            b1c = cx.sb("b1c", [128, 32], F32, pool=bbP)
            omgT = cx.sb("omgT", [128, 128], BF16, pool=bbP)

            for src, dst, nr, ncs in ((bWq, bWq_sc, 8, 1), (bWk, bWk_sc, 8, 1), (bWv, bWv_sc, 8, 1),
                                      (bWo, bWo_sc, 8, 1), (bW1, bW1_sc, 8, 4), (bW2, bW2_sc, 32, 1)):
                w = src.rearrange("(a p) (b q) -> a b p q", p=128, q=1024)
                d2 = dst.rearrange("(a p) (b q) -> a b p q", p=128, q=1024)
                for r in range(nr):
                    for c_ in range(ncs):
                        SY.wait(VE, GP)
                        SY.dma(wtmp[:], w[r, c_])
                        VE.wait(SY)
                        VE.op("tensor_copy", wtmpb[:], wtmp[:])
                        GP.wait(VE)
                        GP.dma(d2[r, c_], wtmpb[:])
            SY.wait(VE, GP, TE)
            SY.dma(wtmp[:, 0:64], omega)
            VE.wait(SY)
            VE.wait(GP)
            VE.op("tensor_copy", wtmpb[:, 0:64], wtmp[:, 0:64])
            TE.wait(VE)
            TE.op("transpose", pf[0][0:64, 0:128], wtmpb[:, 0:64], idb[:])
            VE.wait(TE)
            VE.op("memset", omgT[:], 0.0)
            VE.wait(VE)
            VE.op("tensor_copy", omgT[0:64, :], pf[0][0:64, 0:128])
            VE.op("memset", omgT[64:65, :], -0.5)
            replicate(gamr[:, 0:512], gam[0:1, 0:512], 512)
            replicate(gamr[:, 512:1024], gam[0:1, 512:1024], 512)
            replicate(betr[:, 0:512], bet[0:1, 0:512], 512)
            replicate(betr[:, 512:1024], bet[0:1, 512:1024], 512)
            replicate(b2r[:, 0:512], bb2[0:1, 0:512], 512)
            replicate(b2r[:, 512:1024], bb2[0:1, 512:1024], 512)
            for rr in range(4):
                SY.wait(VE, TE)
                SY.dma(wtmp[0:1, :], bb1[0:1, rr * 1024:(rr + 1) * 1024])
                TE.wait(SY)
                for c8 in range(8):
                    ch = rr * 8 + c8
                    TE.op("transpose", pb[5][:, 0:1], wtmp[0:1, c8 * 128:(c8 + 1) * 128], idf[0:1, 0:1])
                    VE.wait(TE)
                    VE.op("tensor_copy", b1c[:, ch:ch + 1], pb[5][:, 0:1])
                    TE.wait(VE)
            full_barrier()
            for i in range(NT):
                xt = resid[:, i * 1024:(i + 1) * 1024]
                SC.wait(VE)
                SC.op("activation", t0[:, 0:512], xt[:, 0:512], AF.Square, bias=0.0, scale=1.0, accum_out=cc[0][:])
                SC.op("activation", t1[:, 0:512], xt[:, 512:1024], AF.Square, bias=0.0, scale=1.0, accum_out=cc[1][:])
                VE.op("tensor_reduce", cc[2][:], xt, axis=AX, op=OP.add)
                VE.wait(SC)
                VE.wait(VE)
                VE.op("tensor_tensor", cc[3][:], cc[0][:], cc[1][:], OP.add)
                VE.op("tensor_scalar", cc[2][:], cc[2][:], 1.0 / 1024.0, None, OP.mult)
                SC.wait(VE)
                SC.op("activation", cc[4][:], cc[2][:], AF.Square, bias=0.0, scale=1.0)
                VE.wait(SC)
                VE.op("tensor_scalar", cc[3][:], cc[3][:], 1.0 / 1024.0, None, OP.mult)
                VE.wait(VE)
                VE.op("tensor_tensor", cc[3][:], cc[3][:], cc[4][:], OP.subtract)
                VE.wait(VE)
                VE.op("tensor_scalar", cc[5][:], cc[3][:], 1e-5, None, OP.add)
                SC.wait(VE)
                SC.op("activation", cc[4][:], cc[5][:], AF.Sqrt, bias=0.0, scale=1.0)
                VE.wait(SC)
                VE.op("reciprocal", stats1[:, 2 * i:2 * i + 1], cc[4][:])
                VE.wait(VE)
                VE.op("tensor_tensor", cc[4][:], cc[2][:], stats1[:, 2 * i:2 * i + 1], OP.mult)
                VE.wait(VE)
                VE.op("tensor_scalar", stats1[:, 2 * i + 1:2 * i + 2], cc[4][:], -1.0, None, OP.mult)
            VE.op("memset", Gtot[:], 0.0)

            bb1p = []
            l3f = cx.sb("l3f", [128, 8 * 1024], BF16, pool=bb1p)
            qaug = cx.sb("qaug", [128, 1024], BF16, pool=bb1p)
            wqk8 = [cx.sb(f"wqk8{j}", [128, 1024], BF16, pool=bb1p) for j in range(2)]
            vext = cx.sb("vext", [128, 66], BF16, pool=bb1p)
            vtm = cx.sb("vtm", [128, 128], BF16, pool=bb1p)
            VE.op("memset", vext[:], 0.0)
            VE.wait(VE)
            VE.op("memset", vext[:, 64:65], 1.0)
            for g in range(4):
                for ii in range(8):
                    i = g * 8 + ii
                    xt = resid[:, i * 1024:(i + 1) * 1024]
                    VE.wait(TE)
                    VE.op("tensor_scalar", u0[:], xt, stats1[:, 2 * i:2 * i + 1], stats1[:, 2 * i + 1:2 * i + 2], OP.mult, OP.add)
                    VE.wait(VE)
                    VE.op("tensor_copy", wtmpb[:], u0[:])
                    TE.wait(VE)
                    for d in range(ND):
                        TE.op("transpose", pf[d % 2][:, 0:128], wtmpb[:, d * 128:(d + 1) * 128], idb[:])
                        VE.wait(TE)
                        VE.op("tensor_copy", l3f[:, d * 1024 + ii * 128: d * 1024 + ii * 128 + 128], pf[d % 2][:, 0:128])
                        TE.wait(VE)
                for wi, (wsc, kind) in enumerate(((bWv_sc, "v"), (bWk_sc, "k"), (bWq_sc, "q"))):
                    for o8 in range(8):
                        pA = pb[0] if o8 % 2 == 0 else pb[2]
                        pB = pb[1] if o8 % 2 == 0 else pb[3]
                        wpar = o8 % 2
                        SY.wait(TE)
                        SY.dma(wqk8[wpar][:].rearrange("p (a q) -> p a q", q=128),
                               wsc[:, o8 * 128:(o8 + 1) * 128].rearrange("(a p) q -> p a q", p=128))
                        TE.wait(SY)
                        for d in range(ND):
                            TE.op("matmul", pA[:, 0:512], wqk8[wpar][:, d * 128:(d + 1) * 128], l3f[:, d * 1024:d * 1024 + 512], start=(d == 0), stop=(d == ND - 1))
                            TE.op("matmul", pB[:, 0:512], wqk8[wpar][:, d * 128:(d + 1) * 128], l3f[:, d * 1024 + 512:d * 1024 + 1024], start=(d == 0), stop=(d == ND - 1))
                        VE.wait(TE)
                        if kind == "v":
                            VE.op("tensor_copy", b0[:, 0:512], pA[:, 0:512])
                            VE.op("tensor_copy", b1[:, 0:512], pB[:, 0:512])
                            TE.wait(VE)
                            for tt in range(8):
                                src = b0 if tt < 4 else b1
                                TE.op("transpose", pf[tt % 2][:, 0:128], src[:, (tt % 4) * 128:(tt % 4) * 128 + 128], idb[:])
                                VE.wait(TE)
                                VE.op("tensor_copy", vtm[:], pf[tt % 2][:, 0:128])
                                GP.wait(VE)
                                GP.dma(v_sc[g * 1024 + tt * 128: g * 1024 + (tt + 1) * 128, o8 * 128:(o8 + 1) * 128], vtm[:])
                                VE.wait(GP)
                                TE.wait(VE)
                        else:
                            VE.op("tensor_scalar", u0[:, 0:512], pA[:, 0:512], SCALE_QK, None, OP.mult)
                            VE.op("tensor_scalar", u0[:, 512:1024], pB[:, 0:512], SCALE_QK, None, OP.mult)
                            SC.wait(VE)
                            SC.op("activation", u1[:], u0[:], AF.Square, bias=0.0, scale=1.0)
                            VE.wait(SC)
                            VE.op("tensor_copy", b0[:, 0:512], u1[:, 0:512])
                            VE.op("tensor_copy", b1[:, 0:512], u1[:, 512:1024])
                            TE.wait(VE)
                            for hh in range(2):
                                TE.op("matmul", pb[4][0:1, 0:512], onesb[hh * 64:hh * 64 + 64, 0:1], b0[hh * 64:hh * 64 + 64, 0:512], start=True, stop=True)
                                TE.op("matmul", pb[5][0:1, 0:512], onesb[hh * 64:hh * 64 + 64, 0:1], b1[hh * 64:hh * 64 + 64, 0:512], start=True, stop=True)
                                VE.wait(TE)
                                VE.op("tensor_copy", qaug[0:64, :], u0[hh * 64:hh * 64 + 64, :])
                                VE.wait(VE)
                                VE.op("tensor_copy", qaug[64:65, 0:512], pb[4][0:1, 0:512])
                                VE.op("tensor_copy", qaug[64:65, 512:1024], pb[5][0:1, 0:512])
                                VE.wait(VE)
                                TE.wait(VE)
                                h = o8 * 2 + hh
                                for tt in range(8):
                                    TE.op("matmul", pb[4 if tt % 2 == 0 else 5][:, 0:128], qaug[0:65, tt * 128:(tt + 1) * 128], omgT[0:65, :], start=True, stop=True)
                                    VE.wait(TE)
                                    VE.op("tensor_reduce", cc[0][:], pb[4 if tt % 2 == 0 else 5][:, 0:128], axis=AX, op=OP.max)
                                    VE.wait(VE)
                                    VE.op("tensor_scalar", cc[1][:], cc[0][:], -1.0, None, OP.mult)
                                    SC.wait(VE)
                                    SC.op("activation", u1[:, tt * 128:(tt + 1) * 128], pb[4 if tt % 2 == 0 else 5][:, 0:128], AF.Exp, bias=cc[1][:], scale=1.0)
                                    TE.wait(SC)
                                VE.wait(SC)
                                VE.op("tensor_scalar", u1[:], u1[:], float(1.0 / np.sqrt(128.0)), 1e-6, OP.mult, OP.add)
                                VE.wait(VE)
                                VE.wait(GP)
                                VE.op("tensor_copy", wtmpb[:], u1[:])
                                GP.wait(VE)
                                psc = kphi_sc if kind == "k" else qphi_sc
                                for tt in range(8):
                                    GP.dma(psc[h, g * 1024 + tt * 128: g * 1024 + (tt + 1) * 128, :], wtmpb[:, tt * 128:(tt + 1) * 128])
                                VE.wait(GP)
                                if kind == "k":
                                    for tt in range(8):
                                        SY.wait(TE, VE)
                                        SY.dma(vtm[:, 0:64], v_sc[g * 1024 + tt * 128: g * 1024 + (tt + 1) * 128, h * 64:(h + 1) * 64])
                                        VE.wait(SY)
                                        VE.op("tensor_copy", vext[:, 0:64], vtm[:, 0:64])
                                        VE.wait(VE)
                                        TE.wait(VE)
                                        TE.op("matmul", pb[4][:, 0:66], wtmpb[:, tt * 128:(tt + 1) * 128], vext[:], start=True, stop=True)
                                        VE.wait(TE)
                                        VE.op("tensor_tensor", Gtot[:, h * 66:(h + 1) * 66], Gtot[:, h * 66:(h + 1) * 66], pb[4][:, 0:66], OP.add)
                                        VE.wait(VE)
                                        TE.wait(VE)
            full_barrier()
            cx.free_pool(bb1p)
            GP.wait(VE)
            tap(GP, "qph0", qphi_sc[0])
            tap(GP, "kph0", kphi_sc[0])
            tap(GP, "qph7", qphi_sc[7])
            tap(GP, "vsc", v_sc)
            tap(GP, "gtotT", Gtot[:])

            GP.wait(VE)
            GP.dma(ccG_in, Gtot[:])
            GP.wait(GP)
            tokG = GP.coll("AllGather", OP.bypass,
                           replica_groups=[[0, 1, 2, 3], [4, 5, 6, 7]],
                           ins=[ccG_in.opt()], outs=[ccG_out.opt()])
            GP.wait_tok(tokG)
            VE.op("memset", Sext[:], 0.0)
            bb2p = []
            for r in range(4):
                GP.wait(VE, TE)
                GP.dma(wtmp[:], ccG_out[r * 128:(r + 1) * 128, 0:1024])
                GP.dma(t2[:, 0:32], ccG_out[r * 128:(r + 1) * 128, 1024:1056])
                VE.wait(GP)
                VE.op("scalar_tensor_tensor", Sext[:, 0:1024], wtmp[:], rankc[:, r:r + 1], Sext[:, 0:1024], OP.mult, OP.add)
                VE.op("scalar_tensor_tensor", Sext[:, 1024:1056], t2[:, 0:32], rankc[:, r:r + 1], Sext[:, 1024:1056], OP.mult, OP.add)
                VE.wait(VE)

            GP.wait(VE)
            tap(GP, "sextT", Sext[:])
            qpb = cx.sb("qpb", [128, 2 * 128], BF16, pool=bb2p)
            kpb = cx.sb("kpb", [128, 2 * 128], BF16, pool=bb2p)
            vtb = cx.sb("vtb", [128, 1024], BF16, pool=bb2p)
            woB = cx.sb("woB", [128, 8 * 1024], BF16, pool=bb2p)
            hfm = cx.sb("hfm", [128, 8 * 512], BF16, pool=bb2p)
            mid = cx.sb("mid", [128, 4 * 256], BF16, pool=bb2p)
            vexb = b2[:, 384:450]
            qpM = b2[:, 0:128]
            kpM = b2[:, 128:256]
            amb = b2[:, 256:384]
            w2t = [cx.sb(f"w2t{j}", [128, 1024], BF16, pool=bb2p) for j in range(2)]
            VE.op("memset", vexb, 0.0)
            VE.wait(VE)
            VE.op("memset", vexb[:, 64:65], 1.0)
            for r in range(8):
                SY.wait(VE, GP)
                SY.dma(woB[:, r * 1024:(r + 1) * 1024], bWo_sc[r * 128:(r + 1) * 128, :])

            for j in range(NT):
                SY.wait(TE, VE)
                SY.dma(qpb[:, 0:128], qphi_sc[0, j * 128:(j + 1) * 128, :])
                SY.dma(kpb[:, 0:128], kphi_sc[0, j * 128:(j + 1) * 128, :])
                SY.dma(vtb[:], v_sc[j * 128:(j + 1) * 128, :])
                TE.wait(SY)
                VE.wait(SY)
                for h in range(H):
                    hp = h % 2
                    if h + 1 < H:
                        SY.wait(TE)
                        SY.dma(qpb[:, (1 - hp) * 128:(2 - hp) * 128], qphi_sc[h + 1, j * 128:(j + 1) * 128, :])
                        SY.dma(kpb[:, (1 - hp) * 128:(2 - hp) * 128], kphi_sc[h + 1, j * 128:(j + 1) * 128, :])
                    TE.wait(SY)
                    TE.op("transpose", pf[0][:, 0:128], qpb[:, hp * 128:(hp + 1) * 128], idb[:])
                    TE.op("transpose", pf[1][:, 0:128], kpb[:, hp * 128:(hp + 1) * 128], idb[:])
                    VE.wait(TE)
                    VE.op("tensor_copy", qpM, pf[0][:, 0:128])
                    VE.op("tensor_copy", kpM, pf[1][:, 0:128])
                    VE.op("tensor_copy", vexb[:, 0:64], vtb[:, h * 64:(h + 1) * 64])
                    VE.op("tensor_copy", Sexb[:], Sext[:, h * 66:(h + 1) * 66])
                    VE.wait(VE)
                    TE.wait(VE)
                    TE.op("matmul", pb[2][:, 0:128], kpM, qpM, start=True, stop=True)
                    VE.wait(TE)
                    VE.op("tensor_tensor", amb, pb[2][:, 0:128], mg0i[:], OP.mult)
                    VE.wait(VE)
                    TE.wait(VE)
                    TE.op("matmul", pb[3][:, 0:66], amb, vexb, start=True, stop=False)
                    TE.op("matmul", pb[3][:, 0:66], qpM, Sexb[:], start=False, stop=True)
                    TE.op("matmul", pb[4][:, 0:66], kpb[:, hp * 128:(hp + 1) * 128], vexb, start=True, stop=True)
                    VE.wait(TE)
                    VE.op("tensor_scalar", cc[0][:], pb[3][:, 64:65], 1e-6, None, OP.add)
                    VE.op("tensor_tensor", Sext[:, h * 66:(h + 1) * 66], Sext[:, h * 66:(h + 1) * 66], pb[4][:, 0:66], OP.add)
                    VE.wait(VE)
                    VE.op("reciprocal", cc[1][:], cc[0][:])
                    VE.wait(VE)
                    VE.op("tensor_scalar", t0[:, 0:64], pb[3][:, 0:64], cc[1][:], None, OP.mult)
                    VE.wait(VE)
                    VE.op("tensor_copy", b0[:, 0:64], t0[:, 0:64])
                    TE.wait(VE)
                    TE.op("transpose", pf[0][0:64, 0:128], b0[:, 0:64], idb[:])
                    VE.wait(TE)
                    hb = (h % 2) * 64
                    VE.op("tensor_copy", b1[hb:hb + 64, 0:128], pf[0][0:64, 0:128])
                    TE.wait(VE)
                    rhsWo = woB[:, (h // 2) * 1024:(h // 2 + 1) * 1024][hb:hb + 64, :]
                    TE.op("matmul", pb[0][:, 0:512], b1[hb:hb + 64, 0:128], rhsWo[:, 0:512], start=(h == 0), stop=(h == H - 1))
                    TE.op("matmul", pb[1][:, 0:512], b1[hb:hb + 64, 0:128], rhsWo[:, 512:1024], start=(h == 0), stop=(h == H - 1))
                VE.wait(TE)
                xt = resid[:, j * 1024:(j + 1) * 1024]
                VE.op("tensor_scalar", u0[:], xt, stats1[:, 2 * j:2 * j + 1], stats1[:, 2 * j + 1:2 * j + 2], OP.mult, OP.add)
                VE.wait(VE)
                VE.op("tensor_tensor", u0[:, 0:512], u0[:, 0:512], pb[0][:, 0:512], OP.add)
                VE.op("tensor_tensor", u0[:, 512:1024], u0[:, 512:1024], pb[1][:, 0:512], OP.add)
                VE.wait(VE)
                TE.wait(VE)
                SC.wait(VE)
                SC.op("activation", t0[:, 0:512], u0[:, 0:512], AF.Square, bias=0.0, scale=1.0, accum_out=cc[0][:])
                SC.op("activation", t1[:, 0:512], u0[:, 512:1024], AF.Square, bias=0.0, scale=1.0, accum_out=cc[1][:])
                VE.op("tensor_reduce", cc[2][:], u0[:], axis=AX, op=OP.add)
                VE.wait(SC)
                VE.wait(VE)
                VE.op("tensor_tensor", cc[3][:], cc[0][:], cc[1][:], OP.add)
                VE.op("tensor_scalar", cc[2][:], cc[2][:], 1.0 / 1024.0, None, OP.mult)
                SC.wait(VE)
                SC.op("activation", cc[4][:], cc[2][:], AF.Square, bias=0.0, scale=1.0)
                VE.wait(SC)
                VE.op("tensor_scalar", cc[3][:], cc[3][:], 1.0 / 1024.0, None, OP.mult)
                VE.wait(VE)
                VE.op("tensor_tensor", cc[3][:], cc[3][:], cc[4][:], OP.subtract)
                VE.wait(VE)
                VE.op("tensor_scalar", cc[5][:], cc[3][:], 1e6, None, OP.add)
                SC.wait(VE)
                SC.op("activation", cc[4][:], cc[5][:], AF.Sqrt, bias=0.0, scale=1.0)
                VE.wait(SC)
                VE.op("reciprocal", cc[5][:], cc[4][:])
                VE.wait(VE)
                VE.op("tensor_tensor", cc[3][:], cc[2][:], cc[5][:], OP.mult)
                VE.wait(VE)
                VE.op("tensor_scalar", cc[3][:], cc[3][:], -1.0, None, OP.mult)
                VE.wait(VE)
                VE.op("tensor_scalar", u1[:], u0[:], cc[5][:], cc[3][:], OP.mult, OP.add)
                VE.wait(VE)
                VE.op("tensor_tensor", u1[:], u1[:], gamr[:], OP.mult)
                VE.wait(VE)
                VE.op("tensor_tensor", u1[:], u1[:], betr[:], OP.add)
                VE.wait(VE)
                VE.op("tensor_copy", wtmpb[:], u1[:])
                TE.wait(VE)
                for d in range(ND):
                    TE.op("transpose", pf[d % 2][:, 0:128], wtmpb[:, d * 128:(d + 1) * 128], idb[:])
                    VE.wait(TE)
                    VE.op("tensor_copy", hfm[:, d * 512 + (j % 4) * 128: d * 512 + (j % 4) * 128 + 128], pf[d % 2][:, 0:128])
                    TE.wait(VE)
                if j % 4 == 3:
                    gp512 = j // 4
                    for th in range(2):
                        tb = th * 256
                        for q4 in range(8):
                            for fc8 in range(4):
                                fch = q4 * 4 + fc8
                                wpar = fch % 2
                                SY.wait(TE)
                                SY.dma(w2t[wpar][:].rearrange("p (a q) -> p a q", q=128),
                                       bW1_sc[:, fch * 128:(fch + 1) * 128].rearrange("(a p) q -> p a q", p=128))
                                TE.wait(SY)
                                for d in range(ND):
                                    TE.op("matmul", pb[2][:, 0:256], w2t[wpar][:, d * 128:(d + 1) * 128], hfm[:, d * 512 + tb: d * 512 + tb + 256], start=(d == 0), stop=(d == ND - 1))
                                SC.wait(TE)
                                SC.op("activation", t0[:, 0:256], pb[2][:, 0:256], AF.Relu, bias=b1c[:, fch:fch + 1], scale=1.0)
                                VE.wait(SC)
                                VE.op("tensor_copy", mid[:, fc8 * 256:(fc8 + 1) * 256], t0[:, 0:256])
                                TE.wait(VE)
                            for fc8 in range(4):
                                fch = q4 * 4 + fc8
                                wpar = fch % 2
                                SY.wait(TE)
                                SY.dma(w2t[wpar][:], bW2_sc[fch * 128:(fch + 1) * 128, :])
                                TE.wait(SY)
                                for ts in range(2):
                                    pA = pb[3] if ts == 0 else pb[4]
                                    pB = pb[5] if ts == 0 else pf[0][:].bitcast(F32)
                                    TE.op("matmul", pA[:, 0:512],
                                          mid[:, fc8 * 256 + ts * 128: fc8 * 256 + (ts + 1) * 128], w2t[wpar][:, 0:512],
                                          start=(fch == 0), stop=(fch == 31))
                                    TE.op("matmul", pB[:, 0:512],
                                          mid[:, fc8 * 256 + ts * 128: fc8 * 256 + (ts + 1) * 128], w2t[wpar][:, 512:1024],
                                          start=(fch == 0), stop=(fch == 31))
                        VE.wait(TE)
                        pmap = [(pb[3], pb[5]), (pb[4], pf[0][:].bitcast(F32))]
                        for ts in range(2):
                            ti_ = gp512 * 4 + th * 2 + ts
                            r0 = resid[:, ti_ * 1024: ti_ * 1024 + 512]
                            r1 = resid[:, ti_ * 1024 + 512: (ti_ + 1) * 1024]
                            pA, pB = pmap[ts]
                            VE.op("tensor_tensor", t0[:, 0:512], pA[:, 0:512], b2r[:, 0:512], OP.add)
                            VE.op("tensor_tensor", t1[:, 0:512], pB[:, 0:512], b2r[:, 512:1024], OP.add)
                            VE.wait(VE)
                            VE.op("tensor_tensor", r0, r0, t0[:, 0:512], OP.add)
                            VE.op("tensor_tensor", r1, r1, t1[:, 0:512], OP.add)
                            VE.wait(VE)
                        TE.wait(VE)
            full_barrier()
            cx.free_pool(bb2p)
            cx.free_pool(bbP)
        if "ffn1" in PHASES or "ffn2" in PHASES:
            VE.op("memset", mgc[:], MAGIC)
            VE.op("memset", nmgc[:], -MAGIC)

            def dequant(src, nrows, ncols, trans_dst):
                nb = 32
                for r in range(nrows // 128):
                    for cs in range(ncols // 1024):
                        SY.wait(TE, VE, SC)
                        SY.dma(wtmp[:], src[r * 128:(r + 1) * 128, cs * 1024:(cs + 1) * 1024])
                        VE.wait(SY)
                        w3 = wtmp[:].rearrange("p (g k) -> p g k", g=nb)
                        VE.op("tensor_reduce", t0[:, 0:nb], w3, axis=AX, op=OP.max)
                        VE.op("tensor_reduce", t0[:, nb:2 * nb], w3, axis=AX, op=OP.min)
                        VE.wait(VE)
                        VE.op("tensor_scalar", t0[:, nb:2 * nb], t0[:, nb:2 * nb], -1.0, None, OP.mult)
                        VE.wait(VE)
                        VE.op("tensor_tensor", t0[:, 0:nb], t0[:, 0:nb], t0[:, nb:2 * nb], OP.max)
                        VE.wait(VE)
                        VE.op("tensor_scalar", t0[:, 0:nb], t0[:, 0:nb], 1.0 / 7.0, 1e-12, OP.mult, OP.max)
                        VE.wait(VE)
                        VE.op("reciprocal", t0[:, nb:2 * nb], t0[:, 0:nb])
                        VE.wait(VE)
                        s3d = t0[:, nb:2 * nb].unsqueeze(-1).broadcast_to([128, nb, 32])
                        VE.op("tensor_tensor", u0[:].rearrange("p (g k) -> p g k", g=nb), w3, s3d, OP.mult)
                        SC.wait(VE)
                        SC.op("activation", u0[:], u0[:], AF.Identity, bias=mgc[:], scale=1.0)
                        SC.wait(SC)
                        SC.op("activation", u0[:], u0[:], AF.Identity, bias=nmgc[:], scale=1.0)
                        VE.wait(SC)
                        s3e = t0[:, 0:nb].unsqueeze(-1).broadcast_to([128, nb, 32])
                        VE.op("tensor_tensor", u1[:].rearrange("p (g k) -> p g k", g=nb), u0[:].rearrange("p (g k) -> p g k", g=nb), s3e, OP.mult)
                        VE.wait(VE)
                        VE.op("tensor_copy", wtmpb[:], u1[:])
                        TE.wait(VE)
                        for dch in range(8):
                            TE.op("transpose", pf[dch % 2][:, 0:128], wtmpb[:, dch * 128:(dch + 1) * 128], idb[:])
                            VE.wait(TE)
                            VE.op("tensor_copy", b0[:, 0:128], pf[dch % 2][:, 0:128])
                            GP.wait(VE)
                            GP.dma(trans_dst[cs * 1024 + dch * 128: cs * 1024 + (dch + 1) * 128, r * 128:(r + 1) * 128], b0[:, 0:128])
                            VE.wait(GP)
                            TE.wait(VE)

            dequant(w_up, FQ, D, upT_sc)
            dequant(w_gate, FQ, D, gateT_sc)
            dequant(w_down, D, FQ, wdT_sc)
            full_barrier()

            def k1rep(dst, src11):
                VE.op("tensor_copy", b0[0:1, 0:1], src11)
                TE.wait(VE)
                TE.op("matmul", pb[5][:, 0:1], onesb[0:1, :], b0[0:1, 0:1], start=True, stop=True)
                VE.wait(TE)
                VE.op("tensor_copy", dst, pb[5][:, 0:1])
                TE.wait(VE)

            def cross_part_max(col, dst11):
                TE.wait(VE)
                TE.op("transpose", pb[5][0:1, 0:128], col, idf[:])
                VE.wait(TE)
                VE.op("tensor_reduce", dst11, pb[5][0:1, 0:128], axis=AX, op=OP.max)
                VE.wait(VE)
                TE.wait(VE)

            def ffn_emit(tag):
                ff = []
                ofm = cx.sb("ofm" + tag, [128, 24 * 512], BF16, pool=ff)
                wt = [cx.sb(f"wt{tag}{j}", [128, 1024], BF16, pool=ff) for j in range(2)]
                mxc = cx.sb("mxc" + tag, [128, 4], F32, pool=ff)
                scl = cx.sb("scl" + tag, [1, 16], F32, pool=ff)
                sclr = [cx.sb(f"sclr{tag}{j}", [128, 1], F32, pool=ff) for j in range(6)]
                ffx = []
                xqf = cx.sb("xqf" + tag, [128, 8 * 1024], BF16, pool=ffx)
                wtg = [cx.sb(f"wtg{tag}{j}", [128, 1024], BF16, pool=ffx) for j in range(2)]
                co = 0 if tag == "f1" else 4
                VE.op("memset", mxc[:], 0.0)
                for i in range(NT):
                    xt = resid[:, i * 1024:(i + 1) * 1024]
                    SC.wait(VE)
                    SC.op("activation", t0[:, 0:512], xt[:, 0:512], AF.Square, bias=0.0, scale=1.0, accum_out=cc[0][:])
                    SC.op("activation", t1[:, 0:512], xt[:, 512:1024], AF.Square, bias=0.0, scale=1.0, accum_out=cc[1][:])
                    VE.op("tensor_reduce", cc[2][:], xt, axis=AX, op=OP.add)
                    VE.wait(SC)
                    VE.wait(VE)
                    VE.op("tensor_tensor", cc[3][:], cc[0][:], cc[1][:], OP.add)
                    VE.op("tensor_scalar", cc[2][:], cc[2][:], 1.0 / 1024.0, None, OP.mult)
                    SC.wait(VE)
                    SC.op("activation", cc[4][:], cc[2][:], AF.Square, bias=0.0, scale=1.0)
                    VE.wait(SC)
                    VE.op("tensor_scalar", cc[3][:], cc[3][:], 1.0 / 1024.0, None, OP.mult)
                    VE.wait(VE)
                    VE.op("tensor_tensor", cc[3][:], cc[3][:], cc[4][:], OP.subtract)
                    VE.wait(VE)
                    VE.op("tensor_scalar", cc[5][:], cc[3][:], 1e-5, None, OP.add)
                    SC.wait(VE)
                    SC.op("activation", cc[4][:], cc[5][:], AF.Sqrt, bias=0.0, scale=1.0)
                    VE.wait(SC)
                    VE.op("reciprocal", stats1[:, 2 * i:2 * i + 1], cc[4][:])
                    VE.wait(VE)
                    VE.op("tensor_tensor", cc[4][:], cc[2][:], stats1[:, 2 * i:2 * i + 1], OP.mult)
                    VE.wait(VE)
                    VE.op("tensor_scalar", stats1[:, 2 * i + 1:2 * i + 2], cc[4][:], -1.0, None, OP.mult)
                    VE.op("tensor_scalar", u0[:], xt, stats1[:, 2 * i:2 * i + 1], stats1[:, 2 * i + 1:2 * i + 2], OP.mult, OP.add)
                    VE.wait(VE)
                    VE.op("tensor_reduce", cc[0][:], u0[:], axis=AX, op=OP.max)
                    VE.op("tensor_reduce", cc[1][:], u0[:], axis=AX, op=OP.min)
                    VE.wait(VE)
                    VE.op("tensor_scalar", cc[1][:], cc[1][:], -1.0, None, OP.mult)
                    VE.wait(VE)
                    VE.op("tensor_tensor", cc[0][:], cc[0][:], cc[1][:], OP.max)
                    VE.wait(VE)
                    VE.op("tensor_tensor", mxc[:, 0:1], mxc[:, 0:1], cc[0][:], OP.max)
                    VE.wait(VE)
                cross_part_max(mxc[:, 0:1], scl[0:1, 0:1])
                GP.wait(VE)
                GP.dma(cc_mx_in[0:1, 0:1], scl[0:1, 0:1])
                GP.wait(GP)
                tokM = GP.coll("AllReduce", OP.max, replica_groups=[list(range(8))],
                               ins=[cc_mx_in.opt()], outs=[cc_mx_out[co].opt()])
                GP.wait_tok(tokM)
                GP.dma(scl[0:1, 0:1], cc_mx_out[co][0:1, 0:1])
                VE.wait(GP)
                VE.op("tensor_scalar", scl[0:1, 1:2], scl[0:1, 0:1], 1.0 / 32767.0, 1e-12, OP.mult, OP.max)
                VE.wait(VE)
                VE.op("reciprocal", scl[0:1, 2:3], scl[0:1, 1:2])
                VE.wait(VE)
                k1rep(sclr[0][:], scl[0:1, 2:3])

                VE.op("memset", mxc[:, 1:3], 0.0)
                for g in range(4):
                    for ii in range(8):
                        i = g * 8 + ii
                        xt = resid[:, i * 1024:(i + 1) * 1024]
                        VE.wait(TE)
                        VE.op("tensor_scalar", u0[:], xt, stats1[:, 2 * i:2 * i + 1], stats1[:, 2 * i + 1:2 * i + 2], OP.mult, OP.add)
                        SC.wait(VE)
                        SC.op("activation", u1[:], u0[:], AF.Identity, bias=mgc[:], scale=sclr[0][:])
                        SC.wait(SC)
                        SC.op("activation", u1[:], u1[:], AF.Identity, bias=nmgc[:], scale=1.0)
                        VE.wait(SC)
                        VE.op("tensor_copy", wtmpb[:], u1[:])
                        TE.wait(VE)
                        for d in range(ND):
                            TE.op("transpose", pf[d % 2][:, 0:128], wtmpb[:, d * 128:(d + 1) * 128], idb[:])
                            VE.wait(TE)
                            VE.op("tensor_copy", xqf[:, d * 1024 + ii * 128: d * 1024 + ii * 128 + 128], pf[d % 2][:, 0:128])
                            TE.wait(VE)
                    for ht in range(NF):
                        pu = [pb[0], pb[1]] if ht % 2 == 0 else [pb[4], pb[5]]
                        pg_ = [pb[2], pb[3]] if ht % 2 == 0 else [pf[0][:].bitcast(F32), pf[1][:].bitcast(F32)]
                        wpar = ht % 2
                        SY.wait(TE)
                        SY.dma(wt[wpar][:].rearrange("p (a q) -> p a q", q=128),
                               upT_sc[:, ht * 128:(ht + 1) * 128].rearrange("(a p) q -> p a q", p=128))
                        SY.dma(wtg[wpar][:].rearrange("p (a q) -> p a q", q=128),
                               gateT_sc[:, ht * 128:(ht + 1) * 128].rearrange("(a p) q -> p a q", p=128))
                        TE.wait(SY)
                        for d in range(ND):
                            TE.op("matmul", pu[0][:, 0:512], wt[wpar][:, d * 128:(d + 1) * 128], xqf[:, d * 1024:d * 1024 + 512], start=(d == 0), stop=(d == ND - 1))
                            TE.op("matmul", pu[1][:, 0:512], wt[wpar][:, d * 128:(d + 1) * 128], xqf[:, d * 1024 + 512:d * 1024 + 1024], start=(d == 0), stop=(d == ND - 1))
                        for d in range(ND):
                            TE.op("matmul", pg_[0][:, 0:512], wtg[wpar][:, d * 128:(d + 1) * 128], xqf[:, d * 1024:d * 1024 + 512], start=(d == 0), stop=(d == ND - 1))
                            TE.op("matmul", pg_[1][:, 0:512], wtg[wpar][:, d * 128:(d + 1) * 128], xqf[:, d * 1024 + 512:d * 1024 + 1024], start=(d == 0), stop=(d == ND - 1))
                        VE.wait(TE)
                        VE.wait(GP)
                        VE.op("tensor_copy", u0[:, 0:512], pu[0][:, 0:512])
                        VE.op("tensor_copy", u0[:, 512:1024], pu[1][:, 0:512])
                        VE.op("tensor_copy", u1[:, 0:512], pg_[0][:, 0:512])
                        VE.op("tensor_copy", u1[:, 512:1024], pg_[1][:, 0:512])
                        VE.wait(VE)
                        GP.wait(VE)
                        GP.dma(up_sc[ht * 128:(ht + 1) * 128, g * 1024:(g + 1) * 1024], u0[:])
                        GP.dma(gate_sc[ht * 128:(ht + 1) * 128, g * 1024:(g + 1) * 1024], u1[:])
                        for (src, cix) in ((u0, 1), (u1, 2)):
                            VE.op("tensor_reduce", cc[0][:], src[:], axis=AX, op=OP.max)
                            VE.op("tensor_reduce", cc[1][:], src[:], axis=AX, op=OP.min)
                            VE.wait(VE)
                            VE.op("tensor_scalar", cc[1][:], cc[1][:], -1.0, None, OP.mult)
                            VE.wait(VE)
                            VE.op("tensor_tensor", cc[0][:], cc[0][:], cc[1][:], OP.max)
                            VE.wait(VE)
                            VE.op("tensor_tensor", mxc[:, cix:cix + 1], mxc[:, cix:cix + 1], cc[0][:], OP.max)
                            VE.wait(VE)
                cross_part_max(mxc[:, 1:2], scl[0:1, 3:4])
                cross_part_max(mxc[:, 2:3], scl[0:1, 4:5])
                GP.wait(VE)
                GP.dma(cc_mx_in[0:1, 0:2], scl[0:1, 3:5])
                GP.wait(GP)
                tokU = GP.coll("AllReduce", OP.max, replica_groups=[list(range(8))],
                               ins=[cc_mx_in.opt()], outs=[cc_mx_out[co + 1].opt()])
                GP.wait_tok(tokU)
                GP.dma(scl[0:1, 3:5], cc_mx_out[co + 1][0:1, 0:2])
                VE.wait(GP)
                VE.op("tensor_scalar", scl[0:1, 5:7], scl[0:1, 3:5], scl[0:1, 1:2], None, OP.mult)
                VE.wait(VE)
                VE.op("tensor_scalar", scl[0:1, 5:7], scl[0:1, 5:7], 1.0 / 32767.0, 1e-12, OP.mult, OP.max)
                VE.wait(VE)
                VE.op("reciprocal", scl[0:1, 7:9], scl[0:1, 5:7])
                VE.wait(VE)
                VE.op("tensor_scalar", scl[0:1, 9:11], scl[0:1, 7:9], scl[0:1, 1:2], None, OP.mult)
                VE.wait(VE)
                k1rep(sclr[1][:], scl[0:1, 9:10])
                k1rep(sclr[2][:], scl[0:1, 10:11])
                k1rep(sclr[3][:], scl[0:1, 6:7])

                full_barrier()
                cx.free_pool(ffx)
                ua = [cx.sb(f"ua{tag}{j}", [128, 1024], F32, pool=ff) for j in range(2)]
                ub_ = [cx.sb(f"ub{tag}{j}", [128, 1024], F32, pool=ff) for j in range(2)]
                uc = [cx.sb(f"uc{tag}{j}", [128, 1024], F32, pool=ff) for j in range(2)]

                def pipe3(n, fload, fsc, fve, sy_waits_sc=False):
                    tok_sy = {}; tok_sc = {}; tok_ve = {}
                    for k in range(n + 2):
                        if 0 <= k - 2 < n:
                            i = k - 2
                            VE.wait_tok(tok_sc[i])
                            fve(i)
                            tok_ve[i] = VE.seal()
                        if k < n:
                            if k >= 2:
                                SY.wait_tok(tok_ve[k - 2])
                            if sy_waits_sc and k >= 1:
                                SY.wait_tok(tok_sc[k - 1])
                            fload(k)
                            tok_sy[k] = SY.seal()
                        if 0 <= k - 1 < n:
                            i = k - 1
                            SC.wait_tok(tok_sy[i])
                            if i >= 2:
                                SC.wait_tok(tok_ve[i - 2])
                            fsc(i)
                            tok_sc[i] = SC.seal()

                VE.op("memset", mxc[:, 3:4], 0.0)
                VE.seal()
                iters = [(ht, g) for ht in range(NF) for g in range(4)]

                def c1_load(ix):
                    ht, g = iters[ix]
                    SY.dma(ub_[ix % 2][:], gate_sc[ht * 128:(ht + 1) * 128, g * 1024:(g + 1) * 1024])

                def c1_sc(ix):
                    p = ix % 2
                    SC.op("activation", ua[p][:], ub_[p][:], AF.Identity, bias=mgc[:], scale=sclr[2][:])
                    SC.wait(SC)
                    SC.op("activation", ua[p][:], ua[p][:], AF.Identity, bias=nmgc[:], scale=1.0)
                    SC.wait(SC)
                    SC.op("activation", ub_[p][:], ua[p][:], AF.Sigmoid, bias=0.0, scale=sclr[3][:])
                    SC.wait(SC)
                    SC.op("activation", ub_[p][:], ub_[p][:], AF.Identity, bias=mgc[:], scale=65536.0)
                    SC.wait(SC)
                    SC.op("activation", ub_[p][:], ub_[p][:], AF.Identity, bias=nmgc[:], scale=1.0)

                def c1_ve(ix):
                    p = ix % 2
                    VE.op("tensor_scalar", ub_[p][:], ub_[p][:], 65535.0, None, OP.min)
                    VE.wait(VE)
                    VE.op("tensor_tensor", ua[p][:], ua[p][:], ub_[p][:], OP.mult)
                    VE.wait(VE)
                    VE.op("tensor_reduce", cc[0][:], ua[p][:], axis=AX, op=OP.max)
                    VE.op("tensor_reduce", cc[1][:], ua[p][:], axis=AX, op=OP.min)
                    VE.wait(VE)
                    VE.op("tensor_tensor", mxc[:, 3:4], mxc[:, 3:4], cc[0][:], OP.max)
                    VE.wait(VE)
                    VE.op("scalar_tensor_tensor", mxc[:, 3:4], cc[1][:], -1.0, mxc[:, 3:4], OP.mult, OP.max)

                SC.wait(VE)
                SY.wait(GP)
                pipe3(len(iters), c1_load, c1_sc, c1_ve)
                SC.wait(VE)
                VE.wait(SC)
                cross_part_max(mxc[:, 3:4], scl[0:1, 11:12])
                GP.wait(VE)
                GP.dma(cc_mx_in[0:1, 0:1], scl[0:1, 11:12])
                GP.wait(GP)
                tok3 = GP.coll("AllReduce", OP.max, replica_groups=[list(range(8))],
                               ins=[cc_mx_in.opt()], outs=[cc_mx_out[co + 2].opt()])
                GP.wait_tok(tok3)
                GP.dma(scl[0:1, 11:12], cc_mx_out[co + 2][0:1, 0:1])
                VE.wait(GP)
                VE.op("tensor_tensor", scl[0:1, 12:13], scl[0:1, 11:12], scl[0:1, 6:7], OP.mult)
                VE.wait(VE)
                VE.op("tensor_scalar", scl[0:1, 12:13], scl[0:1, 12:13], 1.0 / 65536.0 / 32767.0, 1e-12, OP.mult, OP.max)
                VE.wait(VE)
                VE.op("reciprocal", scl[0:1, 13:14], scl[0:1, 12:13])
                VE.wait(VE)
                VE.op("tensor_tensor", scl[0:1, 13:14], scl[0:1, 13:14], scl[0:1, 6:7], OP.mult)
                VE.wait(VE)
                VE.op("tensor_scalar", scl[0:1, 13:14], scl[0:1, 13:14], 1.0 / 65536.0, None, OP.mult)
                VE.wait(VE)
                k1rep(sclr[4][:], scl[0:1, 13:14])

                VE.op("memset", mxc[:, 0:1], 0.0)
                VE.seal()

                def c2_load(ix):
                    ht, g = iters[ix]
                    SY.dma(ub_[ix % 2][:], gate_sc[ht * 128:(ht + 1) * 128, g * 1024:(g + 1) * 1024])
                    SY.dma(uc[ix % 2][:], up_sc[ht * 128:(ht + 1) * 128, g * 1024:(g + 1) * 1024])

                def c2_sc(ix):
                    p = ix % 2
                    SC.op("activation", ua[p][:], ub_[p][:], AF.Identity, bias=mgc[:], scale=sclr[2][:])
                    SC.wait(SC)
                    SC.op("activation", ua[p][:], ua[p][:], AF.Identity, bias=nmgc[:], scale=1.0)
                    SC.wait(SC)
                    SC.op("activation", ub_[p][:], ua[p][:], AF.Sigmoid, bias=0.0, scale=sclr[3][:])
                    SC.wait(SC)
                    SC.op("activation", ub_[p][:], ub_[p][:], AF.Identity, bias=mgc[:], scale=65536.0)
                    SC.wait(SC)
                    SC.op("activation", ub_[p][:], ub_[p][:], AF.Identity, bias=nmgc[:], scale=1.0)
                    SC.op("activation", uc[p][:], uc[p][:], AF.Identity, bias=mgc[:], scale=sclr[1][:])
                    SC.wait(SC)
                    SC.op("activation", uc[p][:], uc[p][:], AF.Identity, bias=nmgc[:], scale=1.0)

                def c2_ve(ix):
                    p = ix % 2
                    VE.op("tensor_scalar", ub_[p][:], ub_[p][:], 65535.0, None, OP.min)
                    VE.wait(VE)
                    VE.op("tensor_tensor", ua[p][:], ua[p][:], ub_[p][:], OP.mult)
                    VE.wait(VE)
                    SC.wait(VE)
                    SC.op("activation", ua[p][:], ua[p][:], AF.Identity, bias=mgc[:], scale=sclr[4][:])
                    SC.wait(SC)
                    SC.op("activation", ua[p][:], ua[p][:], AF.Identity, bias=nmgc[:], scale=1.0)
                    VE.wait(SC)
                    VE.op("tensor_tensor", ua[p][:], ua[p][:], uc[p][:], OP.mult)
                    VE.wait(VE)
                    VE.op("tensor_reduce", cc[0][:], ua[p][:], axis=AX, op=OP.max)
                    VE.op("tensor_reduce", cc[1][:], ua[p][:], axis=AX, op=OP.min)
                    VE.wait(VE)
                    VE.op("tensor_tensor", mxc[:, 0:1], mxc[:, 0:1], cc[0][:], OP.max)
                    VE.wait(VE)
                    VE.op("scalar_tensor_tensor", mxc[:, 0:1], cc[1][:], -1.0, mxc[:, 0:1], OP.mult, OP.max)

                SC.wait(VE)
                SY.wait(GP)
                pipe3(len(iters), c2_load, c2_sc, c2_ve)
                SC.wait(VE)
                VE.wait(SC)
                cross_part_max(mxc[:, 0:1], scl[0:1, 14:15])
                GP.wait(VE)
                GP.dma(cc_mx_in[0:1, 0:1], scl[0:1, 14:15])
                GP.wait(GP)
                tok4 = GP.coll("AllReduce", OP.max, replica_groups=[list(range(8))],
                               ins=[cc_mx_in.opt()], outs=[cc_mx_out[co + 3].opt()])
                GP.wait_tok(tok4)
                GP.dma(scl[0:1, 14:15], cc_mx_out[co + 3][0:1, 0:1])
                VE.wait(GP)
                VE.op("tensor_tensor", scl[0:1, 15:16], scl[0:1, 12:13], scl[0:1, 5:6], OP.mult)
                VE.wait(VE)
                VE.op("tensor_tensor", scl[0:1, 14:15], scl[0:1, 14:15], scl[0:1, 15:16], OP.mult)
                VE.wait(VE)
                VE.op("tensor_scalar", scl[0:1, 14:15], scl[0:1, 14:15], 1.0 / 32767.0, 1e-12, OP.mult, OP.max)
                VE.wait(VE)
                VE.op("reciprocal", scl[0:1, 13:14], scl[0:1, 14:15])
                VE.wait(VE)
                VE.op("tensor_tensor", scl[0:1, 13:14], scl[0:1, 13:14], scl[0:1, 15:16], OP.mult)
                VE.wait(VE)
                k1rep(sclr[5][:], scl[0:1, 13:14])
                k1rep(sclr[0][:], scl[0:1, 14:15])

                if tag == "f1":
                    GP.wait(VE)
                    tap(GP, "sclT", scl[0:1, :])
                    tap(GP, "upsp", up_sc[0:1024, 0:64])
                    tap(GP, "gtsp", gate_sc[0:1024, 0:64])
                    tap(GP, "uqT", upT_sc[0:256, 0:256])
                    tap(GP, "wdT", wdT_sc[0:256, 0:256])
                    tap(GP, "statsF", stats1[:])
                for gp_ in range(8):
                    VE.wait(TE)
                    VE.seal()

                    def d_load(ix, gp_=gp_):
                        p = ix % 2
                        SY.dma(ub_[p][:, 0:512], gate_sc[ix * 128:(ix + 1) * 128, gp_ * 512:(gp_ + 1) * 512])
                        SY.dma(uc[p][:, 0:512], up_sc[ix * 128:(ix + 1) * 128, gp_ * 512:(gp_ + 1) * 512])

                    def d_sc(ix):
                        p = ix % 2
                        SC.op("activation", ua[p][:, 0:512], ub_[p][:, 0:512], AF.Identity, bias=mgc[:], scale=sclr[2][:])
                        SC.wait(SC)
                        SC.op("activation", ua[p][:, 0:512], ua[p][:, 0:512], AF.Identity, bias=nmgc[:], scale=1.0)
                        SC.wait(SC)
                        SC.op("activation", ub_[p][:, 0:512], ua[p][:, 0:512], AF.Sigmoid, bias=0.0, scale=sclr[3][:])
                        SC.wait(SC)
                        SC.op("activation", ub_[p][:, 0:512], ub_[p][:, 0:512], AF.Identity, bias=mgc[:], scale=65536.0)
                        SC.wait(SC)
                        SC.op("activation", ub_[p][:, 0:512], ub_[p][:, 0:512], AF.Identity, bias=nmgc[:], scale=1.0)
                        SC.op("activation", uc[p][:, 0:512], uc[p][:, 0:512], AF.Identity, bias=mgc[:], scale=sclr[1][:])
                        SC.wait(SC)
                        SC.op("activation", uc[p][:, 0:512], uc[p][:, 0:512], AF.Identity, bias=nmgc[:], scale=1.0)

                    def d_ve(ix):
                        p = ix % 2
                        VE.op("tensor_scalar", ub_[p][:, 0:512], ub_[p][:, 0:512], 65535.0, None, OP.min)
                        VE.wait(VE)
                        VE.op("tensor_tensor", ua[p][:, 0:512], ua[p][:, 0:512], ub_[p][:, 0:512], OP.mult)
                        VE.wait(VE)
                        SC.wait(VE)
                        SC.op("activation", ua[p][:, 0:512], ua[p][:, 0:512], AF.Identity, bias=mgc[:], scale=sclr[4][:])
                        SC.wait(SC)
                        SC.op("activation", ua[p][:, 0:512], ua[p][:, 0:512], AF.Identity, bias=nmgc[:], scale=1.0)
                        VE.wait(SC)
                        VE.op("tensor_tensor", ua[p][:, 0:512], ua[p][:, 0:512], uc[p][:, 0:512], OP.mult)
                        SC.wait(VE)
                        SC.op("activation", ua[p][:, 0:512], ua[p][:, 0:512], AF.Identity, bias=mgc[:], scale=sclr[5][:])
                        SC.wait(SC)
                        SC.op("activation", ua[p][:, 0:512], ua[p][:, 0:512], AF.Identity, bias=nmgc[:], scale=1.0)
                        VE.wait(SC)
                        VE.op("tensor_copy", ofm[:, ix * 512:(ix + 1) * 512], ua[p][:, 0:512])

                    SC.wait(VE)
                    SY.wait(GP)
                    pipe3(NF, d_load, d_sc, d_ve)
                    TE.wait(VE)
                    for h in range(NF):
                        wpar = h % 2
                        SY.wait(TE)
                        SY.dma(wt[wpar][:], wdT_sc[h * 128:(h + 1) * 128, :])
                        TE.wait(SY)
                        for ts in range(4):
                            pA = pb[ts]
                            pB = pb[4 + ts] if ts < 2 else pf[ts - 2][:].bitcast(F32)
                            TE.op("matmul", pA[:, 0:512],
                                  ofm[:, h * 512 + ts * 128: h * 512 + (ts + 1) * 128], wt[wpar][:, 0:512],
                                  start=(h == 0), stop=(h == NF - 1))
                            TE.op("matmul", pB[:, 0:512],
                                  ofm[:, h * 512 + ts * 128: h * 512 + (ts + 1) * 128], wt[wpar][:, 512:1024],
                                  start=(h == 0), stop=(h == NF - 1))
                    VE.wait(TE)
                    base = gp_ * 4
                    for ts in range(4):
                        i = base + ts
                        r0 = resid[:, i * 1024: i * 1024 + 512]
                        r1 = resid[:, i * 1024 + 512: (i + 1) * 1024]
                        pA = pb[ts]
                        pB = pb[4 + ts] if ts < 2 else pf[ts - 2][:].bitcast(F32)
                        VE.op("scalar_tensor_tensor", r0, pA[:, 0:512], sclr[0][:], r0, OP.mult, OP.add)
                        VE.op("scalar_tensor_tensor", r1, pB[:, 0:512], sclr[0][:], r1, OP.mult, OP.add)
                        VE.wait(VE)
                    TE.wait(VE)
                full_barrier()
                cx.free_pool(ff)

            if "ffn1" in PHASES:
                ffn_emit("f1")
            if "attn_b" in PHASES:
                attn_b_emit()
            if "ffn2" in PHASES:
                ffn_emit("f2")
        elif "attn_b" in PHASES:
            attn_b_emit()

        SY.wait(VE, SC, TE, GP)
        for i in range(NT):
            ins = nc.sync.dma_start(out=outt[i], in_=resid[:, i * 1024:(i + 1) * 1024])
            ins.then_inc(souts.h, 16)
            souts.n += 16
        nc.sync.wait_ge(souts.h, souts.n)

    cx.close()
    return nc


_NC_CACHE = {}


def kernel(L0_attn_a_x, L0_attn_a_Wk, L0_attn_a_Wv, L0_attn_a_Wq, L0_attn_a_fast_alpha,
           L0_attn_a_slow_alpha, L0_attn_a_nm_w1, L0_attn_a_nm_b1, L0_attn_a_nm_w2,
           L0_attn_a_nm_b2, L0_attn_a_ir_w1, L0_attn_a_ir_b1, L0_attn_a_ir_w2,
           L0_attn_a_ir_b2, L0_attn_a_pw, L0_attn_a_Wo, L0_attn_a_norm_w,
           L0_attn_b_Wq, L0_attn_b_Wk, L0_attn_b_Wv, L0_attn_b_Wo, L0_attn_b_omega,
           L0_attn_b_W1, L0_attn_b_b1, L0_attn_b_W2, L0_attn_b_b2, L0_attn_b_gamma,
           L0_attn_b_beta, L0_ffn_w_gate, L0_ffn_w_up, L0_ffn_w_down):
    key = (PHASES, tuple(sorted(DEBUG_TAP.keys())))
    if key not in _NC_CACHE:
        _NC_CACHE[key] = build()
    nc = _NC_CACHE[key]

    x = np.ascontiguousarray(np.asarray(L0_attn_a_x, np.float32).reshape(B * S, D))
    r2 = lambda a, shp: np.ascontiguousarray(np.asarray(a, np.float32).reshape(shp))
    in_maps = []
    for c in range(NCORES):
        im = {
            "xs": x[c * TPC:(c + 1) * TPC],
            "aWk": r2(L0_attn_a_Wk, (64, D)), "aWv": r2(L0_attn_a_Wv, (64, D)),
            "aWq": r2(L0_attn_a_Wq, (64, D)),
            "fa": r2(L0_attn_a_fast_alpha, (1, 1)), "sa": r2(L0_attn_a_slow_alpha, (1, 1)),
            "nm_w1": r2(L0_attn_a_nm_w1, (128, 64)), "nm_b1": r2(L0_attn_a_nm_b1, (1, 128)),
            "nm_w2": r2(L0_attn_a_nm_w2, (64, 128)), "nm_b2": r2(L0_attn_a_nm_b2, (1, 64)),
            "ir_w1": r2(L0_attn_a_ir_w1, (64, D)), "ir_b1": r2(L0_attn_a_ir_b1, (1, 64)),
            "ir_w2": r2(L0_attn_a_ir_w2, (1, 64)), "ir_b2": r2(L0_attn_a_ir_b2, (1, 1)),
            "pw": r2(L0_attn_a_pw, (3, D)), "aWo": r2(L0_attn_a_Wo, (D, 64)),
            "norm_w": r2(L0_attn_a_norm_w, (1, 64)),
            "rankm": np.asarray([[1.0 if jj < (c % 4) else 0.0 for jj in range(8)]], np.float32),
        }
        if "attn_b" in PHASES:
            im.update({
                "bWq": r2(L0_attn_b_Wq, (D, D)), "bWk": r2(L0_attn_b_Wk, (D, D)),
                "bWv": r2(L0_attn_b_Wv, (D, D)), "bWo": r2(L0_attn_b_Wo, (D, D)),
                "omega": r2(L0_attn_b_omega, (MPH, 64)),
                "bW1": r2(L0_attn_b_W1, (D, MQ)), "bb1": r2(L0_attn_b_b1, (1, MQ)),
                "bW2": r2(L0_attn_b_W2, (MQ, D)), "bb2": r2(L0_attn_b_b2, (1, D)),
                "gamma": r2(L0_attn_b_gamma, (1, D)), "beta": r2(L0_attn_b_beta, (1, D)),
            })
        if "ffn1" in PHASES or "ffn2" in PHASES:
            im.update({
                "w_gate": r2(L0_ffn_w_gate, (FQ, D)), "w_up": r2(L0_ffn_w_up, (FQ, D)),
                "w_down": r2(L0_ffn_w_down, (D, FQ)),
            })
        in_maps.append(im)

    res = run_bass_kernel_spmd(nc, in_maps, core_ids=list(range(NCORES)))
    outp = np.concatenate([np.asarray(res.results[c]["out"]) for c in range(NCORES)], axis=0)
    if DEBUG_TAP:
        return outp.reshape(B, S, D), [
            {nm: np.asarray(res.results[c]["tp_" + nm]) for nm in DEBUG_TAP} for c in range(NCORES)]
    return outp.reshape(B, S, D)
